# version 04
# baseline (speedup 1.0000x reference)
import numpy as np
import ml_dtypes
import concourse.bass as bass
import concourse.mybir as mybir
from concourse.bass_utils import run_bass_kernel_spmd
from concourse.aot_env import allow_ici_all_to_all

F32 = mybir.dt.float32
BF16 = mybir.dt.bfloat16
U8 = mybir.dt.uint8
AF = mybir.ActivationFunctionType
AL = mybir.AluOpType

NCORES = 8
B, N, D, H, HD = 2, 8192, 1024, 16, 64
TOK = B * N
SLAB = TOK // NCORES
C = 128
NCHUNK = N // C
SUPER = 512
NSUPER = N // SUPER
CPS = SUPER // C
DHID = 4096
KB = D // 128
HB = DHID // 128
HALF = 1024

_STAGE = 3
_SUB = None
SHIFT = 32 * float(np.log(2.0))
UNSCALE = 2.0 ** 64


class Ctr:
    def __init__(self, nc):
        self.nc = nc
        self.h = {}
        self.v = {}

    def new(self, name):
        self.h[name] = self.nc.alloc_semaphore(name=name)
        self.v[name] = 0

    def inc(self, inst, name, n=1):
        inst.then_inc(self.h[name], n)
        self.v[name] += n
        return self.v[name]

    def val(self, name):
        return self.v[name]

    def wait(self, eng, name, val=None):
        eng.wait_ge(self.h[name], self.v[name] if val is None else val)



_DW_OPS = {
    "tensor_tensor", "tensor_add", "tensor_sub", "tensor_mul", "tensor_max",
    "tensor_scalar", "tensor_single_scalar", "tensor_scalar_add",
    "tensor_scalar_sub", "tensor_scalar_mul", "tensor_scalar_max",
    "tensor_scalar_min", "tensor_copy", "tensor_relu", "tensor_reduce",
    "reduce_max", "reduce_sum", "scalar_tensor_tensor", "tensor_tensor_scan",
    "select", "reciprocal", "memset", "memzero", "activation", "copy", "mul",
    "add", "sign", "square", "sqrt", "bn_stats", "bn_aggr", "transpose",
    "stream_shuffle", "copy_predicated",
}


class _DW:

    def __init__(self, e):
        object.__setattr__(self, "_e", e)

    def __getattr__(self, name):
        a = getattr(self._e, name)
        if name in _DW_OPS:
            def call(*args, _a=a, **kw):
                self._e.drain()
                return _a(*args, **kw)
            return call
        return a


def build_program():
    nc = bass.Bass("TRN2", target_bir_lowering=False, debug=False)

    def din(name, shape, dtype=F32):
        return nc.dram_tensor(name, list(shape), dtype, kind="ExternalInput").ap()

    xsT = din("x_slabT", (D, SLAB))
    wq_d = din("wq", (D, 256)); wk_d = din("wk", (D, 256)); wv_d = din("wv", (D, 256))
    wfg_d = din("wfg", (D, 68))
    wf2_d = din("wf2", (64, 256))
    wog1_d = din("wog1", (D, 64)); wog2_d = din("wog2", (64, D))
    wo_d = din("wo", (D, D))
    normw_d = din("normw", (128, KB))
    w1t = din("w1t", (D, DHID)); w2t = din("w2t", (D, DHID))
    h1t = din("h1t", (DHID, DHID)); h2t = din("h2t", (DHID, DHID))
    w3t = din("w3t", (DHID, D))
    s1_d = din("s1", (128, HB)); s2_d = din("s2", (128, HB))
    sh1_d = din("sh1", (128, HB)); sh2_d = din("sh2", (128, HB))
    s3_d = din("s3", (128, KB))
    c_idbf = din("c_idbf", (128, 128), BF16)
    c_idf32 = din("c_idf32", (128, 128))
    c_idpair = din("c_idpair", (128, 64), BF16)
    c_id2pair = din("c_id2pair", (128, 2, 64))
    c_onescol = din("c_onescol", (128, 1))
    c_onesrow = din("c_onesrow", (1, 128))
    c_sel4 = din("c_sel4", (128, 2, 4))
    c_bc4 = din("c_bc4", (4, 2, 128))
    c_msel = din("c_msel", (64, 2))
    c_maskl4 = din("c_maskl4", (128, 4, 128), U8)
    c_masku4 = din("c_masku4", (128, 4, 128), U8)

    yT = nc.dram_tensor("y_slabT", [D, SLAB], F32, kind="ExternalOutput").ap()

    x1_sp = nc.dram_tensor("x1_sp", [D, SLAB], F32).ap()
    x2_sp = nc.dram_tensor("x2_sp", [D, SLAB], F32).ap()
    ag_in = nc.dram_tensor("ag_in", [D, SLAB], BF16)
    xg = nc.dram_tensor("xg", [4, D, SLAB], BF16)
    a2a_in = nc.dram_tensor("a2a_in", [8, 256, SLAB], F32)
    a2a_out = nc.dram_tensor("a2a_out", [8, 256, SLAB], F32)

    ctr = Ctr(nc)
    for s in ["LD", "LW", "LWa", "LWb", "ST", "S2", "MM", "VE", "SC", "GP", "CC"]:
        ctr.new(s)

    _guards = []

    def alloc(name, shape, dtype=F32):
        g = nc.sbuf_tensor(name, list(shape), dtype)
        h = g.__enter__()
        _guards.append(g)
        return h

    def pal(name, shape):
        g = nc.psum_tensor(name, list(shape), F32)
        h = g.__enter__()
        _guards.append(g)
        return h

    def free_to(mark):
        while len(_guards) > mark:
            _guards.pop().__exit__(None, None, None)

    sb_idbf = alloc("sb_idbf", (128, 128), BF16)
    sb_idf32 = alloc("sb_idf32", (128, 128))
    sb_idpair = alloc("sb_idpair", (128, 64), BF16)
    sb_id2pair = alloc("sb_id2pair", (128, 2, 64))
    sb_onescol = alloc("sb_onescol", (128, 1))
    sb_onesrow = alloc("sb_onesrow", (1, 128))
    sb_sel4 = alloc("sb_sel4", (128, 2, 4))
    sb_bc4 = alloc("sb_bc4", (4, 2, 128))
    sb_msel = alloc("sb_msel", (64, 2))
    sb_maskl = alloc("sb_maskl", (128, 4, 128), U8)
    sb_masku = alloc("sb_masku", (128, 4, 128), U8)
    sb_zeros = alloc("sb_zeros", (128, 4, 128))
    sb_normw = alloc("sb_normw", (128, KB))
    sb_rows = alloc("sb_rows", (1, 2048))
    sb_scales = alloc("sb_scales", (128, 6, HB))
    PMARK = len(_guards)

    blk_guard = nc.Block()
    blk = blk_guard.__enter__()

    def WV(f):
        blk.vector(lambda e: f(_DW(e)))

    def WS(f):
        blk.scalar(lambda e: f(_DW(e)))

    def WG(f):
        blk.gpsimd(lambda e: f(_DW(e)))


    @blk.sync
    def _(e):
        for dst, src in [(sb_idbf, c_idbf), (sb_idf32, c_idf32), (sb_idpair, c_idpair),
                         (sb_id2pair, c_id2pair), (sb_onescol, c_onescol),
                         (sb_onesrow, c_onesrow), (sb_sel4, c_sel4), (sb_bc4, c_bc4),
                         (sb_maskl, c_maskl4), (sb_masku, c_masku4)]:
            ctr.inc(e.dma_start(out=dst[:], in_=src), "LD", 16)
        ctr.inc(e.dma_start(out=sb_msel[:], in_=c_msel), "LD", 16)
        ctr.inc(e.dma_start(out=sb_normw[:], in_=normw_d), "LD", 16)
        for i, s in zip([0, 1, 3, 4], [s1_d, s2_d, sh1_d, sh2_d]):
            ctr.inc(e.dma_start(out=sb_scales[:, i, :], in_=s), "LD", 16)
        ctr.inc(e.dma_start(out=sb_scales[:, 5, 0:KB], in_=s3_d), "LD", 16)

    @WV
    def _(e):
        ctr.wait(e, "LD")
        e.memset(sb_zeros[:], 0.0)
        e.tensor_mul(sb_scales[:, 2, :], sb_scales[:, 0, :], sb_scales[:, 1, :])
        ctr.inc(e.tensor_scalar_mul(sb_scales[:, 5, 0:KB], sb_scales[:, 5, 0:KB], 0.5), "VE")

    def barrier():
        names = ["LD", "LW", "LWa", "LWb", "ST", "S2", "MM", "VE", "SC", "GP", "CC"]
        for sec in [blk.sync, blk.gpsimd, blk.scalar, blk.vector, blk.tensor]:
            def body(e, _n=tuple(names)):
                for nm in _n:
                    if ctr.val(nm):
                        ctr.wait(e, nm)
            sec(body)

    def ln_stage(xt, twid, out_tiles, scr, p_stat, p_sq, p_bc, nw=None):
        nq = twid // 512
        ld_now = ctr.val("LD")
        ve_now = ctr.val("VE")

        @blk.tensor
        def _(e):
            e.wait_ge(ctr.h["LD"], ld_now)
            e.wait_ge(ctr.h["VE"], ve_now)
            for tq in range(nq):
                for kb in range(KB):
                    mi = e.matmul(p_stat[0:1, tq * 512:(tq + 1) * 512], sb_onescol[:],
                                  xt[kb][:, tq * 512:(tq + 1) * 512],
                                  start=(kb == 0), stop=(kb == KB - 1))
            ctr.inc(mi, "MM")
        mm_sx = ctr.val("MM")

        sc0 = ctr.val("SC")

        @WS
        def _(e):
            e.wait_ge(ctr.h["LD"], ld_now)
            e.wait_ge(ctr.h["VE"], ve_now)
            for kb in range(KB):
                if kb >= 2:
                    e.wait_ge(ctr.h["MM"], mm_sx + kb - 1)
                ctr.inc(e.activation(scr[:, kb % 2, 0:twid], xt[kb], AF.Square), "SC")

        @blk.tensor
        def _(e):
            for kb in range(KB):
                e.wait_ge(ctr.h["SC"], sc0 + kb + 1)
                for tq in range(nq):
                    mi = e.matmul(p_sq[0:1, tq * 512:(tq + 1) * 512], sb_onescol[:],
                                  scr[:, kb % 2, tq * 512:(tq + 1) * 512],
                                  start=(kb == 0), stop=(kb == KB - 1))
                ctr.inc(mi, "MM")
        mm_sq = ctr.val("MM")

        r0 = scr[0:1, 0, 0:twid]
        r1 = scr[0:1, 1, 0:twid]

        @WV
        def _(e):
            e.wait_ge(ctr.h["MM"], mm_sq)
            e.tensor_scalar_mul(sb_rows[0:1, 0:twid], p_stat[0:1, 0:twid], 1.0 / D)
            e.tensor_scalar_mul(p_sq[0:1, 0:twid], p_sq[0:1, 0:twid], 1.0 / D)
            e.tensor_mul(r0, sb_rows[0:1, 0:twid], sb_rows[0:1, 0:twid])
            e.tensor_sub(p_sq[0:1, 0:twid], p_sq[0:1, 0:twid], r0)
            ctr.inc(e.tensor_scalar_add(p_sq[0:1, 0:twid], p_sq[0:1, 0:twid], 1e-5), "VE")
        v_var = ctr.val("VE")

        @WS
        def _(e):
            e.wait_ge(ctr.h["VE"], v_var)
            ctr.inc(e.activation(p_stat[0:1, 0:twid], p_sq[0:1, 0:twid], AF.Sqrt), "SC")
        sc_sq = ctr.val("SC")

        @WV
        def _(e):
            e.wait_ge(ctr.h["SC"], sc_sq)
            e.reciprocal(r1, p_stat[0:1, 0:twid])
            e.tensor_mul(r0, r1, r1)
            e.tensor_mul(r0, p_sq[0:1, 0:twid], r0)
            e.tensor_scalar(r0, r0, -0.5, 1.5, AL.mult, AL.add)
            ctr.inc(e.tensor_mul(sb_rows[0:1, 1024:1024 + twid], r1, r0), "VE")

        for tq in range(nq):
            v_pre = ctr.val("VE")

            @blk.tensor
            def _(e, tq=tq, v_pre=v_pre):
                e.wait_ge(ctr.h["VE"], v_pre)
                e.matmul(p_bc[:, 0:512], sb_onesrow[:],
                         sb_rows[0:1, tq * 512:(tq + 1) * 512], start=True, stop=True)
                ctr.inc(e.matmul(p_bc[:, 512:1024], sb_onesrow[:],
                                 sb_rows[0:1, 1024 + tq * 512:1024 + (tq + 1) * 512],
                                 start=True, stop=True), "MM")
            mm_bc = ctr.val("MM")

            @WV
            def _(e, tq=tq, mm_bc=mm_bc):
                e.wait_ge(ctr.h["MM"], mm_bc)
                sl = slice(tq * 512, (tq + 1) * 512)
                for kb in range(KB):
                    e.tensor_sub(scr[:, 2, 0:512], xt[kb][:, sl], p_bc[:, 0:512])
                    if nw is None:
                        vi = e.tensor_mul(out_tiles[kb][:, sl], scr[:, 2, 0:512],
                                          p_bc[:, 512:1024])
                    else:
                        vi = e.scalar_tensor_tensor(out_tiles[kb][:, sl], scr[:, 2, 0:512],
                                                    nw[:, kb:kb + 1], p_bc[:, 512:1024],
                                                    AL.mult, AL.mult)
                ctr.inc(vi, "VE")

    def ffn_pass(src_ap, spill_ap, make_ln_out, final_store, uid):
        mark0 = len(_guards)
        xln = alloc("xln" + uid, (128, KB, HALF), BF16)
        Hb = alloc("Hb" + uid, (128, HB, HALF), BF16)
        Gb = alloc("Gb" + uid, (128, HB, HALF), BF16)
        wb = alloc("wb" + uid, (128, 2, HB, 128), BF16)
        scr = alloc("scr" + uid, (128, 4, HALF))
        xres = alloc("xres" + uid, (128, 2, 512))

        GbF = Gb[:].bitcast(F32)
        HbF = Hb[:].bitcast(F32)

        for hf in range(2):
            t0 = hf * HALF
            markp = len(_guards)

            p_stat = pal("pstA" + uid + str(hf), (128, 1024))
            p_sq = pal("psqA" + uid + str(hf), (128, 1024))
            p_bc = pal("pbcA" + uid + str(hf), (128, 1024))

            xtiles = [HbF[:, 4 * kb:4 * kb + 4, :].rearrange("p a b -> p (a b)")
                      for kb in range(KB)]
            xtiles = [HbF[:, 2 * kb:2 * kb + 2, :].rearrange("p a b -> p (a b)")
                      for kb in range(KB)]

            @blk.sync
            def _(e):
                for nm in ["MM", "SC", "GP", "VE", "ST", "LD"]:
                    if ctr.val(nm):
                        ctr.wait(e, nm)
                for kb in range(KB):
                    ctr.inc(e.dma_start(out=xtiles[kb],
                                        in_=src_ap[kb * 128:(kb + 1) * 128, t0:t0 + HALF]),
                            "LD", 16)

            ln_stage(xtiles, HALF, [xln[:, kb, :] for kb in range(KB)],
                     scr, p_stat, p_sq, p_bc)
            free_to(markp)
            if _SUB == 'ln':
                break

            p_f = pal("pf" + uid + str(hf), (128, 2048))
            p_f2 = pal("pf2" + uid + str(hf), (128, 2048))
            pbank = [p_f[:, i * 512:(i + 1) * 512] for i in range(4)] + \
                    [p_f2[:, i * 512:(i + 1) * 512] for i in range(4)]

            lwp0 = [ctr.val("LWa"), ctr.val("LWb")]
            mm0 = ctr.val("MM")
            ve_xln = ctr.val("VE")
            ve0 = ctr.val("VE")

            sc0_pre = ctr.val("SC")
            gp_base = ctr.val("GP")

            def _w12_gp_mul(e, ob):
                p = ob % 2
                for tq in range(2):
                    e.wait_ge(ctr.h["SC"], sc0_pre + 4 * ob + 2 * tq + 2)
                    ctr.inc(e.tensor_mul(scr[:, p, tq * 512:(tq + 1) * 512],
                                         scr[:, p, tq * 512:(tq + 1) * 512],
                                         scr[:, 2 + p, tq * 512:(tq + 1) * 512]), "GP")

            @WG
            def _(e):
                e.wait_ge(ctr.h["VE"], ve_xln)
                for ob in range(HB):
                    p = ob % 2
                    sem = "LWa" if p == 0 else "LWb"
                    if ob >= 2:
                        e.wait_ge(ctr.h["MM"], mm0 + (ob - 1) * 32)
                    ctr.inc(e.dma_start(
                        out=wb[:, p, 0:8, :],
                        in_=w1t[:, ob * 128:(ob + 1) * 128].rearrange("(k p) n -> p k n", p=128)),
                        sem, 16)
                    ctr.inc(e.dma_start(
                        out=wb[:, p, 8:16, :],
                        in_=w2t[:, ob * 128:(ob + 1) * 128].rearrange("(k p) n -> p k n", p=128)),
                        sem, 16)
                    if ob >= 2:
                        _w12_gp_mul(e, ob - 2)
                _w12_gp_mul(e, HB - 2)
                _w12_gp_mul(e, HB - 1)

            @blk.tensor
            def _(e):
                e.wait_ge(ctr.h["VE"], ve_xln)
                for ob in range(HB):
                    p = ob % 2
                    sem = "LWa" if p == 0 else "LWb"
                    e.wait_ge(ctr.h[sem], lwp0[p] + 32 * (ob // 2 + 1))
                    if ob >= 2:
                        e.wait_ge(ctr.h["VE"], ve0 + 2 * (ob - 1))
                    for w12 in range(2):
                        for kb in range(KB):
                            for tq in range(2):
                                bank = pbank[4 * p + 2 * w12 + tq]
                                mi = e.matmul(bank[:, 0:512], wb[:, p, 8 * w12 + kb, :],
                                              xln[:, kb, tq * 512:(tq + 1) * 512],
                                              start=(kb == 0), stop=(kb == KB - 1))
                    ctr.inc(mi, "MM", 32)

            sc0 = ctr.val("SC")

            @WS
            def _(e):
                for ob in range(HB):
                    p = ob % 2
                    e.wait_ge(ctr.h["MM"], mm0 + 32 * (ob + 1))
                    for tq in range(2):
                        ctr.inc(e.activation(scr[:, p, tq * 512:(tq + 1) * 512],
                                             pbank[4 * p + tq][:, 0:512], AF.Sigmoid,
                                             scale=sb_scales[:, 0, ob:ob + 1]), "SC")
                        ctr.inc(e.mul(scr[:, 2 + p, tq * 512:(tq + 1) * 512],
                                      pbank[4 * p + 2 + tq][:, 0:512],
                                      sb_scales[:, 1, ob:ob + 1]), "SC")

            gp0 = gp_base

            @WV
            def _(e):
                for ob in range(HB):
                    p = ob % 2
                    for tq in range(2):
                        e.wait_ge(ctr.h["GP"], gp0 + 2 * ob + tq + 1)
                        ctr.inc(e.scalar_tensor_tensor(Hb[:, ob, tq * 512:(tq + 1) * 512],
                                                       pbank[4 * p + tq][:, 0:512],
                                                       sb_scales[:, 0, ob:ob + 1],
                                                       scr[:, p, tq * 512:(tq + 1) * 512],
                                                       AL.mult, AL.mult), "VE")

            if _SUB == 'w12':
                free_to(markp)
                break
            hstages = [(h1t, 3, Hb, Gb), (h2t, 4, Gb, Hb)]
            if _SUB == 'h1':
                hstages = hstages[:1]
            for (wt, sidx, src_b, dst_b) in hstages:
                lwp1 = [ctr.val("LWa"), ctr.val("LWb")]
                mm1 = ctr.val("MM")
                sc1 = ctr.val("SC")
                pre_ve = ctr.val("VE")
                pre_sc = ctr.val("SC")
                pre_st = ctr.val("ST")

                @WG
                def _(e, wt=wt, mm1=mm1, pre_ve=pre_ve):
                    e.wait_ge(ctr.h["VE"], pre_ve)
                    e.wait_ge(ctr.h["MM"], mm1)
                    for ob in range(HB):
                        p = ob % 2
                        sem = "LWa" if p == 0 else "LWb"
                        if ob >= 2:
                            e.wait_ge(ctr.h["MM"], mm1 + (ob - 1) * 64)
                        ctr.inc(e.dma_start(
                            out=wb[:, p, :, :],
                            in_=wt[:, ob * 128:(ob + 1) * 128].rearrange(
                                "(k p) n -> p k n", p=128)), sem, 16)

                @blk.tensor
                def _(e, lwp1=lwp1, sc1=sc1, src_b=src_b, pre_ve=pre_ve):
                    e.wait_ge(ctr.h["VE"], pre_ve)
                    e.wait_ge(ctr.h["SC"], sc1)
                    for ob in range(HB):
                        p = ob % 2
                        sem = "LWa" if p == 0 else "LWb"
                        e.wait_ge(ctr.h[sem], lwp1[p] + 16 * (ob // 2 + 1))
                        if ob >= 2:
                            e.wait_ge(ctr.h["SC"], sc1 + 2 * (ob - 1))
                        for kk in range(HB):
                            for tq in range(2):
                                bank = pbank[4 * p + tq]
                                mi = e.matmul(bank[:, 0:512], wb[:, p, kk, :],
                                              src_b[:, kk, tq * 512:(tq + 1) * 512],
                                              start=(kk == 0), stop=(kk == HB - 1))
                        ctr.inc(mi, "MM", 64)

                @WS
                def _(e, mm1=mm1, sidx=sidx, dst_b=dst_b, pre_st=pre_st):
                    e.wait_ge(ctr.h["ST"], pre_st)
                    for ob in range(HB):
                        p = ob % 2
                        e.wait_ge(ctr.h["MM"], mm1 + 64 * (ob + 1))
                        for tq in range(2):
                            ctr.inc(e.activation(dst_b[:, ob, tq * 512:(tq + 1) * 512],
                                                 pbank[4 * p + tq][:, 0:512], AF.Gelu,
                                                 scale=sb_scales[:, sidx, ob:ob + 1]), "SC")

            if _SUB in ('h1', 'h2'):
                free_to(markp)
                break
            free_to(markp)

            p_w = pal("pw" + uid + str(hf), (128, 1024))
            lwp3 = [ctr.val("LWa"), ctr.val("LWb")]
            mm3 = ctr.val("MM")
            ve3 = ctr.val("VE")
            ld3 = ctr.val("LD")
            sc_h2 = ctr.val("SC")

            x1tiles = [GbF[:, 2 * kb:2 * kb + 2, :].rearrange("p a b -> p (a b)")
                       for kb in range(KB)]

            @WG
            def _(e):
                e.wait_ge(ctr.h["SC"], sc_h2)
                for ob in range(KB):
                    p = ob % 2
                    sem = "LWa" if p == 0 else "LWb"
                    if ob >= 2:
                        e.wait_ge(ctr.h["MM"], mm3 + (ob - 1) * 64)
                    ctr.inc(e.dma_start(
                        out=wb[:, p, :, :],
                        in_=w3t[:, ob * 128:(ob + 1) * 128].rearrange("(k p) n -> p k n", p=128)),
                        sem, 16)

            @blk.sync
            def _(e):
                ctr.wait(e, "LD")
                for ob in range(KB):
                    if ob >= 1:
                        e.wait_ge(ctr.h["VE"], ve3 + 2 * ob)
                    for tq in range(2):
                        ctr.inc(e.dma_start(out=xres[:, tq, :],
                                            in_=src_ap[ob * 128:(ob + 1) * 128,
                                                       t0 + tq * 512:t0 + (tq + 1) * 512]),
                                "LD", 16)

            @blk.tensor
            def _(e):
                e.wait_ge(ctr.h["SC"], sc_h2)
                for ob in range(KB):
                    p = ob % 2
                    sem = "LWa" if p == 0 else "LWb"
                    e.wait_ge(ctr.h[sem], lwp3[p] + 16 * (ob // 2 + 1))
                    if ob >= 1:
                        e.wait_ge(ctr.h["VE"], ve3 + 2 * ob)
                    for kk in range(HB):
                        for tq in range(2):
                            bank = p_w[:, tq * 512:(tq + 1) * 512]
                            mi = e.matmul(bank, wb[:, p, kk, :],
                                          Hb[:, kk, tq * 512:(tq + 1) * 512],
                                          start=(kk == 0), stop=(kk == HB - 1))
                    ctr.inc(mi, "MM", 64)

            @WV
            def _(e):
                for ob in range(KB):
                    e.wait_ge(ctr.h["MM"], mm3 + 64 * (ob + 1))
                    e.wait_ge(ctr.h["LD"], ld3 + 32 * (ob + 1))
                    for tq in range(2):
                        ctr.inc(e.scalar_tensor_tensor(
                            x1tiles[ob][:, tq * 512:(tq + 1) * 512],
                            p_w[:, tq * 512:(tq + 1) * 512],
                            sb_scales[:, 5, ob:ob + 1],
                            xres[:, tq, :], AL.mult, AL.add), "VE")
            ve_w3 = ctr.val("VE")

            dst_ap = yT if final_store else spill_ap

            @blk.sync
            def _(e):
                ctr.wait(e, "ST")
                for ob in range(KB):
                    e.wait_ge(ctr.h["VE"], ve3 + 2 * (ob + 1))
                    ctr.inc(e.dma_start(
                        out=dst_ap[ob * 128:(ob + 1) * 128, t0:t0 + HALF],
                        in_=x1tiles[ob]), "ST", 16)

            if make_ln_out:
                p_stat = pal("pstB" + uid + str(hf), (128, 1024))
                p_sq = pal("psqB" + uid + str(hf), (128, 1024))
                p_bc2 = pal("pbcB" + uid + str(hf), (128, 1024))
                ln_stage(x1tiles, HALF, [xln[:, kb, :] for kb in range(KB)],
                         scr, p_stat, p_sq, p_bc2)

                @blk.sync
                def _(e):
                    ctr.wait(e, "VE")
                    ctr.wait(e, "ST")
                    for kb in range(KB):
                        ctr.inc(e.dma_start(
                            out=ag_in.ap()[kb * 128:(kb + 1) * 128, t0:t0 + HALF],
                            in_=xln[:, kb, :]), "ST", 16)
            free_to(markp)

        free_to(mark0)

    def attn():
        mark0 = len(_guards)
        wq_s = alloc("wq_s", (128, KB, 256), BF16)
        wk_s = alloc("wk_s", (128, KB, 256), BF16)
        wv_s = alloc("wv_s", (128, KB, 256), BF16)
        wfg_s = alloc("wfg_s", (128, KB, 68), BF16)
        wf2_s = alloc("wf2_s", (64, 2, 128), BF16)
        xgt = alloc("xgt", (128, KB, SUPER), BF16)
        qsil = alloc("qsil", (128, 2, SUPER))
        ksil = alloc("ksil", (128, 2, SUPER))
        khat = alloc("khat", (128, 2, SUPER))
        vbf = alloc("vbf", (128, 2, SUPER), BF16)
        sgk = alloc("sgk", (128, 2, SUPER))
        gkb = alloc("gkb", (128, 2, SUPER))
        rbb = alloc("rbb", (128, 2, SUPER))
        rgbb = alloc("rgbb", (128, 2, SUPER))
        zf_bf = alloc("zf_bf", (64, SUPER), BF16)
        arows = alloc("arows", (4, 4, SUPER))
        ksq = alloc("ksq", (128, 2, SUPER))
        Gw = alloc("Gw", (128, 2, SUPER))
        argE = alloc("argE", (128, 2, SUPER))
        Gsc = alloc("Gsc", (128, CPS, 2, 6))
        egc = alloc("egc", (128, CPS, 2, 1))
        E1 = alloc("E1", (128, 2, SUPER)); E2 = alloc("E2", (128, 2, SUPER))
        E3 = alloc("E3", (128, 2, SUPER)); E4 = alloc("E4", (128, 2, SUPER))
        tE = alloc("tE", (128, 2, SUPER)); kg = alloc("kg", (128, 2, SUPER))
        Qmid = alloc("Qmid", (128, 2, SUPER), BF16); Qful = alloc("Qful", (128, 2, SUPER), BF16)
        Kdot = alloc("Kdot", (128, 2, SUPER), BF16); Khb = alloc("Khb", (128, 2, SUPER), BF16)
        Adot = alloc("Adot", (128, 2, SUPER), BF16); Akm = alloc("Akm", (128, 2, SUPER), BF16)
        Xb = alloc("Xb", (128, 4, CPS, 128), BF16)
        PbA = alloc("PbA", (128, 4, CPS, 128), BF16); PtA = alloc("PtA", (128, 4, CPS, 128), BF16)
        PbB = alloc("PbB", (128, 4, CPS, 128), BF16); PtB = alloc("PtB", (128, 4, CPS, 128), BF16)
        POb = alloc("POb", (128, 4, CPS, 128), BF16)
        KTb = alloc("KTb", (128, 4, CPS, HD), BF16)
        QHb = alloc("QHb", (128, CPS, 4, C), BF16)
        WPb = alloc("WPb", (128, CPS, 4, HD), BF16)
        OLb = alloc("OLb", (128, CPS, 4, HD))
        SLb = alloc("SLb", (128, CPS, 2, HD))
        Sst = alloc("Sst", (128, 2, HD))
        Sbf = alloc("Sbf", (128, 2, HD), BF16)
        Otm = alloc("Otm", (128, 4, HD))
        Ovm = alloc("Ovm", (64, 4, SUPER))
        Ovm_lo = alloc("Ovm_lo", (64, 4, SUPER))
        Ovm_hi = alloc("Ovm_hi", (64, 4, SUPER))

        pmh = pal("pmh", (128, 2048))
        pful = pal("pful", (128, 2048))
        pfB = [pful[:, i * 512:(i + 1) * 512] for i in range(4)]
        pmB = [pmh[:, i * 512:(i + 1) * 512] for i in range(4)]

        def pmv(lo, hi):
            return pmh[:].rearrange("p (h x) -> p h x", x=512)[:, :, lo:hi]

        def pfv(lo, hi, x):
            return pful[:, lo:hi].rearrange("p (h x) -> p h x", x=x)

        @WG
        def _(e):
            ctr.wait(e, "CC")
            for dst, src in [(wq_s, wq_d), (wk_s, wk_d), (wv_s, wv_d), (wfg_s, wfg_d)]:
                ctr.inc(e.dma_start(out=dst[:],
                                    in_=src.rearrange("(k p) n -> p k n", p=128)), "LW", 16)
            ctr.inc(e.dma_start(out=wf2_s[:],
                                in_=wf2_d.rearrange("k (m n) -> k m n", n=128)), "LW", 16)
        lw_w = ctr.val("LW")

        @WV
        def _(e):
            e.memset(QHb[:], 0.0)
            e.memset(WPb[:], 0.0)
            e.memset(Sst[:], 0.0)
            ctr.inc(e.memset(Sbf[:], 0.0), "VE")

        for sc_i in range(NSUPER):
            ts = sc_i * SUPER
            rk = ts // SLAB
            tl = ts % SLAB

            pre_all = {nm: ctr.val(nm) for nm in ["MM", "VE", "SC", "GP", "S2"]}

            @blk.sync
            def _(e, rk=rk, tl=tl, pre=dict(pre_all)):
                ctr.wait(e, "LD")
                ctr.wait(e, "CC")
                for nm in ["VE", "SC", "MM", "GP"]:
                    e.wait_ge(ctr.h[nm], pre[nm])
                for kb in range(KB):
                    ctr.inc(e.dma_start(out=xgt[:, kb, :],
                                        in_=xg.ap()[rk, kb * 128:(kb + 1) * 128, tl:tl + SUPER]),
                            "LD", 16)
            ld_x = ctr.val("LD")

            def proj_mm(w_s, pb, bank, extra_sc=None):
                @blk.tensor
                def _(e, w_s=w_s, pb=pb, bank=bank, extra_sc=extra_sc):
                    e.wait_ge(ctr.h["LD"], ld_x)
                    e.wait_ge(ctr.h["LW"], lw_w)
                    if extra_sc is not None:
                        e.wait_ge(ctr.h["SC"], extra_sc)
                    for kb in range(KB):
                        mi = e.matmul(bank[:, 0:512], w_s[:, kb, pb * 128:(pb + 1) * 128],
                                      xgt[:, kb, :], start=(kb == 0), stop=(kb == KB - 1))
                    ctr.inc(mi, "MM")
                return ctr.val("MM")

            def proj_ev(kind, dst, bank, mm_at, scale=None):
                if kind == "copy":
                    @WV
                    def _(e, dst=dst, bank=bank, mm_at=mm_at):
                        e.wait_ge(ctr.h["MM"], mm_at)
                        ctr.inc(e.tensor_copy(dst, bank), "SC")
                    return ctr.val("SC")
                @WS
                def _(e, kind=kind, dst=dst, bank=bank, mm_at=mm_at):
                    e.wait_ge(ctr.h["MM"], mm_at)
                    if kind == "silu":
                        ctr.inc(e.activation(dst, bank, AF.Silu), "SC")
                    elif kind == "sig":
                        ctr.inc(e.activation(dst, bank, AF.Sigmoid), "SC")
                return ctr.val("SC")

            m = proj_mm(wq_s, 0, pfB[0]); proj_ev("silu", qsil[:, 0, :], pfB[0][:, 0:512], m)
            m = proj_mm(wq_s, 1, pfB[1]); proj_ev("silu", qsil[:, 1, :], pfB[1][:, 0:512], m)
            m = proj_mm(wk_s, 0, pfB[2]); proj_ev("silu", ksil[:, 0, :], pfB[2][:, 0:512], m)
            m = proj_mm(wk_s, 1, pfB[3]); sc_k1 = proj_ev("silu", ksil[:, 1, :], pfB[3][:, 0:512], m)
            sc_q = ctr.val("SC") - 2
            m = proj_mm(wv_s, 0, pfB[0], extra_sc=ctr.val("SC") - 3)
            proj_ev("copy", vbf[:, 0, :], pfB[0][:, 0:512], m)
            m = proj_mm(wv_s, 1, pfB[1], extra_sc=ctr.val("SC") - 3)
            proj_ev("copy", vbf[:, 1, :], pfB[1][:, 0:512], m)

            @blk.tensor
            def _(e, pre_sc=ctr.val("SC") - 2):
                e.wait_ge(ctr.h["SC"], pre_sc)
                for kb in range(KB):
                    mi = e.matmul(pfB[2][0:68, 0:512], wfg_s[:, kb, :], xgt[:, kb, :],
                                  start=(kb == 0), stop=(kb == KB - 1))
                ctr.inc(mi, "MM")
            mm_zfg = ctr.val("MM")

            @WV
            def _(e, mm_zfg=mm_zfg):
                e.wait_ge(ctr.h["MM"], mm_zfg)
                ctr.inc(e.tensor_copy(zf_bf[:, :], pfB[2][0:64, 0:512]), "SC")
            @WS
            def _(e, mm_zfg=mm_zfg):
                e.wait_ge(ctr.h["MM"], mm_zfg)
                ctr.inc(e.activation(arows[0:4, 3, :], pfB[2][64:68, 0:512], AF.Sigmoid), "SC")
            sc_zf = ctr.val("SC")

            @blk.tensor
            def _(e, sc_zf=sc_zf):
                e.wait_ge(ctr.h["SC"], sc_zf)
                ctr.inc(e.matmul(pfB[3][:, 0:512], wf2_s[:, 0, :], zf_bf[:, :],
                                 start=True, stop=True), "MM")
                ctr.inc(e.matmul(pfB[0][:, 0:512], wf2_s[:, 1, :], zf_bf[:, :],
                                 start=True, stop=True), "MM")
            mm_zf2 = ctr.val("MM")

            @WS
            def _(e, mm_zf2=mm_zf2):
                e.wait_ge(ctr.h["MM"], mm_zf2 - 1)
                ctr.inc(e.activation(sgk[:, 0, :], pfB[3][:, 0:512], AF.Sigmoid), "SC")
                e.wait_ge(ctr.h["MM"], mm_zf2)
                ctr.inc(e.activation(sgk[:, 1, :], pfB[0][:, 0:512], AF.Sigmoid), "SC")
                for pb in range(2):
                    ctr.inc(e.activation(gkb[:, pb, :], sgk[:, pb, :], AF.Ln), "SC")
            sc_gk = ctr.val("SC")

            @WV
            def _(e, sc_k1=sc_k1):
                e.wait_ge(ctr.h["SC"], sc_k1)
                for pb in range(2):
                    ctr.inc(e.tensor_mul(ksq[:, pb, :], ksil[:, pb, :], ksil[:, pb, :]), "VE")
            ve_ksq = ctr.val("VE")

            @blk.tensor
            def _(e, ve_ksq=ve_ksq, sc_gk=sc_gk):
                e.wait_ge(ctr.h["VE"], ve_ksq)
                e.wait_ge(ctr.h["SC"], sc_gk)
                for pb in range(2):
                    mi = e.matmul(pfB[1][0:4, 0:512], sb_sel4[:, pb, :], ksq[:, pb, :],
                                  start=(pb == 0), stop=(pb == 1))
                ctr.inc(mi, "MM")
            mm_n2 = ctr.val("MM")

            @WV
            def _(e, mm_n2=mm_n2):
                e.wait_ge(ctr.h["MM"], mm_n2)
                ctr.inc(e.tensor_scalar_max(arows[0:4, 0, :], pfB[1][0:4, 0:512], 1e-24), "VE")
            ve_n2 = ctr.val("VE")

            @WS
            def _(e, ve_n2=ve_n2):
                e.wait_ge(ctr.h["VE"], ve_n2)
                ctr.inc(e.activation(arows[0:4, 1, :], arows[0:4, 0, :], AF.Sqrt), "SC")
            sc_n2 = ctr.val("SC")

            @WV
            def _(e, sc_n2=sc_n2):
                e.wait_ge(ctr.h["SC"], sc_n2)
                e.reciprocal(arows[0:4, 1, :], arows[0:4, 1, :])
                e.tensor_mul(arows[0:4, 2, :], arows[0:4, 1, :], arows[0:4, 1, :])
                e.tensor_mul(arows[0:4, 2, :], arows[0:4, 0, :], arows[0:4, 2, :])
                e.tensor_scalar(arows[0:4, 2, :], arows[0:4, 2, :], -0.5, 1.5, AL.mult, AL.add)
                e.tensor_mul(arows[0:4, 1, :], arows[0:4, 1, :], arows[0:4, 2, :])
                e.tensor_mul(arows[0:4, 2, :], arows[0:4, 1, :], arows[0:4, 3, :])
                ctr.inc(e.tensor_scalar_mul(arows[0:4, 2, :], arows[0:4, 2, :], -1.0), "VE")
            ve_r = ctr.val("VE")

            for pb in range(2):
                @blk.tensor
                def _(e, pb=pb, ve_r=ve_r, pre_ve=ctr.val("VE")):
                    e.wait_ge(ctr.h["VE"], max(ve_r, pre_ve))
                    e.matmul(pfB[2][:, 0:512], sb_bc4[:, pb, :], arows[0:4, 1, :],
                             start=True, stop=True)
                    ctr.inc(e.matmul(pfB[3][:, 0:512], sb_bc4[:, pb, :], arows[0:4, 2, :],
                                     start=True, stop=True), "MM")
                mm_b = ctr.val("MM")

                @WV
                def _(e, pb=pb, mm_b=mm_b):
                    e.wait_ge(ctr.h["MM"], mm_b)
                    e.tensor_copy(rbb[:, pb, :], pfB[2][:, 0:512])
                    ctr.inc(e.tensor_copy(rgbb[:, pb, :], pfB[3][:, 0:512]), "VE")

            @WV
            def _(e):
                for pb in range(2):
                    ctr.inc(e.tensor_mul(khat[:, pb, :], ksil[:, pb, :], rbb[:, pb, :]), "VE")
            ve_kh = ctr.val("VE")

            @WV
            def _(e, sc_gk=sc_gk, ve_kh=ve_kh, pre_sc=ctr.val("SC"),
                  pre_mm=ctr.val("MM")):
                e.wait_ge(ctr.h["SC"], max(sc_gk, pre_sc))
                e.wait_ge(ctr.h["VE"], ve_kh)
                e.wait_ge(ctr.h["MM"], pre_mm)
                for cc in range(CPS):
                    ch = slice(cc * C, (cc + 1) * C)
                    for pb in range(2):
                        e.tensor_tensor_scan(Gw[:, pb, ch], gkb[:, pb, ch], gkb[:, pb, ch],
                                             0.0, AL.add, AL.bypass)
                for cc in range(CPS):
                    for pb in range(2):
                        gm = Gw[:, pb, cc * C + 63:cc * C + 64]
                        gc_ = Gw[:, pb, cc * C + 127:cc * C + 128]
                        e.tensor_scalar(Gsc[:, cc, pb, 0:1], gm, -1.0, -SHIFT,
                                        AL.mult, AL.add)
                        e.tensor_copy(Gsc[:, cc, pb, 1:2], gc_)
                        e.tensor_add(Gsc[:, cc, pb, 3:4], gc_, Gsc[:, cc, pb, 0:1])
                        e.tensor_scalar_add(argE[:, pb, cc * C:(cc + 1) * C],
                                            Gw[:, pb, cc * C:(cc + 1) * C],
                                            Gsc[:, cc, pb, 0:1])
                        e.tensor_scalar(Gsc[:, cc, pb, 5:6], gm, 1.0, -SHIFT,
                                        AL.mult, AL.add)
                        e.tensor_scalar(E2[:, pb, cc * C:(cc + 1) * C],
                                        Gw[:, pb, cc * C:(cc + 1) * C],
                                        -1.0, Gsc[:, cc, pb, 5:6],
                                        AL.mult, AL.add)
                        e.tensor_scalar(E4[:, pb, cc * C:(cc + 1) * C],
                                        Gw[:, pb, cc * C:(cc + 1) * C],
                                        -1.0, gc_,
                                        AL.mult, AL.add)
                ctr.inc(e.engine_nop() if False else e.tensor_copy(
                    Gsc[:, 0, 0, 5:6], Gsc[:, 0, 0, 0:1]), "VE")
            ve_g = ctr.val("VE")

            @WS
            def _(e, ve_g=ve_g):
                e.wait_ge(ctr.h["VE"], ve_g)
                for pb in range(2):
                    e.activation(E1[:, pb, :], argE[:, pb, :], AF.Exp)
                    e.activation(E3[:, pb, :], Gw[:, pb, :], AF.Exp)
                    e.activation(E2[:, pb, :], E2[:, pb, :], AF.Exp)
                    e.activation(E4[:, pb, :], E4[:, pb, :], AF.Exp)
                for cc in range(CPS):
                    for pb in range(2):
                        e.activation(Gsc[:, cc, pb, 2:3], Gsc[:, cc, pb, 0:1],
                                     AF.Exp, scale=-1.0)
                        mi = e.activation(Gsc[:, cc, pb, 4:5], Gsc[:, cc, pb, 3:4], AF.Exp)
                ctr.inc(mi, "SC")
            sc_e = ctr.val("SC")

            @WV
            def _(e, sc_e=sc_e, pre_mm=ctr.val("MM")):
                e.wait_ge(ctr.h["SC"], sc_e)
                e.wait_ge(ctr.h["MM"], pre_mm)
                for cc in range(CPS):
                    for pb in range(2):
                        e.tensor_mul(egc[:, cc, pb, :], Gsc[:, cc, pb, 2:3],
                                     Gsc[:, cc, pb, 4:5])
                e.tensor_mul(Qmid[:], qsil[:], E1[:])
                e.tensor_mul(Qful[:], qsil[:], E3[:])
                e.tensor_mul(Kdot[:], khat[:], E2[:])
                e.tensor_mul(Khb[:], khat[:], E4[:])
                e.tensor_mul(tE[:], sgk[:], E1[:])
                e.tensor_mul(kg[:], ksil[:], rgbb[:])
                e.tensor_mul(Adot[:], kg[:], tE[:])
                for cc in range(CPS):
                    for pb in range(2):
                        mi = e.tensor_scalar_mul(Akm[:, pb, cc * C:(cc + 1) * C],
                                                 Adot[:, pb, cc * C:(cc + 1) * C],
                                                 Gsc[:, cc, pb, 2:3])
                ctr.inc(mi, "VE")
            ve_p = ctr.val("VE")

            @blk.tensor
            def _(e, ve_p=ve_p, pre_sc=ctr.val("SC")):
                e.wait_ge(ctr.h["VE"], ve_p)
                e.wait_ge(ctr.h["SC"], pre_sc)
                for cc in range(CPS):
                    ch = slice(cc * C, (cc + 1) * C)
                    for hh in range(4):
                        pb, hp = hh // 2, hh % 2
                        rg = 64 * hp
                        tp = (rg, 0) if rg else None
                        e.matmul(pmB[hh][:, 128 * cc:128 * cc + 64],
                                 vbf[rg:rg + 64, pb, ch],
                                 sb_idpair[rg:rg + 64, :], start=True, stop=False,
                                 tile_position=tp)
                        mi = e.matmul(pmB[hh][:, 128 * cc + 64:128 * cc + 128],
                                      Akm[rg:rg + 64, pb, ch],
                                      sb_idpair[rg:rg + 64, :], start=False, stop=True,
                                      tile_position=tp)
                ctr.inc(mi, "MM")
            mm_B = ctr.val("MM")

            @WV
            def _(e, mm_B=mm_B):
                e.wait_ge(ctr.h["MM"], mm_B)
                for hh in range(4):
                    vi = e.tensor_copy(Xb[:, hh, :, :], pmB[hh][:, 0:512])
                ctr.inc(vi, "SC")
            sc_B = ctr.val("SC")

            @blk.tensor
            def _(e, sc_B=sc_B):
                e.wait_ge(ctr.h["SC"], sc_B)
                for cc in range(CPS):
                    ch = slice(cc * C, (cc + 1) * C)
                    for hh in range(4):
                        pb, hp = hh // 2, hh % 2
                        rg = 64 * hp
                        tp = (rg, 0) if rg else None
                        ctr.inc(e.matmul(pmB[hh][:, 128 * cc:128 * (cc + 1)],
                                         Adot[rg:rg + 64, pb, ch],
                                         Kdot[rg:rg + 64, pb, ch], start=True, stop=True,
                                         tile_position=tp), "MM")
            mm_M = ctr.val("MM")

            @WV
            def _(e, mm_M=mm_M):
                e.wait_ge(ctr.h["MM"], mm_M)
                for hh in range(4):
                    e.select(PbA[:, hh, :, :], sb_maskl[:], pmB[hh][:, 0:512],
                             sb_zeros[:], add_drain=True)
                ctr.inc(e.tensor_scalar_mul(PbA[:], PbA[:], UNSCALE), "VE")
            ve_N = ctr.val("VE")

            @blk.tensor
            def _(e, ve_N=ve_N):
                e.wait_ge(ctr.h["VE"], ve_N)
                for hh in range(4):
                    for cc in range(CPS):
                        mi = e.matmul(pful[:, 512 * hh + 128 * cc:512 * hh + 128 * cc + 128],
                                      PbA[:, hh, cc, :], sb_idbf[:], start=True, stop=True)
                ctr.inc(mi, "MM")
            mm_NT = ctr.val("MM")

            @WS
            def _(e, mm_NT=mm_NT):
                e.wait_ge(ctr.h["MM"], mm_NT)
                ctr.inc(e.copy(PtA[:], pful[:, 0:2048]), "SC")
            sc_NT = ctr.val("SC")

            curP, curPt, nxtP, nxtPt = PbA, PtA, PbB, PtB
            prev_sc = sc_NT
            for lvl in range(1, 7):
                @blk.tensor
                def _(e, curP=curP, curPt=curPt, prev_sc=prev_sc,
                      pre_ve=ctr.val("VE"), lvl=lvl):
                    e.wait_ge(ctr.h["SC"], prev_sc)
                    e.wait_ge(ctr.h["VE"], pre_ve)
                    for hh in range(4):
                        for cc in range(CPS):
                            mi = e.matmul(
                                pful[:, 512 * hh + 128 * cc:512 * hh + 128 * cc + 128],
                                curPt[:, hh, cc, :], Xb[:, hh, cc, :],
                                start=True, stop=True)
                    ctr.inc(mi, "MM")
                    if lvl < 6:
                        for hh in range(4):
                            for cc in range(CPS):
                                mi = e.matmul(
                                    pmB[hh][:, 128 * cc:128 * (cc + 1)],
                                    curP[:, hh, cc, :], curPt[:, hh, cc, :],
                                    start=True, stop=True)
                        ctr.inc(mi, "MM")
                mm_ap = ctr.val("MM")

                @WV
                def _(e, mm_ap=mm_ap, lvl=lvl):
                    e.wait_ge(ctr.h["MM"], mm_ap)
                    ctr.inc(e.tensor_tensor(Xb[:], Xb[:], pful[:, 0:2048], AL.add), "VE")
                ve_ap = ctr.val("VE")

                if lvl < 6:
                    @WS
                    def _(e, nxtPt=nxtPt, mm_ap=mm_ap):
                        e.wait_ge(ctr.h["MM"], mm_ap)
                        pv = pmh[:].rearrange("p (h x) -> p h x", x=512)
                        ctr.inc(e.copy(nxtPt[:], pv), "SC")
                    sc_pt = ctr.val("SC")

                    if lvl < 5:
                        @blk.tensor
                        def _(e, curPt=curPt, curP=curP, ve_ap=ve_ap):
                            e.wait_ge(ctr.h["VE"], ve_ap)
                            for hh in range(4):
                                for cc in range(CPS):
                                    mi = e.matmul(
                                        pful[:, 512 * hh + 128 * cc:512 * hh + 128 * cc + 128],
                                        curPt[:, hh, cc, :], curP[:, hh, cc, :],
                                        start=True, stop=True)
                            ctr.inc(mi, "MM")
                        mm_bp = ctr.val("MM")

                        @WS
                        def _(e, nxtP=nxtP, mm_bp=mm_bp):
                            e.wait_ge(ctr.h["MM"], mm_bp)
                            ctr.inc(e.copy(nxtP[:], pful[:, 0:2048]), "SC")
                    prev_sc = ctr.val("SC")
                    curP, curPt, nxtP, nxtPt = nxtP, nxtPt, curP, curPt

            @blk.tensor
            def _(e, pre_ve=ctr.val("VE"), pre_sc=ctr.val("SC")):
                e.wait_ge(ctr.h["VE"], pre_ve)
                e.wait_ge(ctr.h["SC"], pre_sc)
                for cc in range(CPS):
                    ch = slice(cc * C, (cc + 1) * C)
                    for hh in range(4):
                        pb, hp = hh // 2, hh % 2
                        rg = 64 * hp
                        tp = (rg, 0) if rg else None
                        ctr.inc(e.matmul(pmB[hh][:, 128 * cc:128 * (cc + 1)],
                                         Kdot[rg:rg + 64, pb, ch],
                                         Qmid[rg:rg + 64, pb, ch], start=True, stop=True,
                                         tile_position=tp), "MM")
            mm_QK = ctr.val("MM")

            @WV
            def _(e, mm_QK=mm_QK):
                e.wait_ge(ctr.h["MM"], mm_QK)
                for hh in range(4):
                    e.select(POb[:, hh, :, :], sb_masku[:], pmB[hh][:, 0:512],
                             sb_zeros[:], add_drain=True)
                ctr.inc(e.tensor_scalar_mul(POb[:], POb[:], UNSCALE), "VE")
            ve_PO = ctr.val("VE")

            @blk.tensor
            def _(e, ve_PO=ve_PO):
                e.wait_ge(ctr.h["VE"], ve_PO)
                for cc in range(CPS):
                    ch = slice(cc * C, (cc + 1) * C)
                    for hh in range(4):
                        pb, hp = hh // 2, hh % 2
                        rg = 64 * hp
                        tp = (rg, 0) if rg else None
                        ctr.inc(e.matmul(pmB[hh][:, 64 * cc:64 * cc + 64],
                                         Khb[rg:rg + 64, pb, ch],
                                         sb_idpair[rg:rg + 64, :], start=True, stop=True,
                                         tile_position=tp), "MM")
            mm_KT = ctr.val("MM")

            @WV
            def _(e, mm_KT=mm_KT):
                e.wait_ge(ctr.h["MM"], mm_KT)
                for hh in range(4):
                    vi = e.tensor_copy(KTb[:, hh, :, :], pmB[hh][:, 0:256])
                ctr.inc(vi, "SC")
            sc_KT = ctr.val("SC")

            @blk.tensor
            def _(e, sc_KT=sc_KT, pre_ve=ctr.val("VE")):
                e.wait_ge(ctr.h["SC"], sc_KT)
                e.wait_ge(ctr.h["VE"], pre_ve)
                for cc in range(CPS):
                    for hh in range(4):
                        mi = e.matmul(pful[:, 256 * cc + 64 * hh:256 * cc + 64 * hh + 64],
                                      POb[:, hh, cc, :], Xb[:, hh, cc, 0:64],
                                      start=True, stop=True)
                ctr.inc(mi, "MM")
                for cc in range(CPS):
                    for hh in range(4):
                        q, hp = hh // 2, hh % 2
                        mi = e.matmul(
                            pful[64 * hp:64 * hp + 64,
                                 1024 + 256 * cc + 128 * q:1024 + 256 * cc + 128 * q + 128],
                            Xb[:, hh, cc, 64:128], POb[:, hh, cc, :],
                            start=True, stop=True, tile_position=(0, 64 * hp))
                ctr.inc(mi, "MM")
                for cc in range(CPS):
                    for hh in range(4):
                        q, hp = hh // 2, hh % 2
                        mi = e.matmul(
                            pmB[0][64 * hp:64 * hp + 64, 128 * cc + 64 * q:128 * cc + 64 * q + 64],
                            Xb[:, hh, cc, 64:128], KTb[:, hh, cc, :],
                            start=True, stop=True, tile_position=(0, 64 * hp))
                ctr.inc(mi, "MM")
                for cc in range(CPS):
                    for hh in range(4):
                        q, hp = hh // 2, hh % 2
                        mi = e.matmul(
                            pmB[1][64 * hp:64 * hp + 64, 128 * cc + 64 * q:128 * cc + 64 * q + 64],
                            KTb[:, hh, cc, :], Xb[:, hh, cc, 0:64],
                            start=True, stop=True, tile_position=(0, 64 * hp))
                ctr.inc(mi, "MM")
            mm_post = ctr.val("MM")

            @WV
            def _(e, mm_post=mm_post):
                e.wait_ge(ctr.h["MM"], mm_post)
                e.tensor_copy(OLb[:], pful[:, 0:1024])
                e.tensor_copy(SLb[:], pmB[1][:, 0:512])
                for cc in range(CPS):
                    ch = slice(cc * C, (cc + 1) * C)
                    for hh in range(4):
                        q, hp = hh // 2, hh % 2
                        e.tensor_tensor(
                            QHb[64 * hp:64 * hp + 64, cc, hh, :],
                            Qful[64 * hp:64 * hp + 64, q, ch],
                            pful[64 * hp:64 * hp + 64,
                                 1024 + 256 * cc + 128 * q:1024 + 256 * cc + 128 * q + 128],
                            AL.add)
                for cc in range(CPS):
                    for hh in range(4):
                        q, hp = hh // 2, hh % 2
                        vi = e.scalar_tensor_tensor(
                            WPb[64 * hp:64 * hp + 64, cc, hh, :],
                            sb_id2pair[64 * hp:64 * hp + 64, 0, :],
                            egc[64 * hp:64 * hp + 64, cc, q, :],
                            pmB[0][64 * hp:64 * hp + 64,
                                   128 * cc + 64 * q:128 * cc + 64 * q + 64],
                            AL.mult, AL.add)
                ctr.inc(vi, "VE")
            ve_chunk = ctr.val("VE")

            for cc in range(CPS):
                tglob = ts + cc * C

                @blk.tensor
                def _(e, cc=cc, pre_ve=ctr.val("VE"), pre_sc=ctr.val("SC")):
                    e.wait_ge(ctr.h["VE"], pre_ve)
                    e.wait_ge(ctr.h["SC"], pre_sc)
                    for hh in range(4):
                        mi = e.matmul(pfB[3][:, 64 * hh:64 * (hh + 1)], QHb[:, cc, hh, :],
                                      Sbf[:, hh // 2, :], start=True, stop=True)
                    ctr.inc(mi, "MM")
                    for hh in range(4):
                        q, hp = hh // 2, hh % 2
                        mi = e.matmul(pfB[3][64 * hp:64 * hp + 64,
                                             256 + 64 * q:256 + 64 * q + 64],
                                      WPb[:, cc, hh, :], Sbf[:, q, :],
                                      start=True, stop=True, tile_position=(0, 64 * hp))
                    ctr.inc(mi, "MM")
                mm_sw = ctr.val("MM")

                @WV
                def _(e, cc=cc, mm_sw=mm_sw):
                    e.wait_ge(ctr.h["MM"], mm_sw)
                    e.tensor_tensor(Otm[:], pfv(1536, 1792, 64), OLb[:, cc, :, :], AL.add)
                    ctr.inc(e.tensor_tensor(Sst[:],
                                            pful[:, 1792:1920].rearrange(
                                                "p (q x) -> p q x", x=64),
                                            SLb[:, cc, :, :], AL.add), "VE")
                ve_sw = ctr.val("VE")

                @WV
                def _(e, ve_sw=ve_sw):
                    e.wait_ge(ctr.h["VE"], ve_sw)
                    ctr.inc(e.tensor_copy(Sbf[:], Sst[:]), "SC")

                @blk.tensor
                def _(e, ve_sw=ve_sw, pre_sc=ctr.val("SC")):
                    e.wait_ge(ctr.h["VE"], ve_sw)
                    e.wait_ge(ctr.h["SC"], pre_sc)
                    for hh in range(4):
                        mi = e.matmul(pfB[0][0:64, 128 * hh:128 * (hh + 1)], Otm[:, hh, :],
                                      sb_idf32[:], start=True, stop=True)
                    ctr.inc(mi, "MM")
                mm_ot = ctr.val("MM")

                @WV
                def _(e, cc=cc, mm_ot=mm_ot, pre_s2=ctr.val("S2")):
                    e.wait_ge(ctr.h["MM"], mm_ot)
                    e.wait_ge(ctr.h["S2"], pre_s2)
                    ctr.inc(e.tensor_copy(Ovm[:, :, cc * C:(cc + 1) * C],
                                          pful[0:64, 0:512].rearrange("p (h x) -> p h x", x=128)),
                            "SC")

            sc_o = ctr.val("SC")
            dsh = ts // 2048
            pre_s2m = ctr.val("S2")

            @WV
            def _(e, sc_o=sc_o, pre_s2m=pre_s2m):
                e.wait_ge(ctr.h["SC"], sc_o)
                e.wait_ge(ctr.h["S2"], pre_s2m)
                e.tensor_scalar_mul(Ovm_lo[:], Ovm[:], sb_msel[:, 0:1])
                ctr.inc(e.tensor_scalar_mul(Ovm_hi[:], Ovm[:], sb_msel[:, 1:2]), "VE")
            ve_ovm = ctr.val("VE")

            @WS
            def _(e, ve_ovm=ve_ovm, dsh=dsh, tl2=ts % 2048):
                e.wait_ge(ctr.h["VE"], ve_ovm)
                ctr.wait(e, "S2")
                for hh in range(4):
                    ctr.inc(e.dma_start(
                        out=a2a_in.ap()[dsh, 64 * hh:64 * hh + 64, tl2:tl2 + SUPER],
                        in_=Ovm_lo[:, hh, :]), "S2", 16)
                    ctr.inc(e.dma_start(
                        out=a2a_in.ap()[dsh + 4, 64 * hh:64 * hh + 64, tl2:tl2 + SUPER],
                        in_=Ovm_hi[:, hh, :]), "S2", 16)

        free_to(mark0)

    def post_attn():
        mark0 = len(_guards)
        wo_s = alloc("wo_s", (128, KB, KB, 128), BF16)
        wog1_s = alloc("wog1_s", (128, KB, 64), BF16)
        wog2_s = alloc("wog2_s", (64, KB, 128), BF16)
        o_t = alloc("o_t", (128, KB, 512))
        og_t = alloc("og_t", (128, KB, 512))
        oln = alloc("oln", (128, KB, 512), BF16)
        xlt = alloc("xlt", (128, KB, 512), BF16)
        zog1 = alloc("zog1", (64, 512), BF16)
        scr4 = alloc("scr4", (128, 4, 512))
        x1t = alloc("x1t", (128, KB, 512))
        x2t = alloc("x2t", (128, KB, 512))

        @WG
        def _(e):
            ctr.wait(e, "CC")
            ctr.inc(e.dma_start(out=wo_s[:],
                                in_=wo_d.rearrange("(k p) (o n) -> p k o n", p=128, n=128)),
                    "LW", 16)
            ctr.inc(e.dma_start(out=wog1_s[:],
                                in_=wog1_d.rearrange("(k p) n -> p k n", p=128)), "LW", 16)
            ctr.inc(e.dma_start(out=wog2_s[:],
                                in_=wog2_d.rearrange("k (o n) -> k o n", n=128)), "LW", 16)
        lw_p4 = ctr.val("LW")

        for tt in range(4):
            t0 = tt * 512
            markt = len(_guards)

            pre = {nm: ctr.val(nm) for nm in ["MM", "VE", "SC", "GP", "ST"]}

            @blk.sync
            def _(e, t0=t0, pre=dict(pre)):
                ctr.wait(e, "LD")
                ctr.wait(e, "CC")
                for nm in ["MM", "VE", "SC"]:
                    e.wait_ge(ctr.h[nm], pre[nm])
                for kb in range(KB):
                    ctr.inc(e.dma_start(
                        out=o_t[:, kb, :],
                        in_=a2a_out.ap()[kb // 2, 128 * (kb % 2):128 * (kb % 2) + 128,
                                         t0:t0 + 512]), "LD", 16)
                for kb in range(KB):
                    ctr.inc(e.dma_start(
                        out=xlt[:, kb, :],
                        in_=ag_in.ap()[kb * 128:(kb + 1) * 128, t0:t0 + 512]), "LD", 16)
            ld4 = ctr.val("LD")

            @WG
            def _(e, t0=t0, ld4=ld4):
                e.wait_ge(ctr.h["LD"], ld4)
                ctr.wait(e, "LW")
                for kb in range(KB):
                    ctr.inc(e.dma_start(
                        out=o_t[:, kb, :],
                        in_=a2a_out.ap()[4 + kb // 2, 128 * (kb % 2):128 * (kb % 2) + 128,
                                         t0:t0 + 512], accum_op=AL.add), "LW", 16)
            lw4 = ctr.val("LW")

            p4a = pal("p4a" + str(tt), (128, 2048))
            p4B = [p4a[:, i * 512:(i + 1) * 512] for i in range(4)]

            @blk.tensor
            def _(e, ld4=ld4):
                e.wait_ge(ctr.h["LD"], ld4)
                e.wait_ge(ctr.h["LW"], lw_p4)
                for kb in range(KB):
                    mi = e.matmul(p4B[0][0:64, 0:512], wog1_s[:, kb, :], xlt[:, kb, :],
                                  start=(kb == 0), stop=(kb == KB - 1))
                ctr.inc(mi, "MM")
            mm_g1 = ctr.val("MM")

            @WV
            def _(e, mm_g1=mm_g1):
                e.wait_ge(ctr.h["MM"], mm_g1)
                ctr.inc(e.tensor_copy(zog1[:], p4B[0][0:64, 0:512]), "SC")
            sc_g1 = ctr.val("SC")

            @blk.tensor
            def _(e, sc_g1=sc_g1):
                e.wait_ge(ctr.h["SC"], sc_g1)
                for ob in range(KB):
                    if ob >= 3:
                        e.wait_ge(ctr.h["SC"], sc_g1 + ob - 2)
                    ctr.inc(e.matmul(p4B[ob % 4][:, 0:512], wog2_s[:, ob, :], zog1[:],
                                     start=True, stop=True), "MM")
            mm_g2 = ctr.val("MM")

            @WS
            def _(e, mm_g2=mm_g2):
                for ob in range(KB):
                    e.wait_ge(ctr.h["MM"], mm_g2 - KB + ob + 1)
                    ctr.inc(e.activation(og_t[:, ob, :], p4B[ob % 4][:, 0:512],
                                         AF.Sigmoid), "SC")
            sc_og = ctr.val("SC")

            @WV
            def _(e, sc_og=sc_og, ld4=ld4, lw4=lw4):
                e.wait_ge(ctr.h["SC"], sc_og)
                e.wait_ge(ctr.h["LD"], ld4)
                e.wait_ge(ctr.h["LW"], lw4)
                for kb in range(KB):
                    ctr.inc(e.tensor_mul(o_t[:, kb, :], o_t[:, kb, :], og_t[:, kb, :]), "VE")
            free_to(markt)

            p_st = pal("pst4" + str(tt), (128, 512))
            p_sq = pal("psq4" + str(tt), (128, 512))
            p_bc = pal("pbc4" + str(tt), (128, 1024))
            ln_stage([o_t[:, kb, :] for kb in range(KB)], 512,
                     [oln[:, kb, :] for kb in range(KB)], scr4, p_st, p_sq, p_bc,
                     nw=sb_normw)
            free_to(markt)

            p4w = pal("p4w" + str(tt), (128, 1024))
            mm4 = ctr.val("MM")
            ve4 = ctr.val("VE")
            ld4b = ctr.val("LD")
            st4 = ctr.val("ST")

            @blk.sync
            def _(e, t0=t0, ve4=ve4):
                ctr.wait(e, "LD")
                e.wait_ge(ctr.h["VE"], ve4)
                for ob in range(KB):
                    ctr.inc(e.dma_start(out=x1t[:, ob, :],
                                        in_=x1_sp[ob * 128:(ob + 1) * 128, t0:t0 + 512]),
                            "LD", 16)
            ld_x1 = ctr.val("LD")

            @blk.tensor
            def _(e, mm4=mm4, ve4=ve4, pre_ve=ctr.val("VE")):
                e.wait_ge(ctr.h["VE"], pre_ve)
                for ob in range(KB):
                    if ob >= 2:
                        e.wait_ge(ctr.h["VE"], ve4 + ob - 1)
                    bank = p4w[:, (ob % 2) * 512:(ob % 2) * 512 + 512]
                    for kb in range(KB):
                        mi = e.matmul(bank, wo_s[:, kb, ob, :], oln[:, kb, :],
                                      start=(kb == 0), stop=(kb == KB - 1))
                    ctr.inc(mi, "MM")

            @WV
            def _(e, mm4=mm4, ld_x1=ld_x1, st4=st4):
                e.wait_ge(ctr.h["LD"], ld_x1)
                e.wait_ge(ctr.h["ST"], st4)
                for ob in range(KB):
                    e.wait_ge(ctr.h["MM"], mm4 + ob + 1)
                    ctr.inc(e.tensor_add(x2t[:, ob, :],
                                         p4w[:, (ob % 2) * 512:(ob % 2) * 512 + 512],
                                         x1t[:, ob, :]), "VE")
            ve_x2 = ctr.val("VE")

            dst4 = yT if _STAGE == 2 else x2_sp

            @blk.sync
            def _(e, t0=t0, ve_x2=ve_x2, dst4=dst4):
                ctr.wait(e, "ST")
                for ob in range(KB):
                    e.wait_ge(ctr.h["VE"], ve_x2 - KB + ob + 1)
                    ctr.inc(e.dma_start(out=dst4[ob * 128:(ob + 1) * 128, t0:t0 + 512],
                                        in_=x2t[:, ob, :]), "ST", 16)
            free_to(markt)

        free_to(mark0)

    ffn_pass(xsT, x1_sp, make_ln_out=True, final_store=(_STAGE == 1), uid="f1")
    barrier()

    if _STAGE >= 2:
        @WG
        def _(e):
            ctr.wait(e, "ST")
            ctr.inc(e.collective_compute(
                "AllGather", AL.bypass,
                replica_groups=[[0, 1, 2, 3], [4, 5, 6, 7]],
                ins=[ag_in.ap().opt()], outs=[xg.ap().opt()]), "CC")
        attn()
        barrier()

        @WG
        def _(e):
            ctr.wait(e, "S2")
            with allow_ici_all_to_all():
                ctr.inc(e.collective_compute(
                    "AllToAll", AL.bypass,
                    replica_groups=[[0, 1, 2, 3, 4, 5, 6, 7]],
                    ins=[a2a_in.ap().opt()], outs=[a2a_out.ap().opt()]), "CC")
        post_attn()
        barrier()

    if _STAGE >= 3:
        ffn_pass(x2_sp, None, make_ln_out=False, final_store=True, uid="f2")

    @blk.sync
    def _(e):
        ctr.wait(e, "ST")

    blk_guard.__exit__(None, None, None)
    free_to(0)
    return nc


def _prep_inputs(inputs):
    f32 = np.float32
    x = np.ascontiguousarray(inputs["L0_attn_x"], dtype=f32).reshape(TOK, D)
    wq = inputs["L0_attn_Wq"]; wk = inputs["L0_attn_Wk"]; wv = inputs["L0_attn_Wv"]
    wgam = inputs["L0_attn_Wgamma"]; wf1 = inputs["L0_attn_Wf1"]; wf2 = inputs["L0_attn_Wf2"]
    wog1 = inputs["L0_attn_Wog1"]; wog2 = inputs["L0_attn_Wog2"]
    normw = inputs["L0_attn_norm_w"]; wo = inputs["L0_attn_Wo"]

    bf = ml_dtypes.bfloat16
    consts = {
        "c_idbf": np.eye(128, dtype=f32).astype(bf),
        "c_idf32": np.eye(128, dtype=f32),
        "c_idpair": np.concatenate([np.eye(64, dtype=f32)] * 2, axis=0).astype(bf),
        "c_id2pair": np.stack([np.concatenate([np.eye(64, dtype=f32)] * 2, axis=0)] * 2,
                              axis=1),
        "c_onescol": np.ones((128, 1), f32),
        "c_onesrow": np.ones((1, 128), f32),
        "c_sel4": np.zeros((128, 2, 4), f32),
        "c_bc4": np.zeros((4, 2, 128), f32),
        "c_maskl4": np.broadcast_to(np.tril(np.ones((128, 128), np.uint8), -1)[:, None, :],
                                    (128, 4, 128)).copy(),
        "c_masku4": np.broadcast_to(np.tril(np.ones((128, 128), np.uint8), 0).T[:, None, :],
                                    (128, 4, 128)).copy(),
    }
    for hh in range(4):
        pb, hp = hh // 2, hh % 2
        consts["c_sel4"][64 * hp:64 * hp + 64, pb, hh] = 1.0
        consts["c_bc4"][hh, pb, 64 * hp:64 * hp + 64] = 1.0

    shared = {
        "wfg": np.ascontiguousarray(
            np.concatenate([wf1, wgam], axis=1), dtype=f32),
        "wog1": np.ascontiguousarray(wog1, f32),
        "wog2": np.ascontiguousarray(wog2, f32),
        "wo": np.ascontiguousarray(wo, f32),
        "normw": np.ascontiguousarray(np.asarray(normw, f32).reshape(KB, 128).T),
        "w1t": np.ascontiguousarray(inputs["L0_ffn_w1_w"].T, f32),
        "w2t": np.ascontiguousarray(inputs["L0_ffn_w2_w"].T, f32),
        "h1t": np.ascontiguousarray(inputs["L0_ffn_h1_w"].T, f32),
        "h2t": np.ascontiguousarray(inputs["L0_ffn_h2_w"].T, f32),
        "w3t": np.ascontiguousarray(inputs["L0_ffn_w3_w"].T, f32),
        "s1": np.ascontiguousarray(np.asarray(inputs["L0_ffn_w1_s"], f32).reshape(HB, 128).T),
        "s2": np.ascontiguousarray(np.asarray(inputs["L0_ffn_w2_s"], f32).reshape(HB, 128).T),
        "sh1": np.ascontiguousarray(np.asarray(inputs["L0_ffn_h1_s"], f32).reshape(HB, 128).T),
        "sh2": np.ascontiguousarray(np.asarray(inputs["L0_ffn_h2_s"], f32).reshape(HB, 128).T),
        "s3": np.ascontiguousarray(np.asarray(inputs["L0_ffn_w3_s"], f32).reshape(KB, 128).T),
    }
    shared.update(consts)

    in_maps = []
    for c in range(NCORES):
        hsel = slice(256 * (c % 4), 256 * (c % 4) + 256)
        gsel = slice(4 * (c % 4), 4 * (c % 4) + 4)
        m = dict(shared)
        m["x_slabT"] = np.ascontiguousarray(x[c * SLAB:(c + 1) * SLAB].T, f32)
        m["wq"] = np.ascontiguousarray(wq[:, hsel], f32)
        m["wk"] = np.ascontiguousarray(wk[:, hsel], f32)
        m["wv"] = np.ascontiguousarray(wv[:, hsel], f32)
        m["wf2"] = np.ascontiguousarray(wf2[:, hsel], f32)
        m["wfg"] = np.ascontiguousarray(
            np.concatenate([wf1, wgam[:, gsel]], axis=1), f32)
        msel = np.zeros((64, 2), f32)
        msel[:, 0 if c < 4 else 1] = 1.0
        m["c_msel"] = msel
        in_maps.append(m)
    return in_maps


def kernel(**inputs):
    in_maps = _prep_inputs(inputs)
    nc = build_program()
    res = run_bass_kernel_spmd(nc, in_maps, core_ids=list(range(NCORES)))
    out = np.concatenate([np.asarray(r["y_slabT"]).T for r in res.results], axis=0)
    return np.ascontiguousarray(out.reshape(B, N, D), dtype=np.float32)


# version 06
# speedup vs baseline: 1.0128x; 1.0128x over previous
import numpy as np
import ml_dtypes
import concourse.bass as bass
import concourse.mybir as mybir
from concourse.bass_utils import run_bass_kernel_spmd
from concourse.aot_env import allow_ici_all_to_all

F32 = mybir.dt.float32
BF16 = mybir.dt.bfloat16
U8 = mybir.dt.uint8
AF = mybir.ActivationFunctionType
AL = mybir.AluOpType

NCORES = 8
B, N, D, H, HD = 2, 8192, 1024, 16, 64
TOK = B * N
SLAB = TOK // NCORES
C = 128
NCHUNK = N // C
SUPER = 512
NSUPER = N // SUPER
CPS = SUPER // C
DHID = 4096
KB = D // 128
HB = DHID // 128
HALF = 1024

_STAGE = 3
_SUB = None
SHIFT = 32 * float(np.log(2.0))
UNSCALE = 2.0 ** 64


class Ctr:
    def __init__(self, nc):
        self.nc = nc
        self.h = {}
        self.v = {}

    def new(self, name):
        self.h[name] = self.nc.alloc_semaphore(name=name)
        self.v[name] = 0

    def inc(self, inst, name, n=1):
        inst.then_inc(self.h[name], n)
        for st in _DW._state.values():
            if st.get("last") is inst:
                st["pending_inc"] = True
        self.v[name] += n
        return self.v[name]

    def val(self, name):
        return self.v[name]

    def wait(self, eng, name, val=None):
        eng.wait_ge(self.h[name], self.v[name] if val is None else val)



_DW_OPS = {
    "tensor_tensor", "tensor_add", "tensor_sub", "tensor_mul", "tensor_max",
    "tensor_scalar", "tensor_single_scalar", "tensor_scalar_add",
    "tensor_scalar_sub", "tensor_scalar_mul", "tensor_scalar_max",
    "tensor_scalar_min", "tensor_copy", "tensor_relu", "tensor_reduce",
    "reduce_max", "reduce_sum", "scalar_tensor_tensor", "tensor_tensor_scan",
    "select", "reciprocal", "memset", "memzero", "activation", "copy", "mul",
    "add", "sign", "square", "sqrt", "bn_stats", "bn_aggr", "transpose",
    "stream_shuffle", "copy_predicated",
}


def _ap_span(ap):
    try:
        name = ap.tensor.name
    except Exception:
        return None
    try:
        off = int(ap.offset)
    except Exception:
        off = 0
    span = 1
    try:
        for step, count in ap.ap:
            span += abs(int(step)) * (int(count) - 1)
    except Exception:
        span = 1 << 40
    return (name, off, off + span)


class _DW:

    _state = {}

    def __init__(self, e):
        object.__setattr__(self, "_e", e)
        key = getattr(e, "engine", None)
        key = str(key) if key is not None else id(type(e))
        st = _DW._state.setdefault(key, {"W": [], "R": []})
        object.__setattr__(self, "_st", st)

    @staticmethod
    def _overlap(iv, lst):
        if iv is None:
            return True
        name, lo, hi = iv
        for n2, l2, h2 in lst:
            if n2 == name and lo < h2 and l2 < hi:
                return True
        return False

    def __getattr__(self, name):
        a = getattr(self._e, name)
        if name == "drain":
            st = self._st

            def dr(_a=a):
                st["W"] = []
                st["R"] = []
                st["pending_inc"] = False
                return _a()
            return dr
        if name not in _DW_OPS:
            return a
        st = self._st

        def call(*args, _a=a, **kw):
            from concourse.ap import AP as _AP
            aps = [x for x in args if isinstance(x, _AP)]
            aps += [v for v in kw.values() if isinstance(v, _AP)]
            outs = aps[:1]
            ins = aps[1:]
            need = st.get("pending_inc", False)
            if not need:
                for x in ins:
                    if _DW._overlap(_ap_span(x), st["W"]):
                        need = True
                        break
            if not need:
                for x in outs:
                    iv = _ap_span(x)
                    if _DW._overlap(iv, st["W"]) or _DW._overlap(iv, st["R"]):
                        need = True
                        break
            if need or len(st["W"]) > 64:
                self._e.drain()
                st["W"] = []
                st["R"] = []
                st["pending_inc"] = False
            r = _a(*args, **kw)
            for x in outs:
                iv = _ap_span(x)
                st["W"].append(iv if iv else ("?", 0, 1 << 40))
            for x in ins:
                iv = _ap_span(x)
                st["R"].append(iv if iv else ("?", 0, 1 << 40))
            st["last"] = r
            return r
        return call


def build_program():
    _DW._state = {}
    nc = bass.Bass("TRN2", target_bir_lowering=False, debug=False)

    def din(name, shape, dtype=F32):
        return nc.dram_tensor(name, list(shape), dtype, kind="ExternalInput").ap()

    xsT = din("x_slabT", (D, SLAB))
    wq_d = din("wq", (D, 256)); wk_d = din("wk", (D, 256)); wv_d = din("wv", (D, 256))
    wfg_d = din("wfg", (D, 68))
    wf2_d = din("wf2", (64, 256))
    wog1_d = din("wog1", (D, 64)); wog2_d = din("wog2", (64, D))
    wo_d = din("wo", (D, D))
    normw_d = din("normw", (128, KB))
    w1t = din("w1t", (D, DHID)); w2t = din("w2t", (D, DHID))
    h1t = din("h1t", (DHID, DHID)); h2t = din("h2t", (DHID, DHID))
    w3t = din("w3t", (DHID, D))
    s1_d = din("s1", (128, HB)); s2_d = din("s2", (128, HB))
    sh1_d = din("sh1", (128, HB)); sh2_d = din("sh2", (128, HB))
    s3_d = din("s3", (128, KB))
    c_idbf = din("c_idbf", (128, 128), BF16)
    c_idf32 = din("c_idf32", (128, 128))
    c_idpair = din("c_idpair", (128, 64), BF16)
    c_id2pair = din("c_id2pair", (128, 2, 64))
    c_onescol = din("c_onescol", (128, 1))
    c_onesrow = din("c_onesrow", (1, 128))
    c_sel4 = din("c_sel4", (128, 2, 4))
    c_bc4 = din("c_bc4", (4, 2, 128))
    c_msel = din("c_msel", (64, 2))
    c_maskl4 = din("c_maskl4", (128, 4, 128), U8)
    c_masku4 = din("c_masku4", (128, 4, 128), U8)

    yT = nc.dram_tensor("y_slabT", [D, SLAB], F32, kind="ExternalOutput").ap()

    x1_sp = nc.dram_tensor("x1_sp", [D, SLAB], F32).ap()
    x2_sp = nc.dram_tensor("x2_sp", [D, SLAB], F32).ap()
    ag_in = nc.dram_tensor("ag_in", [D, SLAB], BF16)
    xg = nc.dram_tensor("xg", [4, D, SLAB], BF16)
    a2a_in = nc.dram_tensor("a2a_in", [8, 256, SLAB], F32)
    a2a_out = nc.dram_tensor("a2a_out", [8, 256, SLAB], F32)

    ctr = Ctr(nc)
    for s in ["LD", "LW", "LWa", "LWb", "ST", "S2", "MM", "VE", "SC", "GP", "CC"]:
        ctr.new(s)

    _guards = []

    def alloc(name, shape, dtype=F32):
        g = nc.sbuf_tensor(name, list(shape), dtype)
        h = g.__enter__()
        _guards.append(g)
        return h

    def pal(name, shape):
        g = nc.psum_tensor(name, list(shape), F32)
        h = g.__enter__()
        _guards.append(g)
        return h

    def free_to(mark):
        while len(_guards) > mark:
            _guards.pop().__exit__(None, None, None)

    sb_idbf = alloc("sb_idbf", (128, 128), BF16)
    sb_idf32 = alloc("sb_idf32", (128, 128))
    sb_idpair = alloc("sb_idpair", (128, 64), BF16)
    sb_id2pair = alloc("sb_id2pair", (128, 2, 64))
    sb_onescol = alloc("sb_onescol", (128, 1))
    sb_onesrow = alloc("sb_onesrow", (1, 128))
    sb_sel4 = alloc("sb_sel4", (128, 2, 4))
    sb_bc4 = alloc("sb_bc4", (4, 2, 128))
    sb_msel = alloc("sb_msel", (64, 2))
    sb_maskl = alloc("sb_maskl", (128, 4, 128), U8)
    sb_masku = alloc("sb_masku", (128, 4, 128), U8)
    sb_zeros = alloc("sb_zeros", (128, 4, 128))
    sb_normw = alloc("sb_normw", (128, KB))
    sb_rows = alloc("sb_rows", (1, 2048))
    sb_scales = alloc("sb_scales", (128, 6, HB))
    PMARK = len(_guards)

    blk_guard = nc.Block()
    blk = blk_guard.__enter__()

    def WV(f):
        blk.vector(lambda e: f(_DW(e)))

    def WS(f):
        blk.scalar(lambda e: f(_DW(e)))

    def WG(f):
        blk.gpsimd(lambda e: f(_DW(e)))


    @blk.sync
    def _(e):
        for dst, src in [(sb_idbf, c_idbf), (sb_idf32, c_idf32), (sb_idpair, c_idpair),
                         (sb_id2pair, c_id2pair), (sb_onescol, c_onescol),
                         (sb_onesrow, c_onesrow), (sb_sel4, c_sel4), (sb_bc4, c_bc4),
                         (sb_maskl, c_maskl4), (sb_masku, c_masku4)]:
            ctr.inc(e.dma_start(out=dst[:], in_=src), "LD", 16)
        ctr.inc(e.dma_start(out=sb_msel[:], in_=c_msel), "LD", 16)
        ctr.inc(e.dma_start(out=sb_normw[:], in_=normw_d), "LD", 16)
        for i, s in zip([0, 1, 3, 4], [s1_d, s2_d, sh1_d, sh2_d]):
            ctr.inc(e.dma_start(out=sb_scales[:, i, :], in_=s), "LD", 16)
        ctr.inc(e.dma_start(out=sb_scales[:, 5, 0:KB], in_=s3_d), "LD", 16)

    @WV
    def _(e):
        ctr.wait(e, "LD")
        e.memset(sb_zeros[:], 0.0)
        e.tensor_mul(sb_scales[:, 2, :], sb_scales[:, 0, :], sb_scales[:, 1, :])
        ctr.inc(e.tensor_scalar_mul(sb_scales[:, 5, 0:KB], sb_scales[:, 5, 0:KB], 0.5), "VE")

    def barrier():
        names = ["LD", "LW", "LWa", "LWb", "ST", "S2", "MM", "VE", "SC", "GP", "CC"]
        for sec in [blk.sync, blk.gpsimd, blk.scalar, blk.vector, blk.tensor]:
            def body(e, _n=tuple(names)):
                for nm in _n:
                    if ctr.val(nm):
                        ctr.wait(e, nm)
            sec(body)

    def ln_stage(xt, twid, out_tiles, scr, p_stat, p_sq, p_bc, nw=None):
        nq = twid // 512
        ld_now = ctr.val("LD")
        ve_now = ctr.val("VE")

        @blk.tensor
        def _(e):
            e.wait_ge(ctr.h["LD"], ld_now)
            e.wait_ge(ctr.h["VE"], ve_now)
            for tq in range(nq):
                for kb in range(KB):
                    mi = e.matmul(p_stat[0:1, tq * 512:(tq + 1) * 512], sb_onescol[:],
                                  xt[kb][:, tq * 512:(tq + 1) * 512],
                                  start=(kb == 0), stop=(kb == KB - 1))
            ctr.inc(mi, "MM")
        mm_sx = ctr.val("MM")

        sc0 = ctr.val("SC")

        @WS
        def _(e):
            e.wait_ge(ctr.h["LD"], ld_now)
            e.wait_ge(ctr.h["VE"], ve_now)
            for kb in range(KB):
                if kb >= 2:
                    e.wait_ge(ctr.h["MM"], mm_sx + kb - 1)
                ctr.inc(e.activation(scr[:, kb % 2, 0:twid], xt[kb], AF.Square), "SC")

        @blk.tensor
        def _(e):
            for kb in range(KB):
                e.wait_ge(ctr.h["SC"], sc0 + kb + 1)
                for tq in range(nq):
                    mi = e.matmul(p_sq[0:1, tq * 512:(tq + 1) * 512], sb_onescol[:],
                                  scr[:, kb % 2, tq * 512:(tq + 1) * 512],
                                  start=(kb == 0), stop=(kb == KB - 1))
                ctr.inc(mi, "MM")
        mm_sq = ctr.val("MM")

        r0 = scr[0:1, 0, 0:twid]
        r1 = scr[0:1, 1, 0:twid]

        @WV
        def _(e):
            e.wait_ge(ctr.h["MM"], mm_sq)
            e.tensor_scalar_mul(sb_rows[0:1, 0:twid], p_stat[0:1, 0:twid], 1.0 / D)
            e.tensor_scalar_mul(p_sq[0:1, 0:twid], p_sq[0:1, 0:twid], 1.0 / D)
            e.tensor_mul(r0, sb_rows[0:1, 0:twid], sb_rows[0:1, 0:twid])
            e.tensor_sub(p_sq[0:1, 0:twid], p_sq[0:1, 0:twid], r0)
            ctr.inc(e.tensor_scalar_add(p_sq[0:1, 0:twid], p_sq[0:1, 0:twid], 1e-5), "VE")
        v_var = ctr.val("VE")

        @WS
        def _(e):
            e.wait_ge(ctr.h["VE"], v_var)
            ctr.inc(e.activation(p_stat[0:1, 0:twid], p_sq[0:1, 0:twid], AF.Sqrt), "SC")
        sc_sq = ctr.val("SC")

        @WV
        def _(e):
            e.wait_ge(ctr.h["SC"], sc_sq)
            e.reciprocal(r1, p_stat[0:1, 0:twid])
            e.tensor_mul(r0, r1, r1)
            e.tensor_mul(r0, p_sq[0:1, 0:twid], r0)
            e.tensor_scalar(r0, r0, -0.5, 1.5, AL.mult, AL.add)
            ctr.inc(e.tensor_mul(sb_rows[0:1, 1024:1024 + twid], r1, r0), "VE")

        for tq in range(nq):
            v_pre = ctr.val("VE")

            @blk.tensor
            def _(e, tq=tq, v_pre=v_pre):
                e.wait_ge(ctr.h["VE"], v_pre)
                e.matmul(p_bc[:, 0:512], sb_onesrow[:],
                         sb_rows[0:1, tq * 512:(tq + 1) * 512], start=True, stop=True)
                ctr.inc(e.matmul(p_bc[:, 512:1024], sb_onesrow[:],
                                 sb_rows[0:1, 1024 + tq * 512:1024 + (tq + 1) * 512],
                                 start=True, stop=True), "MM")
            mm_bc = ctr.val("MM")

            @WV
            def _(e, tq=tq, mm_bc=mm_bc):
                e.wait_ge(ctr.h["MM"], mm_bc)
                sl = slice(tq * 512, (tq + 1) * 512)
                for kb in range(KB):
                    e.tensor_sub(scr[:, 2, 0:512], xt[kb][:, sl], p_bc[:, 0:512])
                    if nw is None:
                        vi = e.tensor_mul(out_tiles[kb][:, sl], scr[:, 2, 0:512],
                                          p_bc[:, 512:1024])
                    else:
                        vi = e.scalar_tensor_tensor(out_tiles[kb][:, sl], scr[:, 2, 0:512],
                                                    nw[:, kb:kb + 1], p_bc[:, 512:1024],
                                                    AL.mult, AL.mult)
                ctr.inc(vi, "VE")

    def ffn_pass(src_ap, spill_ap, make_ln_out, final_store, uid):
        mark0 = len(_guards)
        xln = alloc("xln" + uid, (128, KB, HALF), BF16)
        Hb = alloc("Hb" + uid, (128, HB, HALF), BF16)
        Gb = alloc("Gb" + uid, (128, HB, HALF), BF16)
        wb = alloc("wb" + uid, (128, 2, HB, 128), BF16)
        scr = alloc("scr" + uid, (128, 4, HALF))
        xres = alloc("xres" + uid, (128, 2, 512))

        GbF = Gb[:].bitcast(F32)
        HbF = Hb[:].bitcast(F32)

        for hf in range(2):
            t0 = hf * HALF
            markp = len(_guards)

            p_stat = pal("pstA" + uid + str(hf), (128, 1024))
            p_sq = pal("psqA" + uid + str(hf), (128, 1024))
            p_bc = pal("pbcA" + uid + str(hf), (128, 1024))

            xtiles = [HbF[:, 4 * kb:4 * kb + 4, :].rearrange("p a b -> p (a b)")
                      for kb in range(KB)]
            xtiles = [HbF[:, 2 * kb:2 * kb + 2, :].rearrange("p a b -> p (a b)")
                      for kb in range(KB)]

            @blk.sync
            def _(e):
                for nm in ["MM", "SC", "GP", "VE", "ST", "LD"]:
                    if ctr.val(nm):
                        ctr.wait(e, nm)
                for kb in range(KB):
                    ctr.inc(e.dma_start(out=xtiles[kb],
                                        in_=src_ap[kb * 128:(kb + 1) * 128, t0:t0 + HALF]),
                            "LD", 16)

            ln_stage(xtiles, HALF, [xln[:, kb, :] for kb in range(KB)],
                     scr, p_stat, p_sq, p_bc)
            free_to(markp)
            if _SUB == 'ln':
                break

            p_f = pal("pf" + uid + str(hf), (128, 2048))
            p_f2 = pal("pf2" + uid + str(hf), (128, 2048))
            pbank = [p_f[:, i * 512:(i + 1) * 512] for i in range(4)] + \
                    [p_f2[:, i * 512:(i + 1) * 512] for i in range(4)]

            lwp0 = [ctr.val("LWa"), ctr.val("LWb")]
            mm0 = ctr.val("MM")
            ve_xln = ctr.val("VE")
            ve0 = ctr.val("VE")

            sc0_pre = ctr.val("SC")
            gp_base = ctr.val("GP")

            def _w12_gp_mul(e, ob):
                p = ob % 2
                for tq in range(2):
                    e.wait_ge(ctr.h["SC"], sc0_pre + 4 * ob + 2 * tq + 2)
                    ctr.inc(e.tensor_mul(scr[:, p, tq * 512:(tq + 1) * 512],
                                         scr[:, p, tq * 512:(tq + 1) * 512],
                                         scr[:, 2 + p, tq * 512:(tq + 1) * 512]), "GP")

            @WG
            def _(e):
                e.wait_ge(ctr.h["VE"], ve_xln)
                for ob in range(HB):
                    p = ob % 2
                    sem = "LWa" if p == 0 else "LWb"
                    if ob >= 2:
                        e.wait_ge(ctr.h["MM"], mm0 + (ob - 1) * 32)
                    ctr.inc(e.dma_start(
                        out=wb[:, p, 0:8, :],
                        in_=w1t[:, ob * 128:(ob + 1) * 128].rearrange("(k p) n -> p k n", p=128)),
                        sem, 16)
                    ctr.inc(e.dma_start(
                        out=wb[:, p, 8:16, :],
                        in_=w2t[:, ob * 128:(ob + 1) * 128].rearrange("(k p) n -> p k n", p=128)),
                        sem, 16)
                    if ob >= 2:
                        _w12_gp_mul(e, ob - 2)
                _w12_gp_mul(e, HB - 2)
                _w12_gp_mul(e, HB - 1)

            @blk.tensor
            def _(e):
                e.wait_ge(ctr.h["VE"], ve_xln)
                for ob in range(HB):
                    p = ob % 2
                    sem = "LWa" if p == 0 else "LWb"
                    e.wait_ge(ctr.h[sem], lwp0[p] + 32 * (ob // 2 + 1))
                    if ob >= 2:
                        e.wait_ge(ctr.h["VE"], ve0 + 2 * (ob - 1))
                    for w12 in range(2):
                        for kb in range(KB):
                            for tq in range(2):
                                bank = pbank[4 * p + 2 * w12 + tq]
                                mi = e.matmul(bank[:, 0:512], wb[:, p, 8 * w12 + kb, :],
                                              xln[:, kb, tq * 512:(tq + 1) * 512],
                                              start=(kb == 0), stop=(kb == KB - 1))
                    ctr.inc(mi, "MM", 32)

            sc0 = ctr.val("SC")

            @WS
            def _(e):
                for ob in range(HB):
                    p = ob % 2
                    e.wait_ge(ctr.h["MM"], mm0 + 32 * (ob + 1))
                    for tq in range(2):
                        ctr.inc(e.activation(scr[:, p, tq * 512:(tq + 1) * 512],
                                             pbank[4 * p + tq][:, 0:512], AF.Sigmoid,
                                             scale=sb_scales[:, 0, ob:ob + 1]), "SC")
                        ctr.inc(e.mul(scr[:, 2 + p, tq * 512:(tq + 1) * 512],
                                      pbank[4 * p + 2 + tq][:, 0:512],
                                      sb_scales[:, 1, ob:ob + 1]), "SC")

            gp0 = gp_base

            @WV
            def _(e):
                for ob in range(HB):
                    p = ob % 2
                    for tq in range(2):
                        e.wait_ge(ctr.h["GP"], gp0 + 2 * ob + tq + 1)
                        ctr.inc(e.scalar_tensor_tensor(Hb[:, ob, tq * 512:(tq + 1) * 512],
                                                       pbank[4 * p + tq][:, 0:512],
                                                       sb_scales[:, 0, ob:ob + 1],
                                                       scr[:, p, tq * 512:(tq + 1) * 512],
                                                       AL.mult, AL.mult), "VE")

            if _SUB == 'w12':
                free_to(markp)
                break
            hstages = [(h1t, 3, Hb, Gb), (h2t, 4, Gb, Hb)]
            if _SUB == 'h1':
                hstages = hstages[:1]
            for (wt, sidx, src_b, dst_b) in hstages:
                lwp1 = [ctr.val("LWa"), ctr.val("LWb")]
                mm1 = ctr.val("MM")
                sc1 = ctr.val("SC")
                pre_ve = ctr.val("VE")
                pre_sc = ctr.val("SC")
                pre_st = ctr.val("ST")

                @WG
                def _(e, wt=wt, mm1=mm1, pre_ve=pre_ve):
                    e.wait_ge(ctr.h["VE"], pre_ve)
                    e.wait_ge(ctr.h["MM"], mm1)
                    for ob in range(HB):
                        p = ob % 2
                        sem = "LWa" if p == 0 else "LWb"
                        if ob >= 2:
                            e.wait_ge(ctr.h["MM"], mm1 + (ob - 1) * 64)
                        ctr.inc(e.dma_start(
                            out=wb[:, p, :, :],
                            in_=wt[:, ob * 128:(ob + 1) * 128].rearrange(
                                "(k p) n -> p k n", p=128)), sem, 16)

                @blk.tensor
                def _(e, lwp1=lwp1, sc1=sc1, src_b=src_b, pre_ve=pre_ve):
                    e.wait_ge(ctr.h["VE"], pre_ve)
                    e.wait_ge(ctr.h["SC"], sc1)
                    for ob in range(HB):
                        p = ob % 2
                        sem = "LWa" if p == 0 else "LWb"
                        e.wait_ge(ctr.h[sem], lwp1[p] + 16 * (ob // 2 + 1))
                        if ob >= 2:
                            e.wait_ge(ctr.h["SC"], sc1 + 2 * (ob - 1))
                        for kk in range(HB):
                            for tq in range(2):
                                bank = pbank[4 * p + tq]
                                mi = e.matmul(bank[:, 0:512], wb[:, p, kk, :],
                                              src_b[:, kk, tq * 512:(tq + 1) * 512],
                                              start=(kk == 0), stop=(kk == HB - 1))
                        ctr.inc(mi, "MM", 64)

                @WS
                def _(e, mm1=mm1, sidx=sidx, dst_b=dst_b, pre_st=pre_st):
                    e.wait_ge(ctr.h["ST"], pre_st)
                    for ob in range(HB):
                        p = ob % 2
                        e.wait_ge(ctr.h["MM"], mm1 + 64 * (ob + 1))
                        for tq in range(2):
                            ctr.inc(e.activation(dst_b[:, ob, tq * 512:(tq + 1) * 512],
                                                 pbank[4 * p + tq][:, 0:512], AF.Gelu,
                                                 scale=sb_scales[:, sidx, ob:ob + 1]), "SC")

            if _SUB in ('h1', 'h2'):
                free_to(markp)
                break
            free_to(markp)

            p_w = pal("pw" + uid + str(hf), (128, 1024))
            lwp3 = [ctr.val("LWa"), ctr.val("LWb")]
            mm3 = ctr.val("MM")
            ve3 = ctr.val("VE")
            ld3 = ctr.val("LD")
            sc_h2 = ctr.val("SC")

            x1tiles = [GbF[:, 2 * kb:2 * kb + 2, :].rearrange("p a b -> p (a b)")
                       for kb in range(KB)]

            @WG
            def _(e):
                e.wait_ge(ctr.h["SC"], sc_h2)
                for ob in range(KB):
                    p = ob % 2
                    sem = "LWa" if p == 0 else "LWb"
                    if ob >= 2:
                        e.wait_ge(ctr.h["MM"], mm3 + (ob - 1) * 64)
                    ctr.inc(e.dma_start(
                        out=wb[:, p, :, :],
                        in_=w3t[:, ob * 128:(ob + 1) * 128].rearrange("(k p) n -> p k n", p=128)),
                        sem, 16)

            @blk.sync
            def _(e):
                ctr.wait(e, "LD")
                for ob in range(KB):
                    if ob >= 1:
                        e.wait_ge(ctr.h["VE"], ve3 + 2 * ob)
                    for tq in range(2):
                        ctr.inc(e.dma_start(out=xres[:, tq, :],
                                            in_=src_ap[ob * 128:(ob + 1) * 128,
                                                       t0 + tq * 512:t0 + (tq + 1) * 512]),
                                "LD", 16)

            @blk.tensor
            def _(e):
                e.wait_ge(ctr.h["SC"], sc_h2)
                for ob in range(KB):
                    p = ob % 2
                    sem = "LWa" if p == 0 else "LWb"
                    e.wait_ge(ctr.h[sem], lwp3[p] + 16 * (ob // 2 + 1))
                    if ob >= 1:
                        e.wait_ge(ctr.h["VE"], ve3 + 2 * ob)
                    for kk in range(HB):
                        for tq in range(2):
                            bank = p_w[:, tq * 512:(tq + 1) * 512]
                            mi = e.matmul(bank, wb[:, p, kk, :],
                                          Hb[:, kk, tq * 512:(tq + 1) * 512],
                                          start=(kk == 0), stop=(kk == HB - 1))
                    ctr.inc(mi, "MM", 64)

            @WV
            def _(e):
                for ob in range(KB):
                    e.wait_ge(ctr.h["MM"], mm3 + 64 * (ob + 1))
                    e.wait_ge(ctr.h["LD"], ld3 + 32 * (ob + 1))
                    for tq in range(2):
                        ctr.inc(e.scalar_tensor_tensor(
                            x1tiles[ob][:, tq * 512:(tq + 1) * 512],
                            p_w[:, tq * 512:(tq + 1) * 512],
                            sb_scales[:, 5, ob:ob + 1],
                            xres[:, tq, :], AL.mult, AL.add), "VE")
            ve_w3 = ctr.val("VE")

            dst_ap = yT if final_store else spill_ap

            @blk.sync
            def _(e):
                ctr.wait(e, "ST")
                for ob in range(KB):
                    e.wait_ge(ctr.h["VE"], ve3 + 2 * (ob + 1))
                    ctr.inc(e.dma_start(
                        out=dst_ap[ob * 128:(ob + 1) * 128, t0:t0 + HALF],
                        in_=x1tiles[ob]), "ST", 16)

            if make_ln_out:
                p_stat = pal("pstB" + uid + str(hf), (128, 1024))
                p_sq = pal("psqB" + uid + str(hf), (128, 1024))
                p_bc2 = pal("pbcB" + uid + str(hf), (128, 1024))
                ln_stage(x1tiles, HALF, [xln[:, kb, :] for kb in range(KB)],
                         scr, p_stat, p_sq, p_bc2)

                @blk.sync
                def _(e):
                    ctr.wait(e, "VE")
                    ctr.wait(e, "ST")
                    for kb in range(KB):
                        ctr.inc(e.dma_start(
                            out=ag_in.ap()[kb * 128:(kb + 1) * 128, t0:t0 + HALF],
                            in_=xln[:, kb, :]), "ST", 16)
            free_to(markp)

        free_to(mark0)

    def attn():
        mark0 = len(_guards)
        wq_s = alloc("wq_s", (128, KB, 256), BF16)
        wk_s = alloc("wk_s", (128, KB, 256), BF16)
        wv_s = alloc("wv_s", (128, KB, 256), BF16)
        wfg_s = alloc("wfg_s", (128, KB, 68), BF16)
        wf2_s = alloc("wf2_s", (64, 2, 128), BF16)
        xgt = alloc("xgt", (128, KB, SUPER), BF16)
        qsil = alloc("qsil", (128, 2, SUPER))
        ksil = alloc("ksil", (128, 2, SUPER))
        khat = alloc("khat", (128, 2, SUPER))
        vbf = alloc("vbf", (128, 2, SUPER), BF16)
        sgk = alloc("sgk", (128, 2, SUPER))
        gkb = alloc("gkb", (128, 2, SUPER))
        rbb = alloc("rbb", (128, 2, SUPER))
        rgbb = alloc("rgbb", (128, 2, SUPER))
        zf_bf = alloc("zf_bf", (64, SUPER), BF16)
        arows = alloc("arows", (4, 4, SUPER))
        ksq = alloc("ksq", (128, 2, SUPER))
        Gw = alloc("Gw", (128, 2, SUPER))
        argE = alloc("argE", (128, 2, SUPER))
        Gsc = alloc("Gsc", (128, CPS, 2, 6))
        egc = alloc("egc", (128, CPS, 2, 1))
        E1 = alloc("E1", (128, 2, SUPER)); E2 = alloc("E2", (128, 2, SUPER))
        E3 = alloc("E3", (128, 2, SUPER)); E4 = alloc("E4", (128, 2, SUPER))
        tE = alloc("tE", (128, 2, SUPER)); kg = alloc("kg", (128, 2, SUPER))
        Qmid = alloc("Qmid", (128, 2, SUPER), BF16); Qful = alloc("Qful", (128, 2, SUPER), BF16)
        Kdot = alloc("Kdot", (128, 2, SUPER), BF16); Khb = alloc("Khb", (128, 2, SUPER), BF16)
        Adot = alloc("Adot", (128, 2, SUPER), BF16); Akm = alloc("Akm", (128, 2, SUPER), BF16)
        Xb = alloc("Xb", (128, 4, CPS, 128), BF16)
        PbA = alloc("PbA", (128, 4, CPS, 128), BF16); PtA = alloc("PtA", (128, 4, CPS, 128), BF16)
        PbB = alloc("PbB", (128, 4, CPS, 128), BF16); PtB = alloc("PtB", (128, 4, CPS, 128), BF16)
        POb = alloc("POb", (128, 4, CPS, 128), BF16)
        KTb = alloc("KTb", (128, 4, CPS, HD), BF16)
        QHb = alloc("QHb", (128, CPS, 4, C), BF16)
        WPb = alloc("WPb", (128, CPS, 4, HD), BF16)
        OLb = alloc("OLb", (128, CPS, 4, HD))
        SLb = alloc("SLb", (128, CPS, 2, HD))
        Sst = alloc("Sst", (128, 2, HD))
        Sbf = alloc("Sbf", (128, 2, HD), BF16)
        Otm = alloc("Otm", (128, 4, HD))
        Ovm = alloc("Ovm", (64, 4, SUPER))
        Ovm_lo = alloc("Ovm_lo", (64, 4, SUPER))
        Ovm_hi = alloc("Ovm_hi", (64, 4, SUPER))

        pmh = pal("pmh", (128, 2048))
        pful = pal("pful", (128, 2048))
        pfB = [pful[:, i * 512:(i + 1) * 512] for i in range(4)]
        pmB = [pmh[:, i * 512:(i + 1) * 512] for i in range(4)]

        def pmv(lo, hi):
            return pmh[:].rearrange("p (h x) -> p h x", x=512)[:, :, lo:hi]

        def pfv(lo, hi, x):
            return pful[:, lo:hi].rearrange("p (h x) -> p h x", x=x)

        @WG
        def _(e):
            ctr.wait(e, "CC")
            for dst, src in [(wq_s, wq_d), (wk_s, wk_d), (wv_s, wv_d), (wfg_s, wfg_d)]:
                ctr.inc(e.dma_start(out=dst[:],
                                    in_=src.rearrange("(k p) n -> p k n", p=128)), "LW", 16)
            ctr.inc(e.dma_start(out=wf2_s[:],
                                in_=wf2_d.rearrange("k (m n) -> k m n", n=128)), "LW", 16)
        lw_w = ctr.val("LW")

        @WV
        def _(e):
            e.memset(QHb[:], 0.0)
            e.memset(WPb[:], 0.0)
            e.memset(Sst[:], 0.0)
            ctr.inc(e.memset(Sbf[:], 0.0), "VE")

        for sc_i in range(NSUPER):
            ts = sc_i * SUPER
            rk = ts // SLAB
            tl = ts % SLAB

            pre_all = {nm: ctr.val(nm) for nm in ["MM", "VE", "SC", "GP", "S2"]}

            @blk.sync
            def _(e, rk=rk, tl=tl, pre=dict(pre_all)):
                ctr.wait(e, "LD")
                ctr.wait(e, "CC")
                for nm in ["VE", "SC", "MM", "GP"]:
                    e.wait_ge(ctr.h[nm], pre[nm])
                for kb in range(KB):
                    ctr.inc(e.dma_start(out=xgt[:, kb, :],
                                        in_=xg.ap()[rk, kb * 128:(kb + 1) * 128, tl:tl + SUPER]),
                            "LD", 16)
            ld_x = ctr.val("LD")

            def proj_mm(w_s, pb, bank, extra_sc=None):
                @blk.tensor
                def _(e, w_s=w_s, pb=pb, bank=bank, extra_sc=extra_sc):
                    e.wait_ge(ctr.h["LD"], ld_x)
                    e.wait_ge(ctr.h["LW"], lw_w)
                    if extra_sc is not None:
                        e.wait_ge(ctr.h["SC"], extra_sc)
                    for kb in range(KB):
                        mi = e.matmul(bank[:, 0:512], w_s[:, kb, pb * 128:(pb + 1) * 128],
                                      xgt[:, kb, :], start=(kb == 0), stop=(kb == KB - 1))
                    ctr.inc(mi, "MM")
                return ctr.val("MM")

            def proj_ev(kind, dst, bank, mm_at, scale=None):
                if kind == "copy":
                    @WV
                    def _(e, dst=dst, bank=bank, mm_at=mm_at):
                        e.wait_ge(ctr.h["MM"], mm_at)
                        ctr.inc(e.tensor_copy(dst, bank), "VE")
                    return ctr.val("VE")
                @WS
                def _(e, kind=kind, dst=dst, bank=bank, mm_at=mm_at):
                    e.wait_ge(ctr.h["MM"], mm_at)
                    if kind == "silu":
                        ctr.inc(e.activation(dst, bank, AF.Silu), "SC")
                    elif kind == "sig":
                        ctr.inc(e.activation(dst, bank, AF.Sigmoid), "SC")
                return ctr.val("SC")

            m = proj_mm(wq_s, 0, pfB[0]); proj_ev("silu", qsil[:, 0, :], pfB[0][:, 0:512], m)
            m = proj_mm(wq_s, 1, pfB[1]); proj_ev("silu", qsil[:, 1, :], pfB[1][:, 0:512], m)
            m = proj_mm(wk_s, 0, pfB[2]); proj_ev("silu", ksil[:, 0, :], pfB[2][:, 0:512], m)
            m = proj_mm(wk_s, 1, pfB[3]); sc_k1 = proj_ev("silu", ksil[:, 1, :], pfB[3][:, 0:512], m)
            sc_q = ctr.val("SC") - 2
            m = proj_mm(wv_s, 0, pfB[0], extra_sc=sc_q - 1)
            proj_ev("copy", vbf[:, 0, :], pfB[0][:, 0:512], m)
            m = proj_mm(wv_s, 1, pfB[1], extra_sc=sc_q)
            proj_ev("copy", vbf[:, 1, :], pfB[1][:, 0:512], m)

            @blk.tensor
            def _(e, pre_sc=ctr.val("SC") - 1):
                e.wait_ge(ctr.h["SC"], pre_sc)
                for kb in range(KB):
                    mi = e.matmul(pfB[2][0:68, 0:512], wfg_s[:, kb, :], xgt[:, kb, :],
                                  start=(kb == 0), stop=(kb == KB - 1))
                ctr.inc(mi, "MM")
            mm_zfg = ctr.val("MM")

            @WV
            def _(e, mm_zfg=mm_zfg):
                e.wait_ge(ctr.h["MM"], mm_zfg)
                ctr.inc(e.tensor_copy(zf_bf[:, :], pfB[2][0:64, 0:512]), "VE")
            ve_zf = ctr.val("VE")
            @WS
            def _(e, mm_zfg=mm_zfg):
                e.wait_ge(ctr.h["MM"], mm_zfg)
                ctr.inc(e.activation(arows[0:4, 3, :], pfB[2][64:68, 0:512], AF.Sigmoid), "SC")
            sc_zf = ctr.val("SC")

            @blk.tensor
            def _(e, sc_zf=sc_zf, ve_zf=ve_zf):
                e.wait_ge(ctr.h["SC"], sc_zf)
                e.wait_ge(ctr.h["VE"], ve_zf)
                ctr.inc(e.matmul(pfB[3][:, 0:512], wf2_s[:, 0, :], zf_bf[:, :],
                                 start=True, stop=True), "MM")
                ctr.inc(e.matmul(pfB[0][:, 0:512], wf2_s[:, 1, :], zf_bf[:, :],
                                 start=True, stop=True), "MM")
            mm_zf2 = ctr.val("MM")

            @WS
            def _(e, mm_zf2=mm_zf2):
                e.wait_ge(ctr.h["MM"], mm_zf2 - 1)
                ctr.inc(e.activation(sgk[:, 0, :], pfB[3][:, 0:512], AF.Sigmoid), "SC")
                e.wait_ge(ctr.h["MM"], mm_zf2)
                ctr.inc(e.activation(sgk[:, 1, :], pfB[0][:, 0:512], AF.Sigmoid), "SC")
                for pb in range(2):
                    ctr.inc(e.activation(gkb[:, pb, :], sgk[:, pb, :], AF.Ln), "SC")
            sc_gk = ctr.val("SC")

            @WV
            def _(e, sc_k1=sc_k1):
                e.wait_ge(ctr.h["SC"], sc_k1)
                for pb in range(2):
                    ctr.inc(e.tensor_mul(ksq[:, pb, :], ksil[:, pb, :], ksil[:, pb, :]), "VE")
            ve_ksq = ctr.val("VE")

            @blk.tensor
            def _(e, ve_ksq=ve_ksq, sc_gk=sc_gk):
                e.wait_ge(ctr.h["VE"], ve_ksq)
                e.wait_ge(ctr.h["SC"], sc_gk)
                for pb in range(2):
                    mi = e.matmul(pfB[1][0:4, 0:512], sb_sel4[:, pb, :], ksq[:, pb, :],
                                  start=(pb == 0), stop=(pb == 1))
                ctr.inc(mi, "MM")
            mm_n2 = ctr.val("MM")

            @WV
            def _(e, mm_n2=mm_n2):
                e.wait_ge(ctr.h["MM"], mm_n2)
                ctr.inc(e.tensor_scalar_max(arows[0:4, 0, :], pfB[1][0:4, 0:512], 1e-24), "VE")
            ve_n2 = ctr.val("VE")

            @WS
            def _(e, ve_n2=ve_n2):
                e.wait_ge(ctr.h["VE"], ve_n2)
                ctr.inc(e.activation(arows[0:4, 1, :], arows[0:4, 0, :], AF.Sqrt), "SC")
            sc_n2 = ctr.val("SC")

            @WV
            def _(e, sc_n2=sc_n2):
                e.wait_ge(ctr.h["SC"], sc_n2)
                e.reciprocal(arows[0:4, 1, :], arows[0:4, 1, :])
                e.tensor_mul(arows[0:4, 2, :], arows[0:4, 1, :], arows[0:4, 1, :])
                e.tensor_mul(arows[0:4, 2, :], arows[0:4, 0, :], arows[0:4, 2, :])
                e.tensor_scalar(arows[0:4, 2, :], arows[0:4, 2, :], -0.5, 1.5, AL.mult, AL.add)
                e.tensor_mul(arows[0:4, 1, :], arows[0:4, 1, :], arows[0:4, 2, :])
                e.tensor_mul(arows[0:4, 2, :], arows[0:4, 1, :], arows[0:4, 3, :])
                ctr.inc(e.tensor_scalar_mul(arows[0:4, 2, :], arows[0:4, 2, :], -1.0), "VE")
            ve_r = ctr.val("VE")

            for pb in range(2):
                @blk.tensor
                def _(e, pb=pb, ve_r=ve_r, pre_ve=ctr.val("VE")):
                    e.wait_ge(ctr.h["VE"], max(ve_r, pre_ve))
                    e.matmul(pfB[2][:, 0:512], sb_bc4[:, pb, :], arows[0:4, 1, :],
                             start=True, stop=True)
                    ctr.inc(e.matmul(pfB[3][:, 0:512], sb_bc4[:, pb, :], arows[0:4, 2, :],
                                     start=True, stop=True), "MM")
                mm_b = ctr.val("MM")

                @WV
                def _(e, pb=pb, mm_b=mm_b):
                    e.wait_ge(ctr.h["MM"], mm_b)
                    e.tensor_copy(rbb[:, pb, :], pfB[2][:, 0:512])
                    ctr.inc(e.tensor_copy(rgbb[:, pb, :], pfB[3][:, 0:512]), "VE")

            @WV
            def _(e):
                for pb in range(2):
                    ctr.inc(e.tensor_mul(khat[:, pb, :], ksil[:, pb, :], rbb[:, pb, :]), "VE")
            ve_kh = ctr.val("VE")

            @WV
            def _(e, sc_gk=sc_gk, ve_kh=ve_kh, pre_sc=ctr.val("SC"),
                  pre_mm=ctr.val("MM")):
                e.wait_ge(ctr.h["SC"], max(sc_gk, pre_sc))
                e.wait_ge(ctr.h["VE"], ve_kh)
                e.wait_ge(ctr.h["MM"], pre_mm)
                for cc in range(CPS):
                    ch = slice(cc * C, (cc + 1) * C)
                    for pb in range(2):
                        e.tensor_tensor_scan(Gw[:, pb, ch], gkb[:, pb, ch], gkb[:, pb, ch],
                                             0.0, AL.add, AL.bypass)
                for cc in range(CPS):
                    for pb in range(2):
                        gm = Gw[:, pb, cc * C + 63:cc * C + 64]
                        gc_ = Gw[:, pb, cc * C + 127:cc * C + 128]
                        e.tensor_scalar(Gsc[:, cc, pb, 0:1], gm, -1.0, -SHIFT,
                                        AL.mult, AL.add)
                        e.tensor_copy(Gsc[:, cc, pb, 1:2], gc_)
                        e.tensor_add(Gsc[:, cc, pb, 3:4], gc_, Gsc[:, cc, pb, 0:1])
                        e.tensor_scalar_add(argE[:, pb, cc * C:(cc + 1) * C],
                                            Gw[:, pb, cc * C:(cc + 1) * C],
                                            Gsc[:, cc, pb, 0:1])
                        e.tensor_scalar(Gsc[:, cc, pb, 5:6], gm, 1.0, -SHIFT,
                                        AL.mult, AL.add)
                        e.tensor_scalar(E2[:, pb, cc * C:(cc + 1) * C],
                                        Gw[:, pb, cc * C:(cc + 1) * C],
                                        -1.0, Gsc[:, cc, pb, 5:6],
                                        AL.mult, AL.add)
                        e.tensor_scalar(E4[:, pb, cc * C:(cc + 1) * C],
                                        Gw[:, pb, cc * C:(cc + 1) * C],
                                        -1.0, gc_,
                                        AL.mult, AL.add)
                ctr.inc(e.engine_nop() if False else e.tensor_copy(
                    Gsc[:, 0, 0, 5:6], Gsc[:, 0, 0, 0:1]), "VE")
            ve_g = ctr.val("VE")

            @WS
            def _(e, ve_g=ve_g):
                e.wait_ge(ctr.h["VE"], ve_g)
                for pb in range(2):
                    e.activation(E1[:, pb, :], argE[:, pb, :], AF.Exp)
                    e.activation(E3[:, pb, :], Gw[:, pb, :], AF.Exp)
                    e.activation(E2[:, pb, :], E2[:, pb, :], AF.Exp)
                    e.activation(E4[:, pb, :], E4[:, pb, :], AF.Exp)
                for cc in range(CPS):
                    for pb in range(2):
                        e.activation(Gsc[:, cc, pb, 2:3], Gsc[:, cc, pb, 0:1],
                                     AF.Exp, scale=-1.0)
                        mi = e.activation(Gsc[:, cc, pb, 4:5], Gsc[:, cc, pb, 3:4], AF.Exp)
                ctr.inc(mi, "SC")
            sc_e = ctr.val("SC")

            @WV
            def _(e, sc_e=sc_e, pre_mm=ctr.val("MM")):
                e.wait_ge(ctr.h["SC"], sc_e)
                e.wait_ge(ctr.h["MM"], pre_mm)
                for cc in range(CPS):
                    for pb in range(2):
                        e.tensor_mul(egc[:, cc, pb, :], Gsc[:, cc, pb, 2:3],
                                     Gsc[:, cc, pb, 4:5])
                e.tensor_mul(Qmid[:], qsil[:], E1[:])
                e.tensor_mul(Qful[:], qsil[:], E3[:])
                e.tensor_mul(Kdot[:], khat[:], E2[:])
                e.tensor_mul(Khb[:], khat[:], E4[:])
                e.tensor_mul(tE[:], sgk[:], E1[:])
                e.tensor_mul(kg[:], ksil[:], rgbb[:])
                e.tensor_mul(Adot[:], kg[:], tE[:])
                for cc in range(CPS):
                    for pb in range(2):
                        mi = e.tensor_scalar_mul(Akm[:, pb, cc * C:(cc + 1) * C],
                                                 Adot[:, pb, cc * C:(cc + 1) * C],
                                                 Gsc[:, cc, pb, 2:3])
                ctr.inc(mi, "VE")
            ve_p = ctr.val("VE")

            @blk.tensor
            def _(e, ve_p=ve_p, pre_sc=ctr.val("SC")):
                e.wait_ge(ctr.h["VE"], ve_p)
                e.wait_ge(ctr.h["SC"], pre_sc)
                for cc in range(CPS):
                    ch = slice(cc * C, (cc + 1) * C)
                    for hh in range(4):
                        pb, hp = hh // 2, hh % 2
                        rg = 64 * hp
                        tp = (rg, 0) if rg else None
                        e.matmul(pmB[hh][:, 128 * cc:128 * cc + 64],
                                 vbf[rg:rg + 64, pb, ch],
                                 sb_idpair[rg:rg + 64, :], start=True, stop=False,
                                 tile_position=tp)
                        mi = e.matmul(pmB[hh][:, 128 * cc + 64:128 * cc + 128],
                                      Akm[rg:rg + 64, pb, ch],
                                      sb_idpair[rg:rg + 64, :], start=False, stop=True,
                                      tile_position=tp)
                ctr.inc(mi, "MM")
            mm_B = ctr.val("MM")

            @WV
            def _(e, mm_B=mm_B):
                e.wait_ge(ctr.h["MM"], mm_B)
                for hh in range(4):
                    vi = e.tensor_copy(Xb[:, hh, :, :], pmB[hh][:, 0:512])
                ctr.inc(vi, "SC")
            sc_B = ctr.val("SC")

            @blk.tensor
            def _(e, sc_B=sc_B):
                e.wait_ge(ctr.h["SC"], sc_B)
                for cc in range(CPS):
                    ch = slice(cc * C, (cc + 1) * C)
                    for hh in range(4):
                        pb, hp = hh // 2, hh % 2
                        rg = 64 * hp
                        tp = (rg, 0) if rg else None
                        ctr.inc(e.matmul(pmB[hh][:, 128 * cc:128 * (cc + 1)],
                                         Adot[rg:rg + 64, pb, ch],
                                         Kdot[rg:rg + 64, pb, ch], start=True, stop=True,
                                         tile_position=tp), "MM")
            mm_M = ctr.val("MM")

            @WV
            def _(e, mm_M=mm_M):
                e.wait_ge(ctr.h["MM"], mm_M)
                for hh in range(4):
                    e.select(PbA[:, hh, :, :], sb_maskl[:], pmB[hh][:, 0:512],
                             sb_zeros[:], add_drain=True)
                ctr.inc(e.tensor_scalar_mul(PbA[:], PbA[:], UNSCALE), "VE")
            ve_N = ctr.val("VE")

            @blk.tensor
            def _(e, ve_N=ve_N):
                e.wait_ge(ctr.h["VE"], ve_N)
                for hh in range(4):
                    for cc in range(CPS):
                        mi = e.matmul(pful[:, 512 * hh + 128 * cc:512 * hh + 128 * cc + 128],
                                      PbA[:, hh, cc, :], sb_idbf[:], start=True, stop=True)
                ctr.inc(mi, "MM")
            mm_NT = ctr.val("MM")

            @WS
            def _(e, mm_NT=mm_NT):
                e.wait_ge(ctr.h["MM"], mm_NT)
                ctr.inc(e.copy(PtA[:], pful[:, 0:2048]), "SC")
            sc_NT = ctr.val("SC")

            curP, curPt, nxtP, nxtPt = PbA, PtA, PbB, PtB
            prev_sc = sc_NT
            for lvl in range(1, 7):
                @blk.tensor
                def _(e, curP=curP, curPt=curPt, prev_sc=prev_sc,
                      pre_ve=ctr.val("VE"), lvl=lvl):
                    e.wait_ge(ctr.h["SC"], prev_sc)
                    e.wait_ge(ctr.h["VE"], pre_ve)
                    for hh in range(4):
                        for cc in range(CPS):
                            sl = pful[:, 512 * hh + 128 * cc:512 * hh + 128 * cc + 128]
                            e.matmul(sl, sb_idbf[:], Xb[:, hh, cc, :],
                                     start=True, stop=False)
                            mi = e.matmul(sl, curPt[:, hh, cc, :], Xb[:, hh, cc, :],
                                          start=False, stop=True)
                    ctr.inc(mi, "MM")
                    if lvl < 6:
                        for hh in range(4):
                            for cc in range(CPS):
                                mi = e.matmul(
                                    pmB[hh][:, 128 * cc:128 * (cc + 1)],
                                    curP[:, hh, cc, :], curPt[:, hh, cc, :],
                                    start=True, stop=True)
                        ctr.inc(mi, "MM")
                mm_ap = ctr.val("MM")

                @WV
                def _(e, mm_ap=mm_ap, lvl=lvl):
                    e.wait_ge(ctr.h["MM"], mm_ap)
                    ctr.inc(e.tensor_copy(Xb[:], pful[:, 0:2048]), "VE")
                ve_ap = ctr.val("VE")

                if lvl < 6:
                    @WS
                    def _(e, nxtPt=nxtPt, mm_ap=mm_ap):
                        e.wait_ge(ctr.h["MM"], mm_ap)
                        pv = pmh[:].rearrange("p (h x) -> p h x", x=512)
                        ctr.inc(e.copy(nxtPt[:], pv), "SC")
                    sc_pt = ctr.val("SC")

                    if lvl < 5:
                        @blk.tensor
                        def _(e, curPt=curPt, curP=curP, ve_ap=ve_ap):
                            e.wait_ge(ctr.h["VE"], ve_ap)
                            for hh in range(4):
                                for cc in range(CPS):
                                    mi = e.matmul(
                                        pful[:, 512 * hh + 128 * cc:512 * hh + 128 * cc + 128],
                                        curPt[:, hh, cc, :], curP[:, hh, cc, :],
                                        start=True, stop=True)
                            ctr.inc(mi, "MM")
                        mm_bp = ctr.val("MM")

                        @WS
                        def _(e, nxtP=nxtP, mm_bp=mm_bp):
                            e.wait_ge(ctr.h["MM"], mm_bp)
                            ctr.inc(e.copy(nxtP[:], pful[:, 0:2048]), "SC")
                    prev_sc = ctr.val("SC")
                    curP, curPt, nxtP, nxtPt = nxtP, nxtPt, curP, curPt

            @blk.tensor
            def _(e, pre_ve=ctr.val("VE"), pre_sc=ctr.val("SC")):
                e.wait_ge(ctr.h["VE"], pre_ve)
                e.wait_ge(ctr.h["SC"], pre_sc)
                for cc in range(CPS):
                    ch = slice(cc * C, (cc + 1) * C)
                    for hh in range(4):
                        pb, hp = hh // 2, hh % 2
                        rg = 64 * hp
                        tp = (rg, 0) if rg else None
                        ctr.inc(e.matmul(pmB[hh][:, 128 * cc:128 * (cc + 1)],
                                         Kdot[rg:rg + 64, pb, ch],
                                         Qmid[rg:rg + 64, pb, ch], start=True, stop=True,
                                         tile_position=tp), "MM")
            mm_QK = ctr.val("MM")

            @WV
            def _(e, mm_QK=mm_QK):
                e.wait_ge(ctr.h["MM"], mm_QK)
                for hh in range(4):
                    e.select(POb[:, hh, :, :], sb_masku[:], pmB[hh][:, 0:512],
                             sb_zeros[:], add_drain=True)
                ctr.inc(e.tensor_scalar_mul(POb[:], POb[:], UNSCALE), "VE")
            ve_PO = ctr.val("VE")

            @blk.tensor
            def _(e, ve_PO=ve_PO):
                e.wait_ge(ctr.h["VE"], ve_PO)
                for cc in range(CPS):
                    ch = slice(cc * C, (cc + 1) * C)
                    for hh in range(4):
                        pb, hp = hh // 2, hh % 2
                        rg = 64 * hp
                        tp = (rg, 0) if rg else None
                        ctr.inc(e.matmul(pmB[hh][:, 64 * cc:64 * cc + 64],
                                         Khb[rg:rg + 64, pb, ch],
                                         sb_idpair[rg:rg + 64, :], start=True, stop=True,
                                         tile_position=tp), "MM")
            mm_KT = ctr.val("MM")

            @WV
            def _(e, mm_KT=mm_KT):
                e.wait_ge(ctr.h["MM"], mm_KT)
                for hh in range(4):
                    vi = e.tensor_copy(KTb[:, hh, :, :], pmB[hh][:, 0:256])
                ctr.inc(vi, "SC")
            sc_KT = ctr.val("SC")

            @blk.tensor
            def _(e, sc_KT=sc_KT, pre_ve=ctr.val("VE")):
                e.wait_ge(ctr.h["SC"], sc_KT)
                e.wait_ge(ctr.h["VE"], pre_ve)
                for cc in range(CPS):
                    for hh in range(4):
                        mi = e.matmul(pful[:, 256 * cc + 64 * hh:256 * cc + 64 * hh + 64],
                                      POb[:, hh, cc, :], Xb[:, hh, cc, 0:64],
                                      start=True, stop=True)
                ctr.inc(mi, "MM")
                for cc in range(CPS):
                    for hh in range(4):
                        q, hp = hh // 2, hh % 2
                        mi = e.matmul(
                            pful[64 * hp:64 * hp + 64,
                                 1024 + 256 * cc + 128 * q:1024 + 256 * cc + 128 * q + 128],
                            Xb[:, hh, cc, 64:128], POb[:, hh, cc, :],
                            start=True, stop=True, tile_position=(0, 64 * hp))
                ctr.inc(mi, "MM")
                for cc in range(CPS):
                    for hh in range(4):
                        q, hp = hh // 2, hh % 2
                        mi = e.matmul(
                            pmB[0][64 * hp:64 * hp + 64, 128 * cc + 64 * q:128 * cc + 64 * q + 64],
                            Xb[:, hh, cc, 64:128], KTb[:, hh, cc, :],
                            start=True, stop=True, tile_position=(0, 64 * hp))
                ctr.inc(mi, "MM")
                for cc in range(CPS):
                    for hh in range(4):
                        q, hp = hh // 2, hh % 2
                        mi = e.matmul(
                            pmB[1][64 * hp:64 * hp + 64, 128 * cc + 64 * q:128 * cc + 64 * q + 64],
                            KTb[:, hh, cc, :], Xb[:, hh, cc, 0:64],
                            start=True, stop=True, tile_position=(0, 64 * hp))
                ctr.inc(mi, "MM")
            mm_post = ctr.val("MM")

            @WV
            def _(e, mm_post=mm_post):
                e.wait_ge(ctr.h["MM"], mm_post)
                e.tensor_copy(OLb[:], pful[:, 0:1024])
                e.tensor_copy(SLb[:], pmB[1][:, 0:512])
                for cc in range(CPS):
                    ch = slice(cc * C, (cc + 1) * C)
                    for hh in range(4):
                        q, hp = hh // 2, hh % 2
                        e.tensor_tensor(
                            QHb[64 * hp:64 * hp + 64, cc, hh, :],
                            Qful[64 * hp:64 * hp + 64, q, ch],
                            pful[64 * hp:64 * hp + 64,
                                 1024 + 256 * cc + 128 * q:1024 + 256 * cc + 128 * q + 128],
                            AL.add)
                for cc in range(CPS):
                    for hh in range(4):
                        q, hp = hh // 2, hh % 2
                        vi = e.scalar_tensor_tensor(
                            WPb[64 * hp:64 * hp + 64, cc, hh, :],
                            sb_id2pair[64 * hp:64 * hp + 64, 0, :],
                            egc[64 * hp:64 * hp + 64, cc, q, :],
                            pmB[0][64 * hp:64 * hp + 64,
                                   128 * cc + 64 * q:128 * cc + 64 * q + 64],
                            AL.mult, AL.add)
                ctr.inc(vi, "VE")
            ve_chunk = ctr.val("VE")

            for cc in range(CPS):
                @blk.tensor
                def _(e, cc=cc, pre_ve=ctr.val("VE"), pre_sc=ctr.val("SC")):
                    e.wait_ge(ctr.h["VE"], pre_ve)
                    e.wait_ge(ctr.h["SC"], pre_sc)
                    if cc > 0:
                        for hh in range(4):
                            mi = e.matmul(pfB[0][0:64, 128 * hh:128 * (hh + 1)],
                                          Otm[:, hh, :], sb_idf32[:],
                                          start=True, stop=True)
                    for hh in range(4):
                        mi = e.matmul(pfB[3][:, 64 * hh:64 * (hh + 1)], QHb[:, cc, hh, :],
                                      Sbf[:, hh // 2, :], start=True, stop=True)
                    for hh in range(4):
                        q, hp = hh // 2, hh % 2
                        mi = e.matmul(pfB[3][64 * hp:64 * hp + 64,
                                             256 + 64 * q:256 + 64 * q + 64],
                                      WPb[:, cc, hh, :], Sbf[:, q, :],
                                      start=True, stop=True, tile_position=(0, 64 * hp))
                    ctr.inc(mi, "MM")
                mm_sw = ctr.val("MM")

                @WV
                def _(e, cc=cc, mm_sw=mm_sw):
                    e.wait_ge(ctr.h["MM"], mm_sw)
                    if cc > 0:
                        e.tensor_copy(Ovm[:, :, (cc - 1) * C:cc * C],
                                      pful[0:64, 0:512].rearrange("p (h x) -> p h x", x=128))
                    e.tensor_tensor(Otm[:], pfv(1536, 1792, 64), OLb[:, cc, :, :], AL.add)
                    e.tensor_tensor(Sst[:],
                                    pful[:, 1792:1920].rearrange("p (q x) -> p q x", x=64),
                                    SLb[:, cc, :, :], AL.add)
                    ctr.inc(e.tensor_copy(Sbf[:], Sst[:]), "VE")

            @blk.tensor
            def _(e, pre_ve=ctr.val("VE")):
                e.wait_ge(ctr.h["VE"], pre_ve)
                for hh in range(4):
                    mi = e.matmul(pfB[0][0:64, 128 * hh:128 * (hh + 1)],
                                  Otm[:, hh, :], sb_idf32[:], start=True, stop=True)
                ctr.inc(mi, "MM")
            mm_ot = ctr.val("MM")

            @WV
            def _(e, mm_ot=mm_ot):
                e.wait_ge(ctr.h["MM"], mm_ot)
                ctr.inc(e.tensor_copy(Ovm[:, :, (CPS - 1) * C:CPS * C],
                                      pful[0:64, 0:512].rearrange("p (h x) -> p h x", x=128)),
                        "SC")

            sc_o = ctr.val("SC")
            dsh = ts // 2048
            pre_s2m = ctr.val("S2")

            @WV
            def _(e, sc_o=sc_o, pre_s2m=pre_s2m):
                e.wait_ge(ctr.h["SC"], sc_o)
                e.wait_ge(ctr.h["S2"], pre_s2m)
                e.tensor_scalar_mul(Ovm_lo[:], Ovm[:], sb_msel[:, 0:1])
                ctr.inc(e.tensor_scalar_mul(Ovm_hi[:], Ovm[:], sb_msel[:, 1:2]), "VE")
            ve_ovm = ctr.val("VE")

            @WS
            def _(e, ve_ovm=ve_ovm, dsh=dsh, tl2=ts % 2048):
                e.wait_ge(ctr.h["VE"], ve_ovm)
                ctr.wait(e, "S2")
                for hh in range(4):
                    ctr.inc(e.dma_start(
                        out=a2a_in.ap()[dsh, 64 * hh:64 * hh + 64, tl2:tl2 + SUPER],
                        in_=Ovm_lo[:, hh, :]), "S2", 16)
                    ctr.inc(e.dma_start(
                        out=a2a_in.ap()[dsh + 4, 64 * hh:64 * hh + 64, tl2:tl2 + SUPER],
                        in_=Ovm_hi[:, hh, :]), "S2", 16)

        free_to(mark0)

    def post_attn():
        mark0 = len(_guards)
        wo_s = alloc("wo_s", (128, KB, KB, 128), BF16)
        wog1_s = alloc("wog1_s", (128, KB, 64), BF16)
        wog2_s = alloc("wog2_s", (64, KB, 128), BF16)
        o_t = alloc("o_t", (128, KB, 512))
        og_t = alloc("og_t", (128, KB, 512))
        oln = alloc("oln", (128, KB, 512), BF16)
        xlt = alloc("xlt", (128, KB, 512), BF16)
        zog1 = alloc("zog1", (64, 512), BF16)
        scr4 = alloc("scr4", (128, 4, 512))
        x1t = alloc("x1t", (128, KB, 512))
        x2t = alloc("x2t", (128, KB, 512))

        @WG
        def _(e):
            ctr.wait(e, "CC")
            ctr.inc(e.dma_start(out=wo_s[:],
                                in_=wo_d.rearrange("(k p) (o n) -> p k o n", p=128, n=128)),
                    "LW", 16)
            ctr.inc(e.dma_start(out=wog1_s[:],
                                in_=wog1_d.rearrange("(k p) n -> p k n", p=128)), "LW", 16)
            ctr.inc(e.dma_start(out=wog2_s[:],
                                in_=wog2_d.rearrange("k (o n) -> k o n", n=128)), "LW", 16)
        lw_p4 = ctr.val("LW")

        for tt in range(4):
            t0 = tt * 512
            markt = len(_guards)

            pre = {nm: ctr.val(nm) for nm in ["MM", "VE", "SC", "GP", "ST"]}

            @blk.sync
            def _(e, t0=t0, pre=dict(pre)):
                ctr.wait(e, "LD")
                ctr.wait(e, "CC")
                for nm in ["MM", "VE", "SC"]:
                    e.wait_ge(ctr.h[nm], pre[nm])
                for kb in range(KB):
                    ctr.inc(e.dma_start(
                        out=o_t[:, kb, :],
                        in_=a2a_out.ap()[kb // 2, 128 * (kb % 2):128 * (kb % 2) + 128,
                                         t0:t0 + 512]), "LD", 16)
                for kb in range(KB):
                    ctr.inc(e.dma_start(
                        out=xlt[:, kb, :],
                        in_=ag_in.ap()[kb * 128:(kb + 1) * 128, t0:t0 + 512]), "LD", 16)
            ld4 = ctr.val("LD")

            @WG
            def _(e, t0=t0, ld4=ld4):
                e.wait_ge(ctr.h["LD"], ld4)
                ctr.wait(e, "LW")
                for kb in range(KB):
                    ctr.inc(e.dma_start(
                        out=o_t[:, kb, :],
                        in_=a2a_out.ap()[4 + kb // 2, 128 * (kb % 2):128 * (kb % 2) + 128,
                                         t0:t0 + 512], accum_op=AL.add), "LW", 16)
            lw4 = ctr.val("LW")

            p4a = pal("p4a" + str(tt), (128, 2048))
            p4B = [p4a[:, i * 512:(i + 1) * 512] for i in range(4)]

            @blk.tensor
            def _(e, ld4=ld4):
                e.wait_ge(ctr.h["LD"], ld4)
                e.wait_ge(ctr.h["LW"], lw_p4)
                for kb in range(KB):
                    mi = e.matmul(p4B[0][0:64, 0:512], wog1_s[:, kb, :], xlt[:, kb, :],
                                  start=(kb == 0), stop=(kb == KB - 1))
                ctr.inc(mi, "MM")
            mm_g1 = ctr.val("MM")

            @WV
            def _(e, mm_g1=mm_g1):
                e.wait_ge(ctr.h["MM"], mm_g1)
                ctr.inc(e.tensor_copy(zog1[:], p4B[0][0:64, 0:512]), "SC")
            sc_g1 = ctr.val("SC")

            @blk.tensor
            def _(e, sc_g1=sc_g1):
                e.wait_ge(ctr.h["SC"], sc_g1)
                for ob in range(KB):
                    if ob >= 3:
                        e.wait_ge(ctr.h["SC"], sc_g1 + ob - 2)
                    ctr.inc(e.matmul(p4B[ob % 4][:, 0:512], wog2_s[:, ob, :], zog1[:],
                                     start=True, stop=True), "MM")
            mm_g2 = ctr.val("MM")

            @WS
            def _(e, mm_g2=mm_g2):
                for ob in range(KB):
                    e.wait_ge(ctr.h["MM"], mm_g2 - KB + ob + 1)
                    ctr.inc(e.activation(og_t[:, ob, :], p4B[ob % 4][:, 0:512],
                                         AF.Sigmoid), "SC")
            sc_og = ctr.val("SC")

            @WV
            def _(e, sc_og=sc_og, ld4=ld4, lw4=lw4):
                e.wait_ge(ctr.h["SC"], sc_og)
                e.wait_ge(ctr.h["LD"], ld4)
                e.wait_ge(ctr.h["LW"], lw4)
                for kb in range(KB):
                    ctr.inc(e.tensor_mul(o_t[:, kb, :], o_t[:, kb, :], og_t[:, kb, :]), "VE")
            free_to(markt)

            p_st = pal("pst4" + str(tt), (128, 512))
            p_sq = pal("psq4" + str(tt), (128, 512))
            p_bc = pal("pbc4" + str(tt), (128, 1024))
            ln_stage([o_t[:, kb, :] for kb in range(KB)], 512,
                     [oln[:, kb, :] for kb in range(KB)], scr4, p_st, p_sq, p_bc,
                     nw=sb_normw)
            free_to(markt)

            p4w = pal("p4w" + str(tt), (128, 1024))
            mm4 = ctr.val("MM")
            ve4 = ctr.val("VE")
            ld4b = ctr.val("LD")
            st4 = ctr.val("ST")

            @blk.sync
            def _(e, t0=t0, ve4=ve4):
                ctr.wait(e, "LD")
                e.wait_ge(ctr.h["VE"], ve4)
                for ob in range(KB):
                    ctr.inc(e.dma_start(out=x1t[:, ob, :],
                                        in_=x1_sp[ob * 128:(ob + 1) * 128, t0:t0 + 512]),
                            "LD", 16)
            ld_x1 = ctr.val("LD")

            @blk.tensor
            def _(e, mm4=mm4, ve4=ve4, pre_ve=ctr.val("VE")):
                e.wait_ge(ctr.h["VE"], pre_ve)
                for ob in range(KB):
                    if ob >= 2:
                        e.wait_ge(ctr.h["VE"], ve4 + ob - 1)
                    bank = p4w[:, (ob % 2) * 512:(ob % 2) * 512 + 512]
                    for kb in range(KB):
                        mi = e.matmul(bank, wo_s[:, kb, ob, :], oln[:, kb, :],
                                      start=(kb == 0), stop=(kb == KB - 1))
                    ctr.inc(mi, "MM")

            @WV
            def _(e, mm4=mm4, ld_x1=ld_x1, st4=st4):
                e.wait_ge(ctr.h["LD"], ld_x1)
                e.wait_ge(ctr.h["ST"], st4)
                for ob in range(KB):
                    e.wait_ge(ctr.h["MM"], mm4 + ob + 1)
                    ctr.inc(e.tensor_add(x2t[:, ob, :],
                                         p4w[:, (ob % 2) * 512:(ob % 2) * 512 + 512],
                                         x1t[:, ob, :]), "VE")
            ve_x2 = ctr.val("VE")

            dst4 = yT if _STAGE == 2 else x2_sp

            @blk.sync
            def _(e, t0=t0, ve_x2=ve_x2, dst4=dst4):
                ctr.wait(e, "ST")
                for ob in range(KB):
                    e.wait_ge(ctr.h["VE"], ve_x2 - KB + ob + 1)
                    ctr.inc(e.dma_start(out=dst4[ob * 128:(ob + 1) * 128, t0:t0 + 512],
                                        in_=x2t[:, ob, :]), "ST", 16)
            free_to(markt)

        free_to(mark0)

    ffn_pass(xsT, x1_sp, make_ln_out=True, final_store=(_STAGE == 1), uid="f1")
    barrier()

    if _STAGE >= 2:
        @WG
        def _(e):
            ctr.wait(e, "ST")
            ctr.inc(e.collective_compute(
                "AllGather", AL.bypass,
                replica_groups=[[0, 1, 2, 3], [4, 5, 6, 7]],
                ins=[ag_in.ap().opt()], outs=[xg.ap().opt()]), "CC")
        attn()
        barrier()

        @WG
        def _(e):
            ctr.wait(e, "S2")
            with allow_ici_all_to_all():
                ctr.inc(e.collective_compute(
                    "AllToAll", AL.bypass,
                    replica_groups=[[0, 1, 2, 3, 4, 5, 6, 7]],
                    ins=[a2a_in.ap().opt()], outs=[a2a_out.ap().opt()]), "CC")
        post_attn()
        barrier()

    if _STAGE >= 3:
        ffn_pass(x2_sp, None, make_ln_out=False, final_store=True, uid="f2")

    @blk.sync
    def _(e):
        ctr.wait(e, "ST")

    blk_guard.__exit__(None, None, None)
    free_to(0)
    return nc


def _prep_inputs(inputs):
    f32 = np.float32
    x = np.ascontiguousarray(inputs["L0_attn_x"], dtype=f32).reshape(TOK, D)
    wq = inputs["L0_attn_Wq"]; wk = inputs["L0_attn_Wk"]; wv = inputs["L0_attn_Wv"]
    wgam = inputs["L0_attn_Wgamma"]; wf1 = inputs["L0_attn_Wf1"]; wf2 = inputs["L0_attn_Wf2"]
    wog1 = inputs["L0_attn_Wog1"]; wog2 = inputs["L0_attn_Wog2"]
    normw = inputs["L0_attn_norm_w"]; wo = inputs["L0_attn_Wo"]

    bf = ml_dtypes.bfloat16
    consts = {
        "c_idbf": np.eye(128, dtype=f32).astype(bf),
        "c_idf32": np.eye(128, dtype=f32),
        "c_idpair": np.concatenate([np.eye(64, dtype=f32)] * 2, axis=0).astype(bf),
        "c_id2pair": np.stack([np.concatenate([np.eye(64, dtype=f32)] * 2, axis=0)] * 2,
                              axis=1),
        "c_onescol": np.ones((128, 1), f32),
        "c_onesrow": np.ones((1, 128), f32),
        "c_sel4": np.zeros((128, 2, 4), f32),
        "c_bc4": np.zeros((4, 2, 128), f32),
        "c_maskl4": np.broadcast_to(np.tril(np.ones((128, 128), np.uint8), -1)[:, None, :],
                                    (128, 4, 128)).copy(),
        "c_masku4": np.broadcast_to(np.tril(np.ones((128, 128), np.uint8), 0).T[:, None, :],
                                    (128, 4, 128)).copy(),
    }
    for hh in range(4):
        pb, hp = hh // 2, hh % 2
        consts["c_sel4"][64 * hp:64 * hp + 64, pb, hh] = 1.0
        consts["c_bc4"][hh, pb, 64 * hp:64 * hp + 64] = 1.0

    shared = {
        "wfg": np.ascontiguousarray(
            np.concatenate([wf1, wgam], axis=1), dtype=f32),
        "wog1": np.ascontiguousarray(wog1, f32),
        "wog2": np.ascontiguousarray(wog2, f32),
        "wo": np.ascontiguousarray(wo, f32),
        "normw": np.ascontiguousarray(np.asarray(normw, f32).reshape(KB, 128).T),
        "w1t": np.ascontiguousarray(inputs["L0_ffn_w1_w"].T, f32),
        "w2t": np.ascontiguousarray(inputs["L0_ffn_w2_w"].T, f32),
        "h1t": np.ascontiguousarray(inputs["L0_ffn_h1_w"].T, f32),
        "h2t": np.ascontiguousarray(inputs["L0_ffn_h2_w"].T, f32),
        "w3t": np.ascontiguousarray(inputs["L0_ffn_w3_w"].T, f32),
        "s1": np.ascontiguousarray(np.asarray(inputs["L0_ffn_w1_s"], f32).reshape(HB, 128).T),
        "s2": np.ascontiguousarray(np.asarray(inputs["L0_ffn_w2_s"], f32).reshape(HB, 128).T),
        "sh1": np.ascontiguousarray(np.asarray(inputs["L0_ffn_h1_s"], f32).reshape(HB, 128).T),
        "sh2": np.ascontiguousarray(np.asarray(inputs["L0_ffn_h2_s"], f32).reshape(HB, 128).T),
        "s3": np.ascontiguousarray(np.asarray(inputs["L0_ffn_w3_s"], f32).reshape(KB, 128).T),
    }
    shared.update(consts)

    in_maps = []
    for c in range(NCORES):
        hsel = slice(256 * (c % 4), 256 * (c % 4) + 256)
        gsel = slice(4 * (c % 4), 4 * (c % 4) + 4)
        m = dict(shared)
        m["x_slabT"] = np.ascontiguousarray(x[c * SLAB:(c + 1) * SLAB].T, f32)
        m["wq"] = np.ascontiguousarray(wq[:, hsel], f32)
        m["wk"] = np.ascontiguousarray(wk[:, hsel], f32)
        m["wv"] = np.ascontiguousarray(wv[:, hsel], f32)
        m["wf2"] = np.ascontiguousarray(wf2[:, hsel], f32)
        m["wfg"] = np.ascontiguousarray(
            np.concatenate([wf1, wgam[:, gsel]], axis=1), f32)
        msel = np.zeros((64, 2), f32)
        msel[:, 0 if c < 4 else 1] = 1.0
        m["c_msel"] = msel
        in_maps.append(m)
    return in_maps


def kernel(**inputs):
    in_maps = _prep_inputs(inputs)
    nc = build_program()
    res = run_bass_kernel_spmd(nc, in_maps, core_ids=list(range(NCORES)))
    out = np.concatenate([np.asarray(r["y_slabT"]).T for r in res.results], axis=0)
    return np.ascontiguousarray(out.reshape(B, N, D), dtype=np.float32)


# version 07
# speedup vs baseline: 1.0288x; 1.0158x over previous
import numpy as np
import ml_dtypes
import concourse.bass as bass
import concourse.mybir as mybir
from concourse.bass_utils import run_bass_kernel_spmd
from concourse.aot_env import allow_ici_all_to_all

F32 = mybir.dt.float32
BF16 = mybir.dt.bfloat16
U8 = mybir.dt.uint8
AF = mybir.ActivationFunctionType
AL = mybir.AluOpType

NCORES = 8
B, N, D, H, HD = 2, 8192, 1024, 16, 64
TOK = B * N
SLAB = TOK // NCORES
C = 128
NCHUNK = N // C
SUPER = 512
NSUPER = N // SUPER
CPS = SUPER // C
DHID = 4096
KB = D // 128
HB = DHID // 128
HALF = 1024

_STAGE = 3
_SUB = None
SHIFT = 32 * float(np.log(2.0))
UNSCALE = 2.0 ** 64


class Ctr:
    def __init__(self, nc):
        self.nc = nc
        self.h = {}
        self.v = {}

    def new(self, name):
        self.h[name] = self.nc.alloc_semaphore(name=name)
        self.v[name] = 0

    def inc(self, inst, name, n=1):
        inst.then_inc(self.h[name], n)
        for st in _DW._state.values():
            if st.get("last") is inst:
                st["pending_inc"] = True
        self.v[name] += n
        return self.v[name]

    def val(self, name):
        return self.v[name]

    def wait(self, eng, name, val=None):
        eng.wait_ge(self.h[name], self.v[name] if val is None else val)



_DW_OPS = {
    "tensor_tensor", "tensor_add", "tensor_sub", "tensor_mul", "tensor_max",
    "tensor_scalar", "tensor_single_scalar", "tensor_scalar_add",
    "tensor_scalar_sub", "tensor_scalar_mul", "tensor_scalar_max",
    "tensor_scalar_min", "tensor_copy", "tensor_relu", "tensor_reduce",
    "reduce_max", "reduce_sum", "scalar_tensor_tensor", "tensor_tensor_scan",
    "select", "reciprocal", "memset", "memzero", "activation", "copy", "mul",
    "add", "sign", "square", "sqrt", "bn_stats", "bn_aggr", "transpose",
    "stream_shuffle", "copy_predicated",
}


def _ap_span(ap):
    try:
        name = ap.tensor.name
    except Exception:
        return None
    try:
        off = int(ap.offset)
    except Exception:
        off = 0
    span = 1
    try:
        for step, count in ap.ap:
            span += abs(int(step)) * (int(count) - 1)
    except Exception:
        span = 1 << 40
    return (name, off, off + span)


class _DW:

    _state = {}

    def __init__(self, e):
        object.__setattr__(self, "_e", e)
        key = getattr(e, "engine", None)
        key = str(key) if key is not None else id(type(e))
        st = _DW._state.setdefault(key, {"W": [], "R": []})
        object.__setattr__(self, "_st", st)

    @staticmethod
    def _overlap(iv, lst):
        if iv is None:
            return True
        name, lo, hi = iv
        for n2, l2, h2 in lst:
            if n2 == name and lo < h2 and l2 < hi:
                return True
        return False

    def __getattr__(self, name):
        a = getattr(self._e, name)
        if name == "drain":
            st = self._st

            def dr(_a=a):
                st["W"] = []
                st["R"] = []
                st["pending_inc"] = False
                return _a()
            return dr
        if name not in _DW_OPS:
            return a
        st = self._st

        def call(*args, _a=a, **kw):
            from concourse.ap import AP as _AP
            aps = [x for x in args if isinstance(x, _AP)]
            aps += [v for v in kw.values() if isinstance(v, _AP)]
            outs = aps[:1]
            ins = aps[1:]
            need = st.get("pending_inc", False)
            if not need:
                for x in ins:
                    if _DW._overlap(_ap_span(x), st["W"]):
                        need = True
                        break
            if not need:
                for x in outs:
                    iv = _ap_span(x)
                    if _DW._overlap(iv, st["W"]) or _DW._overlap(iv, st["R"]):
                        need = True
                        break
            if need or len(st["W"]) > 64:
                self._e.drain()
                st["W"] = []
                st["R"] = []
                st["pending_inc"] = False
            r = _a(*args, **kw)
            for x in outs:
                iv = _ap_span(x)
                st["W"].append(iv if iv else ("?", 0, 1 << 40))
            for x in ins:
                iv = _ap_span(x)
                st["R"].append(iv if iv else ("?", 0, 1 << 40))
            st["last"] = r
            return r
        return call


def build_program():
    _DW._state = {}
    nc = bass.Bass("TRN2", target_bir_lowering=False, debug=False)

    def din(name, shape, dtype=F32):
        return nc.dram_tensor(name, list(shape), dtype, kind="ExternalInput").ap()

    xsT = din("x_slabT", (D, SLAB))
    wq_d = din("wq", (D, 256)); wk_d = din("wk", (D, 256)); wv_d = din("wv", (D, 256))
    wfg_d = din("wfg", (D, 68))
    wf2_d = din("wf2", (64, 256))
    wog1_d = din("wog1", (D, 64)); wog2_d = din("wog2", (64, D))
    wo_d = din("wo", (D, D))
    normw_d = din("normw", (128, KB))
    w1t = din("w1t", (D, DHID)); w2t = din("w2t", (D, DHID))
    h1t = din("h1t", (DHID, DHID)); h2t = din("h2t", (DHID, DHID))
    w3t = din("w3t", (DHID, D))
    s1_d = din("s1", (128, HB)); s2_d = din("s2", (128, HB))
    sh1_d = din("sh1", (128, HB)); sh2_d = din("sh2", (128, HB))
    s3_d = din("s3", (128, KB))
    c_idbf = din("c_idbf", (128, 128), BF16)
    c_idf32 = din("c_idf32", (128, 128))
    c_idpair = din("c_idpair", (128, 64), BF16)
    c_id2pair = din("c_id2pair", (128, 2, 64))
    c_onescol = din("c_onescol", (128, 1))
    c_onesrow = din("c_onesrow", (1, 128))
    c_sel4 = din("c_sel4", (128, 2, 4))
    c_bc4 = din("c_bc4", (4, 2, 128))
    c_msel = din("c_msel", (64, 2))
    c_maskl4 = din("c_maskl4", (128, 4, 128), U8)
    c_masku4 = din("c_masku4", (128, 4, 128), U8)

    yT = nc.dram_tensor("y_slabT", [D, SLAB], F32, kind="ExternalOutput").ap()

    x1_sp = nc.dram_tensor("x1_sp", [D, SLAB], F32).ap()
    x2_sp = nc.dram_tensor("x2_sp", [D, SLAB], F32).ap()
    ag_in = nc.dram_tensor("ag_in", [D, SLAB], BF16)
    xg = nc.dram_tensor("xg", [4, D, SLAB], BF16)
    a2a_in = nc.dram_tensor("a2a_in", [8, 256, SLAB], F32)
    a2a_out = nc.dram_tensor("a2a_out", [8, 256, SLAB], F32)

    ctr = Ctr(nc)
    for s in ["LD", "LW", "LWa", "LWb", "ST", "S2", "MM", "VE", "SC", "GP", "CC"]:
        ctr.new(s)

    _guards = []

    def alloc(name, shape, dtype=F32):
        g = nc.sbuf_tensor(name, list(shape), dtype)
        h = g.__enter__()
        _guards.append(g)
        return h

    def pal(name, shape):
        g = nc.psum_tensor(name, list(shape), F32)
        h = g.__enter__()
        _guards.append(g)
        return h

    def free_to(mark):
        while len(_guards) > mark:
            _guards.pop().__exit__(None, None, None)

    sb_idbf = alloc("sb_idbf", (128, 128), BF16)
    sb_idf32 = alloc("sb_idf32", (128, 128))
    sb_idpair = alloc("sb_idpair", (128, 64), BF16)
    sb_id2pair = alloc("sb_id2pair", (128, 2, 64))
    sb_onescol = alloc("sb_onescol", (128, 1))
    sb_onesrow = alloc("sb_onesrow", (1, 128))
    sb_sel4 = alloc("sb_sel4", (128, 2, 4))
    sb_bc4 = alloc("sb_bc4", (4, 2, 128))
    sb_msel = alloc("sb_msel", (64, 2))
    sb_maskl = alloc("sb_maskl", (128, 4, 128), U8)
    sb_masku = alloc("sb_masku", (128, 4, 128), U8)
    sb_zeros = alloc("sb_zeros", (128, 4, 128))
    sb_normw = alloc("sb_normw", (128, KB))
    sb_rows = alloc("sb_rows", (1, 2048))
    sb_scales = alloc("sb_scales", (128, 6, HB))
    PMARK = len(_guards)

    blk_guard = nc.Block()
    blk = blk_guard.__enter__()

    def WV(f):
        blk.vector(lambda e: f(_DW(e)))

    def WS(f):
        blk.scalar(lambda e: f(_DW(e)))

    def WG(f):
        blk.gpsimd(lambda e: f(_DW(e)))


    @blk.sync
    def _(e):
        for dst, src in [(sb_idbf, c_idbf), (sb_idf32, c_idf32), (sb_idpair, c_idpair),
                         (sb_id2pair, c_id2pair), (sb_onescol, c_onescol),
                         (sb_onesrow, c_onesrow), (sb_sel4, c_sel4), (sb_bc4, c_bc4),
                         (sb_maskl, c_maskl4), (sb_masku, c_masku4)]:
            ctr.inc(e.dma_start(out=dst[:], in_=src), "LD", 16)
        ctr.inc(e.dma_start(out=sb_msel[:], in_=c_msel), "LD", 16)
        ctr.inc(e.dma_start(out=sb_normw[:], in_=normw_d), "LD", 16)
        for i, s in zip([0, 1, 3, 4], [s1_d, s2_d, sh1_d, sh2_d]):
            ctr.inc(e.dma_start(out=sb_scales[:, i, :], in_=s), "LD", 16)
        ctr.inc(e.dma_start(out=sb_scales[:, 5, 0:KB], in_=s3_d), "LD", 16)

    @WV
    def _(e):
        ctr.wait(e, "LD")
        e.memset(sb_zeros[:], 0.0)
        e.tensor_mul(sb_scales[:, 2, :], sb_scales[:, 0, :], sb_scales[:, 1, :])
        ctr.inc(e.tensor_scalar_mul(sb_scales[:, 5, 0:KB], sb_scales[:, 5, 0:KB], 0.5), "VE")

    def barrier():
        names = ["LD", "LW", "LWa", "LWb", "ST", "S2", "MM", "VE", "SC", "GP", "CC"]
        for sec in [blk.sync, blk.gpsimd, blk.scalar, blk.vector, blk.tensor]:
            def body(e, _n=tuple(names)):
                for nm in _n:
                    if ctr.val(nm):
                        ctr.wait(e, nm)
            sec(body)

    def ln_stage(xt, twid, out_tiles, scr, p_stat, p_sq, p_bc, nw=None):
        nq = twid // 512
        ld_now = ctr.val("LD")
        ve_now = ctr.val("VE")

        @blk.tensor
        def _(e):
            e.wait_ge(ctr.h["LD"], ld_now)
            e.wait_ge(ctr.h["VE"], ve_now)
            for tq in range(nq):
                for kb in range(KB):
                    mi = e.matmul(p_stat[0:1, tq * 512:(tq + 1) * 512], sb_onescol[:],
                                  xt[kb][:, tq * 512:(tq + 1) * 512],
                                  start=(kb == 0), stop=(kb == KB - 1))
            ctr.inc(mi, "MM")
        mm_sx = ctr.val("MM")

        sc0 = ctr.val("SC")

        @WS
        def _(e):
            e.wait_ge(ctr.h["LD"], ld_now)
            e.wait_ge(ctr.h["VE"], ve_now)
            for kb in range(KB):
                if kb >= 2:
                    e.wait_ge(ctr.h["MM"], mm_sx + kb - 1)
                ctr.inc(e.activation(scr[:, kb % 2, 0:twid], xt[kb], AF.Square), "SC")

        @blk.tensor
        def _(e):
            for kb in range(KB):
                e.wait_ge(ctr.h["SC"], sc0 + kb + 1)
                for tq in range(nq):
                    mi = e.matmul(p_sq[0:1, tq * 512:(tq + 1) * 512], sb_onescol[:],
                                  scr[:, kb % 2, tq * 512:(tq + 1) * 512],
                                  start=(kb == 0), stop=(kb == KB - 1))
                ctr.inc(mi, "MM")
        mm_sq = ctr.val("MM")

        r0 = scr[0:1, 0, 0:twid]
        r1 = scr[0:1, 1, 0:twid]

        @WV
        def _(e):
            e.wait_ge(ctr.h["MM"], mm_sq)
            e.tensor_scalar_mul(sb_rows[0:1, 0:twid], p_stat[0:1, 0:twid], 1.0 / D)
            e.tensor_scalar_mul(p_sq[0:1, 0:twid], p_sq[0:1, 0:twid], 1.0 / D)
            e.tensor_mul(r0, sb_rows[0:1, 0:twid], sb_rows[0:1, 0:twid])
            e.tensor_sub(p_sq[0:1, 0:twid], p_sq[0:1, 0:twid], r0)
            ctr.inc(e.tensor_scalar_add(p_sq[0:1, 0:twid], p_sq[0:1, 0:twid], 1e-5), "VE")
        v_var = ctr.val("VE")

        @WS
        def _(e):
            e.wait_ge(ctr.h["VE"], v_var)
            ctr.inc(e.activation(p_stat[0:1, 0:twid], p_sq[0:1, 0:twid], AF.Sqrt), "SC")
        sc_sq = ctr.val("SC")

        @WV
        def _(e):
            e.wait_ge(ctr.h["SC"], sc_sq)
            e.reciprocal(r1, p_stat[0:1, 0:twid])
            e.tensor_mul(r0, r1, r1)
            e.tensor_mul(r0, p_sq[0:1, 0:twid], r0)
            e.tensor_scalar(r0, r0, -0.5, 1.5, AL.mult, AL.add)
            ctr.inc(e.tensor_mul(sb_rows[0:1, 1024:1024 + twid], r1, r0), "VE")

        for tq in range(nq):
            v_pre = ctr.val("VE")

            @blk.tensor
            def _(e, tq=tq, v_pre=v_pre):
                e.wait_ge(ctr.h["VE"], v_pre)
                e.matmul(p_bc[:, 0:512], sb_onesrow[:],
                         sb_rows[0:1, tq * 512:(tq + 1) * 512], start=True, stop=True)
                ctr.inc(e.matmul(p_bc[:, 512:1024], sb_onesrow[:],
                                 sb_rows[0:1, 1024 + tq * 512:1024 + (tq + 1) * 512],
                                 start=True, stop=True), "MM")
            mm_bc = ctr.val("MM")

            @WV
            def _(e, tq=tq, mm_bc=mm_bc):
                e.wait_ge(ctr.h["MM"], mm_bc)
                sl = slice(tq * 512, (tq + 1) * 512)
                for kb in range(KB):
                    e.tensor_sub(scr[:, 2, 0:512], xt[kb][:, sl], p_bc[:, 0:512])
                    if nw is None:
                        vi = e.tensor_mul(out_tiles[kb][:, sl], scr[:, 2, 0:512],
                                          p_bc[:, 512:1024])
                    else:
                        vi = e.scalar_tensor_tensor(out_tiles[kb][:, sl], scr[:, 2, 0:512],
                                                    nw[:, kb:kb + 1], p_bc[:, 512:1024],
                                                    AL.mult, AL.mult)
                ctr.inc(vi, "VE")

    def ffn_pass(src_ap, spill_ap, make_ln_out, final_store, uid):
        mark0 = len(_guards)
        xln = alloc("xln" + uid, (128, KB, HALF), BF16)
        Hb = alloc("Hb" + uid, (128, HB, HALF), BF16)
        Gb = alloc("Gb" + uid, (128, HB, HALF), BF16)
        wb = alloc("wb" + uid, (128, 2, HB, 128), BF16)
        scr = alloc("scr" + uid, (128, 4, HALF))
        xres = alloc("xres" + uid, (128, 2, 512))

        GbF = Gb[:].bitcast(F32)
        HbF = Hb[:].bitcast(F32)

        for hf in range(2):
            t0 = hf * HALF
            markp = len(_guards)

            p_stat = pal("pstA" + uid + str(hf), (128, 1024))
            p_sq = pal("psqA" + uid + str(hf), (128, 1024))
            p_bc = pal("pbcA" + uid + str(hf), (128, 1024))

            xtiles = [HbF[:, 4 * kb:4 * kb + 4, :].rearrange("p a b -> p (a b)")
                      for kb in range(KB)]
            xtiles = [HbF[:, 2 * kb:2 * kb + 2, :].rearrange("p a b -> p (a b)")
                      for kb in range(KB)]

            @blk.sync
            def _(e):
                for nm in ["MM", "SC", "GP", "VE", "ST", "LD"]:
                    if ctr.val(nm):
                        ctr.wait(e, nm)
                for kb in range(KB):
                    ctr.inc(e.dma_start(out=xtiles[kb],
                                        in_=src_ap[kb * 128:(kb + 1) * 128, t0:t0 + HALF]),
                            "LD", 16)

            ln_stage(xtiles, HALF, [xln[:, kb, :] for kb in range(KB)],
                     scr, p_stat, p_sq, p_bc)
            free_to(markp)
            if _SUB == 'ln':
                break

            p_f = pal("pf" + uid + str(hf), (128, 2048))
            p_f2 = pal("pf2" + uid + str(hf), (128, 2048))
            pbank = [p_f[:, i * 512:(i + 1) * 512] for i in range(4)] + \
                    [p_f2[:, i * 512:(i + 1) * 512] for i in range(4)]

            lwp0 = [ctr.val("LWa"), ctr.val("LWb")]
            mm0 = ctr.val("MM")
            ve_xln = ctr.val("VE")
            ve0 = ctr.val("VE")

            sc0_pre = ctr.val("SC")
            gp_base = ctr.val("GP")

            def _w12_gp_mul(e, ob):
                p = ob % 2
                for tq in range(2):
                    e.wait_ge(ctr.h["SC"], sc0_pre + 4 * ob + 2 * tq + 2)
                    ctr.inc(e.tensor_mul(scr[:, p, tq * 512:(tq + 1) * 512],
                                         scr[:, p, tq * 512:(tq + 1) * 512],
                                         scr[:, 2 + p, tq * 512:(tq + 1) * 512]), "GP")

            @WG
            def _(e):
                e.wait_ge(ctr.h["VE"], ve_xln)
                for ob in range(HB):
                    p = ob % 2
                    sem = "LWa" if p == 0 else "LWb"
                    if ob >= 2:
                        e.wait_ge(ctr.h["MM"], mm0 + (ob - 1) * 32)
                    ctr.inc(e.dma_start(
                        out=wb[:, p, 0:8, :],
                        in_=w1t[:, ob * 128:(ob + 1) * 128].rearrange("(k p) n -> p k n", p=128)),
                        sem, 16)
                    ctr.inc(e.dma_start(
                        out=wb[:, p, 8:16, :],
                        in_=w2t[:, ob * 128:(ob + 1) * 128].rearrange("(k p) n -> p k n", p=128)),
                        sem, 16)
                    if ob >= 2:
                        _w12_gp_mul(e, ob - 2)
                _w12_gp_mul(e, HB - 2)
                _w12_gp_mul(e, HB - 1)

            @blk.tensor
            def _(e):
                e.wait_ge(ctr.h["VE"], ve_xln)
                for ob in range(HB):
                    p = ob % 2
                    sem = "LWa" if p == 0 else "LWb"
                    e.wait_ge(ctr.h[sem], lwp0[p] + 32 * (ob // 2 + 1))
                    if ob >= 2:
                        e.wait_ge(ctr.h["VE"], ve0 + 2 * (ob - 1))
                    for w12 in range(2):
                        for kb in range(KB):
                            for tq in range(2):
                                bank = pbank[4 * p + 2 * w12 + tq]
                                mi = e.matmul(bank[:, 0:512], wb[:, p, 8 * w12 + kb, :],
                                              xln[:, kb, tq * 512:(tq + 1) * 512],
                                              start=(kb == 0), stop=(kb == KB - 1))
                    ctr.inc(mi, "MM", 32)

            sc0 = ctr.val("SC")

            @WS
            def _(e):
                for ob in range(HB):
                    p = ob % 2
                    e.wait_ge(ctr.h["MM"], mm0 + 32 * (ob + 1))
                    for tq in range(2):
                        ctr.inc(e.activation(scr[:, p, tq * 512:(tq + 1) * 512],
                                             pbank[4 * p + tq][:, 0:512], AF.Sigmoid,
                                             scale=sb_scales[:, 0, ob:ob + 1]), "SC")
                        ctr.inc(e.mul(scr[:, 2 + p, tq * 512:(tq + 1) * 512],
                                      pbank[4 * p + 2 + tq][:, 0:512],
                                      sb_scales[:, 1, ob:ob + 1]), "SC")

            gp0 = gp_base

            @WV
            def _(e):
                for ob in range(HB):
                    p = ob % 2
                    for tq in range(2):
                        e.wait_ge(ctr.h["GP"], gp0 + 2 * ob + tq + 1)
                        ctr.inc(e.scalar_tensor_tensor(Hb[:, ob, tq * 512:(tq + 1) * 512],
                                                       pbank[4 * p + tq][:, 0:512],
                                                       sb_scales[:, 0, ob:ob + 1],
                                                       scr[:, p, tq * 512:(tq + 1) * 512],
                                                       AL.mult, AL.mult), "VE")

            if _SUB == 'w12':
                free_to(markp)
                break
            hstages = [(h1t, 3, Hb, Gb), (h2t, 4, Gb, Hb)]
            if _SUB == 'h1':
                hstages = hstages[:1]
            for (wt, sidx, src_b, dst_b) in hstages:
                lwp1 = [ctr.val("LWa"), ctr.val("LWb")]
                mm1 = ctr.val("MM")
                sc1 = ctr.val("SC")
                pre_ve = ctr.val("VE")
                pre_sc = ctr.val("SC")
                pre_st = ctr.val("ST")

                @WG
                def _(e, wt=wt, mm1=mm1, pre_ve=pre_ve):
                    e.wait_ge(ctr.h["VE"], pre_ve)
                    e.wait_ge(ctr.h["MM"], mm1)
                    for ob in range(HB):
                        p = ob % 2
                        sem = "LWa" if p == 0 else "LWb"
                        if ob >= 2:
                            e.wait_ge(ctr.h["MM"], mm1 + (ob - 1) * 64)
                        ctr.inc(e.dma_start(
                            out=wb[:, p, :, :],
                            in_=wt[:, ob * 128:(ob + 1) * 128].rearrange(
                                "(k p) n -> p k n", p=128)), sem, 16)

                @blk.tensor
                def _(e, lwp1=lwp1, sc1=sc1, src_b=src_b, pre_ve=pre_ve):
                    e.wait_ge(ctr.h["VE"], pre_ve)
                    e.wait_ge(ctr.h["SC"], sc1)
                    for ob in range(HB):
                        p = ob % 2
                        sem = "LWa" if p == 0 else "LWb"
                        e.wait_ge(ctr.h[sem], lwp1[p] + 16 * (ob // 2 + 1))
                        if ob >= 2:
                            e.wait_ge(ctr.h["SC"], sc1 + 2 * (ob - 1))
                        for kk in range(HB):
                            for tq in range(2):
                                bank = pbank[4 * p + tq]
                                mi = e.matmul(bank[:, 0:512], wb[:, p, kk, :],
                                              src_b[:, kk, tq * 512:(tq + 1) * 512],
                                              start=(kk == 0), stop=(kk == HB - 1))
                        ctr.inc(mi, "MM", 64)

                @WS
                def _(e, mm1=mm1, sidx=sidx, dst_b=dst_b, pre_st=pre_st):
                    e.wait_ge(ctr.h["ST"], pre_st)
                    for ob in range(HB):
                        p = ob % 2
                        e.wait_ge(ctr.h["MM"], mm1 + 64 * (ob + 1))
                        for tq in range(2):
                            ctr.inc(e.activation(dst_b[:, ob, tq * 512:(tq + 1) * 512],
                                                 pbank[4 * p + tq][:, 0:512], AF.Gelu,
                                                 scale=sb_scales[:, sidx, ob:ob + 1]), "SC")

            if _SUB in ('h1', 'h2'):
                free_to(markp)
                break
            free_to(markp)

            p_w = pal("pw" + uid + str(hf), (128, 1024))
            lwp3 = [ctr.val("LWa"), ctr.val("LWb")]
            mm3 = ctr.val("MM")
            ve3 = ctr.val("VE")
            ld3 = ctr.val("LD")
            sc_h2 = ctr.val("SC")

            x1tiles = [GbF[:, 2 * kb:2 * kb + 2, :].rearrange("p a b -> p (a b)")
                       for kb in range(KB)]

            @WG
            def _(e):
                e.wait_ge(ctr.h["SC"], sc_h2)
                for ob in range(KB):
                    p = ob % 2
                    sem = "LWa" if p == 0 else "LWb"
                    if ob >= 2:
                        e.wait_ge(ctr.h["MM"], mm3 + (ob - 1) * 64)
                    ctr.inc(e.dma_start(
                        out=wb[:, p, :, :],
                        in_=w3t[:, ob * 128:(ob + 1) * 128].rearrange("(k p) n -> p k n", p=128)),
                        sem, 16)

            @blk.sync
            def _(e):
                ctr.wait(e, "LD")
                for ob in range(KB):
                    if ob >= 1:
                        e.wait_ge(ctr.h["VE"], ve3 + 2 * ob)
                    for tq in range(2):
                        ctr.inc(e.dma_start(out=xres[:, tq, :],
                                            in_=src_ap[ob * 128:(ob + 1) * 128,
                                                       t0 + tq * 512:t0 + (tq + 1) * 512]),
                                "LD", 16)

            @blk.tensor
            def _(e):
                e.wait_ge(ctr.h["SC"], sc_h2)
                for ob in range(KB):
                    p = ob % 2
                    sem = "LWa" if p == 0 else "LWb"
                    e.wait_ge(ctr.h[sem], lwp3[p] + 16 * (ob // 2 + 1))
                    if ob >= 1:
                        e.wait_ge(ctr.h["VE"], ve3 + 2 * ob)
                    for kk in range(HB):
                        for tq in range(2):
                            bank = p_w[:, tq * 512:(tq + 1) * 512]
                            mi = e.matmul(bank, wb[:, p, kk, :],
                                          Hb[:, kk, tq * 512:(tq + 1) * 512],
                                          start=(kk == 0), stop=(kk == HB - 1))
                    ctr.inc(mi, "MM", 64)

            @WV
            def _(e):
                for ob in range(KB):
                    e.wait_ge(ctr.h["MM"], mm3 + 64 * (ob + 1))
                    e.wait_ge(ctr.h["LD"], ld3 + 32 * (ob + 1))
                    for tq in range(2):
                        ctr.inc(e.scalar_tensor_tensor(
                            x1tiles[ob][:, tq * 512:(tq + 1) * 512],
                            p_w[:, tq * 512:(tq + 1) * 512],
                            sb_scales[:, 5, ob:ob + 1],
                            xres[:, tq, :], AL.mult, AL.add), "VE")
            ve_w3 = ctr.val("VE")

            dst_ap = yT if final_store else spill_ap

            @blk.sync
            def _(e):
                ctr.wait(e, "ST")
                for ob in range(KB):
                    e.wait_ge(ctr.h["VE"], ve3 + 2 * (ob + 1))
                    ctr.inc(e.dma_start(
                        out=dst_ap[ob * 128:(ob + 1) * 128, t0:t0 + HALF],
                        in_=x1tiles[ob]), "ST", 16)

            if make_ln_out:
                p_stat = pal("pstB" + uid + str(hf), (128, 1024))
                p_sq = pal("psqB" + uid + str(hf), (128, 1024))
                p_bc2 = pal("pbcB" + uid + str(hf), (128, 1024))
                ln_stage(x1tiles, HALF, [xln[:, kb, :] for kb in range(KB)],
                         scr, p_stat, p_sq, p_bc2)

                @blk.sync
                def _(e):
                    ctr.wait(e, "VE")
                    ctr.wait(e, "ST")
                    for kb in range(KB):
                        ctr.inc(e.dma_start(
                            out=ag_in.ap()[kb * 128:(kb + 1) * 128, t0:t0 + HALF],
                            in_=xln[:, kb, :]), "ST", 16)
            free_to(markp)

        free_to(mark0)

    def attn():
        mark0 = len(_guards)
        wq_s = alloc("wq_s", (128, KB, 256), BF16)
        wk_s = alloc("wk_s", (128, KB, 256), BF16)
        wv_s = alloc("wv_s", (128, KB, 256), BF16)
        wfg_s = alloc("wfg_s", (128, KB, 68), BF16)
        wf2_s = alloc("wf2_s", (64, 2, 128), BF16)
        xgt = alloc("xgt", (128, 2, KB, SUPER), BF16)
        qsil = alloc("qsil", (128, 2, 2, SUPER))
        ksil = alloc("ksil", (128, 2, 2, SUPER))
        khat = alloc("khat", (128, 2, 2, SUPER))
        vbf = alloc("vbf", (128, 2, 2, SUPER), BF16)
        sgk = alloc("sgk", (128, 2, 2, SUPER))
        gkb = alloc("gkb", (128, 2, 2, SUPER))
        rbb = alloc("rbb", (128, 2, SUPER))
        rgbb = alloc("rgbb", (128, 2, 2, SUPER))
        zf_bf = alloc("zf_bf", (64, SUPER), BF16)
        arows = alloc("arows", (4, 4, SUPER))
        Gw = alloc("Gw", (128, 2, SUPER))
        Gsc = alloc("Gsc", (128, CPS, 2, 6))
        egc = alloc("egc", (128, CPS, 2, 1))
        E1 = alloc("E1", (128, 2, SUPER)); E2 = alloc("E2", (128, 2, SUPER))
        E4 = alloc("E4", (128, 2, SUPER))
        tE = alloc("tE", (128, 2, SUPER))
        ksq = tE
        Qmid = alloc("Qmid", (128, 2, SUPER), BF16); Qful = alloc("Qful", (128, 2, SUPER), BF16)
        Kdot = alloc("Kdot", (128, 2, SUPER), BF16); Khb = alloc("Khb", (128, 2, SUPER), BF16)
        Adot = alloc("Adot", (128, 2, SUPER), BF16); Akm = alloc("Akm", (128, 2, SUPER), BF16)
        Xb = alloc("Xb", (128, 4, CPS, 128), BF16)
        PbA = alloc("PbA", (128, 4, CPS, 128), BF16); PtA = alloc("PtA", (128, 4, CPS, 128), BF16)
        PbB = alloc("PbB", (128, 4, CPS, 128), BF16); PtB = alloc("PtB", (128, 4, CPS, 128), BF16)
        POb = alloc("POb", (128, 4, CPS, 128), BF16)
        KTb = alloc("KTb", (128, 4, CPS, HD), BF16)
        QHb = alloc("QHb", (128, CPS, 4, C), BF16)
        WPb = alloc("WPb", (128, CPS, 4, HD), BF16)
        OLb = alloc("OLb", (128, CPS, 4, HD))
        SLb = alloc("SLb", (128, CPS, 2, HD))
        Sst = alloc("Sst", (128, 2, HD))
        Sbf = alloc("Sbf", (128, 2, HD), BF16)
        Otm = alloc("Otm", (128, 4, HD))
        Ovm = alloc("Ovm", (64, 4, SUPER))
        Ovm_lo = alloc("Ovm_lo", (64, 4, SUPER))
        Ovm_hi = alloc("Ovm_hi", (64, 4, SUPER))

        pmh = pal("pmh", (128, 2048))
        pful = pal("pful", (128, 2048))
        pfB = [pful[:, i * 512:(i + 1) * 512] for i in range(4)]
        pmB = [pmh[:, i * 512:(i + 1) * 512] for i in range(4)]

        def pmv(lo, hi):
            return pmh[:].rearrange("p (h x) -> p h x", x=512)[:, :, lo:hi]

        def pfv(lo, hi, x):
            return pful[:, lo:hi].rearrange("p (h x) -> p h x", x=x)

        @WG
        def _(e):
            ctr.wait(e, "CC")
            for dst, src in [(wq_s, wq_d), (wk_s, wk_d), (wv_s, wv_d), (wfg_s, wfg_d)]:
                ctr.inc(e.dma_start(out=dst[:],
                                    in_=src.rearrange("(k p) n -> p k n", p=128)), "LW", 16)
            ctr.inc(e.dma_start(out=wf2_s[:],
                                in_=wf2_d.rearrange("k (m n) -> k m n", n=128)), "LW", 16)
        lw_w = ctr.val("LW")

        @WV
        def _(e):
            e.memset(QHb[:], 0.0)
            e.memset(WPb[:], 0.0)
            e.memset(Sst[:], 0.0)
            ctr.inc(e.memset(Sbf[:], 0.0), "VE")

        pstate = {"ld": {-1: 0}, "zfg": {}, "vechunk": {-1: 0}}

        def sc_body(sc_i):
            ts = sc_i * SUPER
            rk = ts // SLAB
            tl = ts % SLAB
            p = sc_i % 2

            prev_ld = pstate["ld"][sc_i - 1]
            guard_mm = pstate["zfg"].get(sc_i - 2, 0)

            @blk.sync
            def _(e, rk=rk, tl=tl, p=p, prev_ld=prev_ld, guard_mm=guard_mm):
                ctr.wait(e, "CC")
                e.wait_ge(ctr.h["LD"], prev_ld)
                if guard_mm:
                    e.wait_ge(ctr.h["MM"], guard_mm)
                for kb in range(KB):
                    ctr.inc(e.dma_start(out=xgt[:, p, kb, :],
                                        in_=xg.ap()[rk, kb * 128:(kb + 1) * 128, tl:tl + SUPER]),
                            "LD", 16)
            ld_x = ctr.val("LD")
            pstate["ld"][sc_i] = ld_x

            def proj_mm(w_s, pb, bank, extra_sc=None, extra_ve=None):
                @blk.tensor
                def _(e, w_s=w_s, pb=pb, bank=bank, extra_sc=extra_sc,
                      extra_ve=extra_ve, p=p):
                    e.wait_ge(ctr.h["LD"], ld_x)
                    e.wait_ge(ctr.h["LW"], lw_w)
                    if extra_ve is not None:
                        e.wait_ge(ctr.h["VE"], extra_ve)
                    if extra_sc is not None:
                        e.wait_ge(ctr.h["SC"], extra_sc)
                    for kb in range(KB):
                        mi = e.matmul(bank[:, 0:512], w_s[:, kb, pb * 128:(pb + 1) * 128],
                                      xgt[:, p, kb, :], start=(kb == 0), stop=(kb == KB - 1))
                    ctr.inc(mi, "MM")
                return ctr.val("MM")

            def proj_ev(kind, dst, bank, mm_at, scale=None):
                if kind == "copy":
                    @WV
                    def _(e, dst=dst, bank=bank, mm_at=mm_at):
                        e.wait_ge(ctr.h["MM"], mm_at)
                        ctr.inc(e.tensor_copy(dst, bank), "VE")
                    return ctr.val("VE")
                @WS
                def _(e, kind=kind, dst=dst, bank=bank, mm_at=mm_at):
                    e.wait_ge(ctr.h["MM"], mm_at)
                    if kind == "silu":
                        ctr.inc(e.activation(dst, bank, AF.Silu), "SC")
                    elif kind == "sig":
                        ctr.inc(e.activation(dst, bank, AF.Sigmoid), "SC")
                return ctr.val("SC")

            ve_pfree = pstate["vechunk"][sc_i - 1]
            m = proj_mm(wq_s, 0, pfB[0], extra_ve=ve_pfree)
            proj_ev("silu", qsil[:, p, 0, :], pfB[0][:, 0:512], m)
            m = proj_mm(wq_s, 1, pfB[1]); proj_ev("silu", qsil[:, p, 1, :], pfB[1][:, 0:512], m)
            m = proj_mm(wk_s, 0, pfB[2]); proj_ev("silu", ksil[:, p, 0, :], pfB[2][:, 0:512], m)
            m = proj_mm(wk_s, 1, pfB[3]); sc_k1 = proj_ev("silu", ksil[:, p, 1, :], pfB[3][:, 0:512], m)
            sc_q = ctr.val("SC") - 2
            m = proj_mm(wv_s, 0, pfB[0], extra_sc=sc_q - 1)
            proj_ev("copy", vbf[:, p, 0, :], pfB[0][:, 0:512], m)
            m = proj_mm(wv_s, 1, pfB[1], extra_sc=sc_q)
            proj_ev("copy", vbf[:, p, 1, :], pfB[1][:, 0:512], m)

            @blk.tensor
            def _(e, pre_sc=ctr.val("SC") - 1, p=p):
                e.wait_ge(ctr.h["SC"], pre_sc)
                for kb in range(KB):
                    mi = e.matmul(pfB[2][0:68, 0:512], wfg_s[:, kb, :], xgt[:, p, kb, :],
                                  start=(kb == 0), stop=(kb == KB - 1))
                ctr.inc(mi, "MM")
            mm_zfg = ctr.val("MM")
            pstate["zfg"][sc_i] = mm_zfg

            @WV
            def _(e, mm_zfg=mm_zfg):
                e.wait_ge(ctr.h["MM"], mm_zfg)
                ctr.inc(e.tensor_copy(zf_bf[:, :], pfB[2][0:64, 0:512]), "VE")
            ve_zf = ctr.val("VE")
            @WS
            def _(e, mm_zfg=mm_zfg):
                e.wait_ge(ctr.h["MM"], mm_zfg)
                ctr.inc(e.activation(arows[0:4, 3, :], pfB[2][64:68, 0:512], AF.Sigmoid), "SC")
            sc_zf = ctr.val("SC")

            @blk.tensor
            def _(e, sc_zf=sc_zf, ve_zf=ve_zf):
                e.wait_ge(ctr.h["SC"], sc_zf)
                e.wait_ge(ctr.h["VE"], ve_zf)
                ctr.inc(e.matmul(pfB[3][:, 0:512], wf2_s[:, 0, :], zf_bf[:, :],
                                 start=True, stop=True), "MM")
                ctr.inc(e.matmul(pfB[0][:, 0:512], wf2_s[:, 1, :], zf_bf[:, :],
                                 start=True, stop=True), "MM")
            mm_zf2 = ctr.val("MM")

            @WS
            def _(e, mm_zf2=mm_zf2, p=p):
                e.wait_ge(ctr.h["MM"], mm_zf2 - 1)
                ctr.inc(e.activation(sgk[:, p, 0, :], pfB[3][:, 0:512], AF.Sigmoid), "SC")
                e.wait_ge(ctr.h["MM"], mm_zf2)
                ctr.inc(e.activation(sgk[:, p, 1, :], pfB[0][:, 0:512], AF.Sigmoid), "SC")
                for pb in range(2):
                    ctr.inc(e.activation(gkb[:, p, pb, :], sgk[:, p, pb, :], AF.Ln), "SC")
            sc_gk = ctr.val("SC")

            @WV
            def _(e, sc_k1=sc_k1, p=p):
                e.wait_ge(ctr.h["SC"], sc_k1)
                for pb in range(2):
                    ctr.inc(e.tensor_mul(ksq[:, pb, :], ksil[:, p, pb, :], ksil[:, p, pb, :]), "VE")
            ve_ksq = ctr.val("VE")

            @blk.tensor
            def _(e, ve_ksq=ve_ksq, sc_gk=sc_gk):
                e.wait_ge(ctr.h["VE"], ve_ksq)
                e.wait_ge(ctr.h["SC"], sc_gk)
                for pb in range(2):
                    mi = e.matmul(pfB[1][0:4, 0:512], sb_sel4[:, pb, :], ksq[:, pb, :],
                                  start=(pb == 0), stop=(pb == 1))
                ctr.inc(mi, "MM")
            mm_n2 = ctr.val("MM")

            @WV
            def _(e, mm_n2=mm_n2):
                e.wait_ge(ctr.h["MM"], mm_n2)
                ctr.inc(e.tensor_scalar_max(arows[0:4, 0, :], pfB[1][0:4, 0:512], 1e-24), "VE")
            ve_n2 = ctr.val("VE")

            @WS
            def _(e, ve_n2=ve_n2):
                e.wait_ge(ctr.h["VE"], ve_n2)
                ctr.inc(e.activation(arows[0:4, 1, :], arows[0:4, 0, :], AF.Sqrt), "SC")
            sc_n2 = ctr.val("SC")

            @WV
            def _(e, sc_n2=sc_n2):
                e.wait_ge(ctr.h["SC"], sc_n2)
                e.reciprocal(arows[0:4, 1, :], arows[0:4, 1, :])
                e.tensor_mul(arows[0:4, 2, :], arows[0:4, 1, :], arows[0:4, 1, :])
                e.tensor_mul(arows[0:4, 2, :], arows[0:4, 0, :], arows[0:4, 2, :])
                e.tensor_scalar(arows[0:4, 2, :], arows[0:4, 2, :], -0.5, 1.5, AL.mult, AL.add)
                e.tensor_mul(arows[0:4, 1, :], arows[0:4, 1, :], arows[0:4, 2, :])
                e.tensor_mul(arows[0:4, 2, :], arows[0:4, 1, :], arows[0:4, 3, :])
                ctr.inc(e.tensor_scalar_mul(arows[0:4, 2, :], arows[0:4, 2, :], -1.0), "VE")
            ve_r = ctr.val("VE")

            for pb in range(2):
                @blk.tensor
                def _(e, pb=pb, ve_r=ve_r, pre_ve=ctr.val("VE")):
                    e.wait_ge(ctr.h["VE"], max(ve_r, pre_ve))
                    e.matmul(pfB[2][:, 0:512], sb_bc4[:, pb, :], arows[0:4, 1, :],
                             start=True, stop=True)
                    ctr.inc(e.matmul(pfB[3][:, 0:512], sb_bc4[:, pb, :], arows[0:4, 2, :],
                                     start=True, stop=True), "MM")
                mm_b = ctr.val("MM")

                @WV
                def _(e, pb=pb, mm_b=mm_b, p=p):
                    e.wait_ge(ctr.h["MM"], mm_b)
                    e.tensor_copy(rbb[:, pb, :], pfB[2][:, 0:512])
                    ctr.inc(e.tensor_copy(rgbb[:, p, pb, :], pfB[3][:, 0:512]), "VE")

            @WV
            def _(e, p=p):
                for pb in range(2):
                    ctr.inc(e.tensor_mul(khat[:, p, pb, :], ksil[:, p, pb, :], rbb[:, pb, :]), "VE")
            ve_kh = ctr.val("VE")

            yield

            @WV
            def _(e, sc_gk=sc_gk, ve_kh=ve_kh, pre_sc=ctr.val("SC"),
                  pre_mm=ctr.val("MM"), p=p):
                e.wait_ge(ctr.h["SC"], max(sc_gk, pre_sc))
                e.wait_ge(ctr.h["VE"], ve_kh)
                e.wait_ge(ctr.h["MM"], pre_mm)
                for cc in range(CPS):
                    ch = slice(cc * C, (cc + 1) * C)
                    for pb in range(2):
                        e.tensor_tensor_scan(Gw[:, pb, ch], gkb[:, p, pb, ch], gkb[:, p, pb, ch],
                                             0.0, AL.add, AL.bypass)
                for cc in range(CPS):
                    for pb in range(2):
                        gm = Gw[:, pb, cc * C + 63:cc * C + 64]
                        gc_ = Gw[:, pb, cc * C + 127:cc * C + 128]
                        e.tensor_scalar(Gsc[:, cc, pb, 0:1], gm, -1.0, -SHIFT,
                                        AL.mult, AL.add)
                        e.tensor_copy(Gsc[:, cc, pb, 1:2], gc_)
                        e.tensor_add(Gsc[:, cc, pb, 3:4], gc_, Gsc[:, cc, pb, 0:1])
                        e.tensor_scalar_add(E1[:, pb, cc * C:(cc + 1) * C],
                                            Gw[:, pb, cc * C:(cc + 1) * C],
                                            Gsc[:, cc, pb, 0:1])
                        e.tensor_scalar(Gsc[:, cc, pb, 5:6], gm, 1.0, -SHIFT,
                                        AL.mult, AL.add)
                        e.tensor_scalar(E2[:, pb, cc * C:(cc + 1) * C],
                                        Gw[:, pb, cc * C:(cc + 1) * C],
                                        -1.0, Gsc[:, cc, pb, 5:6],
                                        AL.mult, AL.add)
                        e.tensor_scalar(E4[:, pb, cc * C:(cc + 1) * C],
                                        Gw[:, pb, cc * C:(cc + 1) * C],
                                        -1.0, gc_,
                                        AL.mult, AL.add)
                ctr.inc(e.engine_nop() if False else e.tensor_copy(
                    Gsc[:, 0, 0, 5:6], Gsc[:, 0, 0, 0:1]), "VE")
            ve_g = ctr.val("VE")

            @WS
            def _(e, ve_g=ve_g):
                e.wait_ge(ctr.h["VE"], ve_g)
                for pb in range(2):
                    e.activation(E1[:, pb, :], E1[:, pb, :], AF.Exp)
                    e.activation(Gw[:, pb, :], Gw[:, pb, :], AF.Exp)
                    e.activation(E2[:, pb, :], E2[:, pb, :], AF.Exp)
                    e.activation(E4[:, pb, :], E4[:, pb, :], AF.Exp)
                for cc in range(CPS):
                    for pb in range(2):
                        e.activation(Gsc[:, cc, pb, 2:3], Gsc[:, cc, pb, 0:1],
                                     AF.Exp, scale=-1.0)
                        mi = e.activation(Gsc[:, cc, pb, 4:5], Gsc[:, cc, pb, 3:4], AF.Exp)
                ctr.inc(mi, "SC")
            sc_e = ctr.val("SC")

            @WV
            def _(e, sc_e=sc_e, pre_mm=ctr.val("MM"), p=p):
                e.wait_ge(ctr.h["SC"], sc_e)
                e.wait_ge(ctr.h["MM"], pre_mm)
                for cc in range(CPS):
                    for pb in range(2):
                        e.tensor_mul(egc[:, cc, pb, :], Gsc[:, cc, pb, 2:3],
                                     Gsc[:, cc, pb, 4:5])
                e.tensor_mul(Qmid[:], qsil[:, p], E1[:])
                e.tensor_mul(Qful[:], qsil[:, p], Gw[:])
                e.tensor_mul(Kdot[:], khat[:, p], E2[:])
                e.tensor_mul(Khb[:], khat[:, p], E4[:])
                e.tensor_mul(tE[:], sgk[:, p], E1[:])
                e.tensor_mul(tE[:], tE[:], rgbb[:, p])
                e.tensor_mul(Adot[:], tE[:], ksil[:, p])
                for cc in range(CPS):
                    for pb in range(2):
                        mi = e.tensor_scalar_mul(Akm[:, pb, cc * C:(cc + 1) * C],
                                                 Adot[:, pb, cc * C:(cc + 1) * C],
                                                 Gsc[:, cc, pb, 2:3])
                ctr.inc(mi, "VE")
            ve_p = ctr.val("VE")

            @blk.tensor
            def _(e, ve_p=ve_p, pre_sc=ctr.val("SC"), p=p):
                e.wait_ge(ctr.h["VE"], ve_p)
                e.wait_ge(ctr.h["SC"], pre_sc)
                for cc in range(CPS):
                    ch = slice(cc * C, (cc + 1) * C)
                    for hh in range(4):
                        pb, hp = hh // 2, hh % 2
                        rg = 64 * hp
                        tp = (rg, 0) if rg else None
                        e.matmul(pmB[hh][:, 128 * cc:128 * cc + 64],
                                 vbf[rg:rg + 64, p, pb, ch],
                                 sb_idpair[rg:rg + 64, :], start=True, stop=False,
                                 tile_position=tp)
                        mi = e.matmul(pmB[hh][:, 128 * cc + 64:128 * cc + 128],
                                      Akm[rg:rg + 64, pb, ch],
                                      sb_idpair[rg:rg + 64, :], start=False, stop=True,
                                      tile_position=tp)
                ctr.inc(mi, "MM")
            mm_B = ctr.val("MM")

            @WV
            def _(e, mm_B=mm_B):
                e.wait_ge(ctr.h["MM"], mm_B)
                for hh in range(4):
                    vi = e.tensor_copy(Xb[:, hh, :, :], pmB[hh][:, 0:512])
                ctr.inc(vi, "SC")
            sc_B = ctr.val("SC")

            @blk.tensor
            def _(e, sc_B=sc_B):
                e.wait_ge(ctr.h["SC"], sc_B)
                for cc in range(CPS):
                    ch = slice(cc * C, (cc + 1) * C)
                    for hh in range(4):
                        pb, hp = hh // 2, hh % 2
                        rg = 64 * hp
                        tp = (rg, 0) if rg else None
                        ctr.inc(e.matmul(pmB[hh][:, 128 * cc:128 * (cc + 1)],
                                         Adot[rg:rg + 64, pb, ch],
                                         Kdot[rg:rg + 64, pb, ch], start=True, stop=True,
                                         tile_position=tp), "MM")
            mm_M = ctr.val("MM")

            @WV
            def _(e, mm_M=mm_M):
                e.wait_ge(ctr.h["MM"], mm_M)
                for hh in range(4):
                    e.select(PbA[:, hh, :, :], sb_maskl[:], pmB[hh][:, 0:512],
                             sb_zeros[:], add_drain=True)
                ctr.inc(e.tensor_scalar_mul(PbA[:], PbA[:], UNSCALE), "VE")
            ve_N = ctr.val("VE")

            @blk.tensor
            def _(e, ve_N=ve_N):
                e.wait_ge(ctr.h["VE"], ve_N)
                for hh in range(4):
                    for cc in range(CPS):
                        mi = e.matmul(pful[:, 512 * hh + 128 * cc:512 * hh + 128 * cc + 128],
                                      PbA[:, hh, cc, :], sb_idbf[:], start=True, stop=True)
                ctr.inc(mi, "MM")
            mm_NT = ctr.val("MM")

            @WS
            def _(e, mm_NT=mm_NT):
                e.wait_ge(ctr.h["MM"], mm_NT)
                ctr.inc(e.copy(PtA[:], pful[:, 0:2048]), "SC")
            sc_NT = ctr.val("SC")

            curP, curPt, nxtP, nxtPt = PbA, PtA, PbB, PtB
            prev_sc = sc_NT
            for lvl in range(1, 7):
                @blk.tensor
                def _(e, curP=curP, curPt=curPt, prev_sc=prev_sc,
                      pre_ve=ctr.val("VE"), lvl=lvl):
                    e.wait_ge(ctr.h["SC"], prev_sc)
                    e.wait_ge(ctr.h["VE"], pre_ve)
                    for hh in range(4):
                        for cc in range(CPS):
                            sl = pful[:, 512 * hh + 128 * cc:512 * hh + 128 * cc + 128]
                            e.matmul(sl, sb_idbf[:], Xb[:, hh, cc, :],
                                     start=True, stop=False)
                            mi = e.matmul(sl, curPt[:, hh, cc, :], Xb[:, hh, cc, :],
                                          start=False, stop=True)
                    ctr.inc(mi, "MM")
                    if lvl < 6:
                        for hh in range(4):
                            for cc in range(CPS):
                                mi = e.matmul(
                                    pmB[hh][:, 128 * cc:128 * (cc + 1)],
                                    curP[:, hh, cc, :], curPt[:, hh, cc, :],
                                    start=True, stop=True)
                        ctr.inc(mi, "MM")
                mm_ap = ctr.val("MM")

                @WV
                def _(e, mm_ap=mm_ap, lvl=lvl):
                    e.wait_ge(ctr.h["MM"], mm_ap)
                    ctr.inc(e.tensor_copy(Xb[:], pful[:, 0:2048]), "VE")
                ve_ap = ctr.val("VE")

                if lvl < 6:
                    @WS
                    def _(e, nxtPt=nxtPt, mm_ap=mm_ap):
                        e.wait_ge(ctr.h["MM"], mm_ap)
                        pv = pmh[:].rearrange("p (h x) -> p h x", x=512)
                        ctr.inc(e.copy(nxtPt[:], pv), "SC")
                    sc_pt = ctr.val("SC")

                    if lvl < 5:
                        @blk.tensor
                        def _(e, curPt=curPt, curP=curP, ve_ap=ve_ap):
                            e.wait_ge(ctr.h["VE"], ve_ap)
                            for hh in range(4):
                                for cc in range(CPS):
                                    mi = e.matmul(
                                        pful[:, 512 * hh + 128 * cc:512 * hh + 128 * cc + 128],
                                        curPt[:, hh, cc, :], curP[:, hh, cc, :],
                                        start=True, stop=True)
                            ctr.inc(mi, "MM")
                        mm_bp = ctr.val("MM")

                        @WS
                        def _(e, nxtP=nxtP, mm_bp=mm_bp):
                            e.wait_ge(ctr.h["MM"], mm_bp)
                            ctr.inc(e.copy(nxtP[:], pful[:, 0:2048]), "SC")
                    prev_sc = ctr.val("SC")
                    curP, curPt, nxtP, nxtPt = nxtP, nxtPt, curP, curPt

            @blk.tensor
            def _(e, pre_ve=ctr.val("VE"), pre_sc=ctr.val("SC")):
                e.wait_ge(ctr.h["VE"], pre_ve)
                e.wait_ge(ctr.h["SC"], pre_sc)
                for cc in range(CPS):
                    ch = slice(cc * C, (cc + 1) * C)
                    for hh in range(4):
                        pb, hp = hh // 2, hh % 2
                        rg = 64 * hp
                        tp = (rg, 0) if rg else None
                        ctr.inc(e.matmul(pmB[hh][:, 128 * cc:128 * (cc + 1)],
                                         Kdot[rg:rg + 64, pb, ch],
                                         Qmid[rg:rg + 64, pb, ch], start=True, stop=True,
                                         tile_position=tp), "MM")
            mm_QK = ctr.val("MM")

            @WV
            def _(e, mm_QK=mm_QK):
                e.wait_ge(ctr.h["MM"], mm_QK)
                for hh in range(4):
                    e.select(POb[:, hh, :, :], sb_masku[:], pmB[hh][:, 0:512],
                             sb_zeros[:], add_drain=True)
                ctr.inc(e.tensor_scalar_mul(POb[:], POb[:], UNSCALE), "VE")
            ve_PO = ctr.val("VE")

            @blk.tensor
            def _(e, ve_PO=ve_PO):
                e.wait_ge(ctr.h["VE"], ve_PO)
                for cc in range(CPS):
                    ch = slice(cc * C, (cc + 1) * C)
                    for hh in range(4):
                        pb, hp = hh // 2, hh % 2
                        rg = 64 * hp
                        tp = (rg, 0) if rg else None
                        ctr.inc(e.matmul(pmB[hh][:, 64 * cc:64 * cc + 64],
                                         Khb[rg:rg + 64, pb, ch],
                                         sb_idpair[rg:rg + 64, :], start=True, stop=True,
                                         tile_position=tp), "MM")
            mm_KT = ctr.val("MM")

            @WV
            def _(e, mm_KT=mm_KT):
                e.wait_ge(ctr.h["MM"], mm_KT)
                for hh in range(4):
                    vi = e.tensor_copy(KTb[:, hh, :, :], pmB[hh][:, 0:256])
                ctr.inc(vi, "SC")
            sc_KT = ctr.val("SC")

            @blk.tensor
            def _(e, sc_KT=sc_KT, pre_ve=ctr.val("VE")):
                e.wait_ge(ctr.h["SC"], sc_KT)
                e.wait_ge(ctr.h["VE"], pre_ve)
                for cc in range(CPS):
                    for hh in range(4):
                        mi = e.matmul(pful[:, 256 * cc + 64 * hh:256 * cc + 64 * hh + 64],
                                      POb[:, hh, cc, :], Xb[:, hh, cc, 0:64],
                                      start=True, stop=True)
                ctr.inc(mi, "MM")
                for cc in range(CPS):
                    for hh in range(4):
                        q, hp = hh // 2, hh % 2
                        mi = e.matmul(
                            pful[64 * hp:64 * hp + 64,
                                 1024 + 256 * cc + 128 * q:1024 + 256 * cc + 128 * q + 128],
                            Xb[:, hh, cc, 64:128], POb[:, hh, cc, :],
                            start=True, stop=True, tile_position=(0, 64 * hp))
                ctr.inc(mi, "MM")
                for cc in range(CPS):
                    for hh in range(4):
                        q, hp = hh // 2, hh % 2
                        mi = e.matmul(
                            pmB[0][64 * hp:64 * hp + 64, 128 * cc + 64 * q:128 * cc + 64 * q + 64],
                            Xb[:, hh, cc, 64:128], KTb[:, hh, cc, :],
                            start=True, stop=True, tile_position=(0, 64 * hp))
                ctr.inc(mi, "MM")
                for cc in range(CPS):
                    for hh in range(4):
                        q, hp = hh // 2, hh % 2
                        mi = e.matmul(
                            pmB[1][64 * hp:64 * hp + 64, 128 * cc + 64 * q:128 * cc + 64 * q + 64],
                            KTb[:, hh, cc, :], Xb[:, hh, cc, 0:64],
                            start=True, stop=True, tile_position=(0, 64 * hp))
                ctr.inc(mi, "MM")
            mm_post = ctr.val("MM")

            @WV
            def _(e, mm_post=mm_post):
                e.wait_ge(ctr.h["MM"], mm_post)
                e.tensor_copy(OLb[:], pful[:, 0:1024])
                e.tensor_copy(SLb[:], pmB[1][:, 0:512])
                for cc in range(CPS):
                    ch = slice(cc * C, (cc + 1) * C)
                    for hh in range(4):
                        q, hp = hh // 2, hh % 2
                        e.tensor_tensor(
                            QHb[64 * hp:64 * hp + 64, cc, hh, :],
                            Qful[64 * hp:64 * hp + 64, q, ch],
                            pful[64 * hp:64 * hp + 64,
                                 1024 + 256 * cc + 128 * q:1024 + 256 * cc + 128 * q + 128],
                            AL.add)
                for cc in range(CPS):
                    for hh in range(4):
                        q, hp = hh // 2, hh % 2
                        vi = e.scalar_tensor_tensor(
                            WPb[64 * hp:64 * hp + 64, cc, hh, :],
                            sb_id2pair[64 * hp:64 * hp + 64, 0, :],
                            egc[64 * hp:64 * hp + 64, cc, q, :],
                            pmB[0][64 * hp:64 * hp + 64,
                                   128 * cc + 64 * q:128 * cc + 64 * q + 64],
                            AL.mult, AL.add)
                ctr.inc(vi, "VE")
            ve_chunk = ctr.val("VE")
            pstate["vechunk"][sc_i] = ve_chunk

            yield

            for cc in range(CPS):
                @blk.tensor
                def _(e, cc=cc, pre_ve=ctr.val("VE"), pre_sc=ctr.val("SC")):
                    e.wait_ge(ctr.h["VE"], pre_ve)
                    e.wait_ge(ctr.h["SC"], pre_sc)
                    if cc > 0:
                        for hh in range(4):
                            mi = e.matmul(pmh[0:64, 1024 + 128 * hh:1024 + 128 * (hh + 1)],
                                          Otm[:, hh, :], sb_idf32[:],
                                          start=True, stop=True)
                    for hh in range(4):
                        mi = e.matmul(pmh[:, 1536 + 64 * hh:1536 + 64 * (hh + 1)],
                                      QHb[:, cc, hh, :],
                                      Sbf[:, hh // 2, :], start=True, stop=True)
                    for hh in range(4):
                        q, hp = hh // 2, hh % 2
                        mi = e.matmul(pmh[64 * hp:64 * hp + 64,
                                          1792 + 64 * q:1792 + 64 * q + 64],
                                      WPb[:, cc, hh, :], Sbf[:, q, :],
                                      start=True, stop=True, tile_position=(0, 64 * hp))
                    ctr.inc(mi, "MM")
                mm_sw = ctr.val("MM")

                @WV
                def _(e, cc=cc, mm_sw=mm_sw):
                    e.wait_ge(ctr.h["MM"], mm_sw)
                    if cc > 0:
                        e.tensor_copy(Ovm[:, :, (cc - 1) * C:cc * C],
                                      pmh[0:64, 1024:1536].rearrange("p (h x) -> p h x", x=128))
                    e.tensor_tensor(Otm[:],
                                    pmh[:, 1536:1792].rearrange("p (h x) -> p h x", x=64),
                                    OLb[:, cc, :, :], AL.add)
                    e.tensor_tensor(Sst[:],
                                    pmh[:, 1792:1920].rearrange("p (q x) -> p q x", x=64),
                                    SLb[:, cc, :, :], AL.add)
                    ctr.inc(e.tensor_copy(Sbf[:], Sst[:]), "VE")

            @blk.tensor
            def _(e, pre_ve=ctr.val("VE")):
                e.wait_ge(ctr.h["VE"], pre_ve)
                for hh in range(4):
                    mi = e.matmul(pmh[0:64, 1024 + 128 * hh:1024 + 128 * (hh + 1)],
                                  Otm[:, hh, :], sb_idf32[:], start=True, stop=True)
                ctr.inc(mi, "MM")
            mm_ot = ctr.val("MM")

            @WV
            def _(e, mm_ot=mm_ot):
                e.wait_ge(ctr.h["MM"], mm_ot)
                ctr.inc(e.tensor_copy(Ovm[:, :, (CPS - 1) * C:CPS * C],
                                      pmh[0:64, 1024:1536].rearrange("p (h x) -> p h x", x=128)),
                        "SC")

            sc_o = ctr.val("SC")
            dsh = ts // 2048
            pre_s2m = ctr.val("S2")

            @WV
            def _(e, sc_o=sc_o, pre_s2m=pre_s2m):
                e.wait_ge(ctr.h["SC"], sc_o)
                e.wait_ge(ctr.h["S2"], pre_s2m)
                e.tensor_scalar_mul(Ovm_lo[:], Ovm[:], sb_msel[:, 0:1])
                ctr.inc(e.tensor_scalar_mul(Ovm_hi[:], Ovm[:], sb_msel[:, 1:2]), "VE")
            ve_ovm = ctr.val("VE")

            @WS
            def _(e, ve_ovm=ve_ovm, dsh=dsh, tl2=ts % 2048):
                e.wait_ge(ctr.h["VE"], ve_ovm)
                ctr.wait(e, "S2")
                for hh in range(4):
                    ctr.inc(e.dma_start(
                        out=a2a_in.ap()[dsh, 64 * hh:64 * hh + 64, tl2:tl2 + SUPER],
                        in_=Ovm_lo[:, hh, :]), "S2", 16)
                    ctr.inc(e.dma_start(
                        out=a2a_in.ap()[dsh + 4, 64 * hh:64 * hh + 64, tl2:tl2 + SUPER],
                        in_=Ovm_hi[:, hh, :]), "S2", 16)

        gens = [sc_body(i) for i in range(NSUPER)]
        next(gens[0])
        for i in range(NSUPER):
            next(gens[i])
            if i + 1 < NSUPER:
                next(gens[i + 1])
            for _ in gens[i]:
                pass

        free_to(mark0)

    def post_attn():
        mark0 = len(_guards)
        wo_s = alloc("wo_s", (128, KB, KB, 128), BF16)
        wog1_s = alloc("wog1_s", (128, KB, 64), BF16)
        wog2_s = alloc("wog2_s", (64, KB, 128), BF16)
        o_t = alloc("o_t", (128, KB, 512))
        og_t = alloc("og_t", (128, KB, 512))
        oln = alloc("oln", (128, KB, 512), BF16)
        xlt = alloc("xlt", (128, KB, 512), BF16)
        zog1 = alloc("zog1", (64, 512), BF16)
        scr4 = alloc("scr4", (128, 4, 512))
        x1t = alloc("x1t", (128, KB, 512))
        x2t = alloc("x2t", (128, KB, 512))

        @WG
        def _(e):
            ctr.wait(e, "CC")
            ctr.inc(e.dma_start(out=wo_s[:],
                                in_=wo_d.rearrange("(k p) (o n) -> p k o n", p=128, n=128)),
                    "LW", 16)
            ctr.inc(e.dma_start(out=wog1_s[:],
                                in_=wog1_d.rearrange("(k p) n -> p k n", p=128)), "LW", 16)
            ctr.inc(e.dma_start(out=wog2_s[:],
                                in_=wog2_d.rearrange("k (o n) -> k o n", n=128)), "LW", 16)
        lw_p4 = ctr.val("LW")

        for tt in range(4):
            t0 = tt * 512
            markt = len(_guards)

            pre = {nm: ctr.val(nm) for nm in ["MM", "VE", "SC", "GP", "ST"]}

            @blk.sync
            def _(e, t0=t0, pre=dict(pre)):
                ctr.wait(e, "LD")
                ctr.wait(e, "CC")
                for nm in ["MM", "VE", "SC"]:
                    e.wait_ge(ctr.h[nm], pre[nm])
                for kb in range(KB):
                    ctr.inc(e.dma_start(
                        out=o_t[:, kb, :],
                        in_=a2a_out.ap()[kb // 2, 128 * (kb % 2):128 * (kb % 2) + 128,
                                         t0:t0 + 512]), "LD", 16)
                for kb in range(KB):
                    ctr.inc(e.dma_start(
                        out=xlt[:, kb, :],
                        in_=ag_in.ap()[kb * 128:(kb + 1) * 128, t0:t0 + 512]), "LD", 16)
            ld4 = ctr.val("LD")

            @WG
            def _(e, t0=t0, ld4=ld4):
                e.wait_ge(ctr.h["LD"], ld4)
                ctr.wait(e, "LW")
                for kb in range(KB):
                    ctr.inc(e.dma_start(
                        out=o_t[:, kb, :],
                        in_=a2a_out.ap()[4 + kb // 2, 128 * (kb % 2):128 * (kb % 2) + 128,
                                         t0:t0 + 512], accum_op=AL.add), "LW", 16)
            lw4 = ctr.val("LW")

            p4a = pal("p4a" + str(tt), (128, 2048))
            p4B = [p4a[:, i * 512:(i + 1) * 512] for i in range(4)]

            @blk.tensor
            def _(e, ld4=ld4):
                e.wait_ge(ctr.h["LD"], ld4)
                e.wait_ge(ctr.h["LW"], lw_p4)
                for kb in range(KB):
                    mi = e.matmul(p4B[0][0:64, 0:512], wog1_s[:, kb, :], xlt[:, kb, :],
                                  start=(kb == 0), stop=(kb == KB - 1))
                ctr.inc(mi, "MM")
            mm_g1 = ctr.val("MM")

            @WV
            def _(e, mm_g1=mm_g1):
                e.wait_ge(ctr.h["MM"], mm_g1)
                ctr.inc(e.tensor_copy(zog1[:], p4B[0][0:64, 0:512]), "SC")
            sc_g1 = ctr.val("SC")

            @blk.tensor
            def _(e, sc_g1=sc_g1):
                e.wait_ge(ctr.h["SC"], sc_g1)
                for ob in range(KB):
                    if ob >= 3:
                        e.wait_ge(ctr.h["SC"], sc_g1 + ob - 2)
                    ctr.inc(e.matmul(p4B[ob % 4][:, 0:512], wog2_s[:, ob, :], zog1[:],
                                     start=True, stop=True), "MM")
            mm_g2 = ctr.val("MM")

            @WS
            def _(e, mm_g2=mm_g2):
                for ob in range(KB):
                    e.wait_ge(ctr.h["MM"], mm_g2 - KB + ob + 1)
                    ctr.inc(e.activation(og_t[:, ob, :], p4B[ob % 4][:, 0:512],
                                         AF.Sigmoid), "SC")
            sc_og = ctr.val("SC")

            @WV
            def _(e, sc_og=sc_og, ld4=ld4, lw4=lw4):
                e.wait_ge(ctr.h["SC"], sc_og)
                e.wait_ge(ctr.h["LD"], ld4)
                e.wait_ge(ctr.h["LW"], lw4)
                for kb in range(KB):
                    ctr.inc(e.tensor_mul(o_t[:, kb, :], o_t[:, kb, :], og_t[:, kb, :]), "VE")
            free_to(markt)

            p_st = pal("pst4" + str(tt), (128, 512))
            p_sq = pal("psq4" + str(tt), (128, 512))
            p_bc = pal("pbc4" + str(tt), (128, 1024))
            ln_stage([o_t[:, kb, :] for kb in range(KB)], 512,
                     [oln[:, kb, :] for kb in range(KB)], scr4, p_st, p_sq, p_bc,
                     nw=sb_normw)
            free_to(markt)

            p4w = pal("p4w" + str(tt), (128, 1024))
            mm4 = ctr.val("MM")
            ve4 = ctr.val("VE")
            ld4b = ctr.val("LD")
            st4 = ctr.val("ST")

            @blk.sync
            def _(e, t0=t0, ve4=ve4):
                ctr.wait(e, "LD")
                e.wait_ge(ctr.h["VE"], ve4)
                for ob in range(KB):
                    ctr.inc(e.dma_start(out=x1t[:, ob, :],
                                        in_=x1_sp[ob * 128:(ob + 1) * 128, t0:t0 + 512]),
                            "LD", 16)
            ld_x1 = ctr.val("LD")

            @blk.tensor
            def _(e, mm4=mm4, ve4=ve4, pre_ve=ctr.val("VE")):
                e.wait_ge(ctr.h["VE"], pre_ve)
                for ob in range(KB):
                    if ob >= 2:
                        e.wait_ge(ctr.h["VE"], ve4 + ob - 1)
                    bank = p4w[:, (ob % 2) * 512:(ob % 2) * 512 + 512]
                    for kb in range(KB):
                        mi = e.matmul(bank, wo_s[:, kb, ob, :], oln[:, kb, :],
                                      start=(kb == 0), stop=(kb == KB - 1))
                    ctr.inc(mi, "MM")

            @WV
            def _(e, mm4=mm4, ld_x1=ld_x1, st4=st4):
                e.wait_ge(ctr.h["LD"], ld_x1)
                e.wait_ge(ctr.h["ST"], st4)
                for ob in range(KB):
                    e.wait_ge(ctr.h["MM"], mm4 + ob + 1)
                    ctr.inc(e.tensor_add(x2t[:, ob, :],
                                         p4w[:, (ob % 2) * 512:(ob % 2) * 512 + 512],
                                         x1t[:, ob, :]), "VE")
            ve_x2 = ctr.val("VE")

            dst4 = yT if _STAGE == 2 else x2_sp

            @blk.sync
            def _(e, t0=t0, ve_x2=ve_x2, dst4=dst4):
                ctr.wait(e, "ST")
                for ob in range(KB):
                    e.wait_ge(ctr.h["VE"], ve_x2 - KB + ob + 1)
                    ctr.inc(e.dma_start(out=dst4[ob * 128:(ob + 1) * 128, t0:t0 + 512],
                                        in_=x2t[:, ob, :]), "ST", 16)
            free_to(markt)

        free_to(mark0)

    ffn_pass(xsT, x1_sp, make_ln_out=True, final_store=(_STAGE == 1), uid="f1")
    barrier()

    if _STAGE >= 2:
        @WG
        def _(e):
            ctr.wait(e, "ST")
            ctr.inc(e.collective_compute(
                "AllGather", AL.bypass,
                replica_groups=[[0, 1, 2, 3], [4, 5, 6, 7]],
                ins=[ag_in.ap().opt()], outs=[xg.ap().opt()]), "CC")
        attn()
        barrier()

        @WG
        def _(e):
            ctr.wait(e, "S2")
            with allow_ici_all_to_all():
                ctr.inc(e.collective_compute(
                    "AllToAll", AL.bypass,
                    replica_groups=[[0, 1, 2, 3, 4, 5, 6, 7]],
                    ins=[a2a_in.ap().opt()], outs=[a2a_out.ap().opt()]), "CC")
        post_attn()
        barrier()

    if _STAGE >= 3:
        ffn_pass(x2_sp, None, make_ln_out=False, final_store=True, uid="f2")

    @blk.sync
    def _(e):
        ctr.wait(e, "ST")

    blk_guard.__exit__(None, None, None)
    free_to(0)
    return nc


def _prep_inputs(inputs):
    f32 = np.float32
    x = np.ascontiguousarray(inputs["L0_attn_x"], dtype=f32).reshape(TOK, D)
    wq = inputs["L0_attn_Wq"]; wk = inputs["L0_attn_Wk"]; wv = inputs["L0_attn_Wv"]
    wgam = inputs["L0_attn_Wgamma"]; wf1 = inputs["L0_attn_Wf1"]; wf2 = inputs["L0_attn_Wf2"]
    wog1 = inputs["L0_attn_Wog1"]; wog2 = inputs["L0_attn_Wog2"]
    normw = inputs["L0_attn_norm_w"]; wo = inputs["L0_attn_Wo"]

    bf = ml_dtypes.bfloat16
    consts = {
        "c_idbf": np.eye(128, dtype=f32).astype(bf),
        "c_idf32": np.eye(128, dtype=f32),
        "c_idpair": np.concatenate([np.eye(64, dtype=f32)] * 2, axis=0).astype(bf),
        "c_id2pair": np.stack([np.concatenate([np.eye(64, dtype=f32)] * 2, axis=0)] * 2,
                              axis=1),
        "c_onescol": np.ones((128, 1), f32),
        "c_onesrow": np.ones((1, 128), f32),
        "c_sel4": np.zeros((128, 2, 4), f32),
        "c_bc4": np.zeros((4, 2, 128), f32),
        "c_maskl4": np.broadcast_to(np.tril(np.ones((128, 128), np.uint8), -1)[:, None, :],
                                    (128, 4, 128)).copy(),
        "c_masku4": np.broadcast_to(np.tril(np.ones((128, 128), np.uint8), 0).T[:, None, :],
                                    (128, 4, 128)).copy(),
    }
    for hh in range(4):
        pb, hp = hh // 2, hh % 2
        consts["c_sel4"][64 * hp:64 * hp + 64, pb, hh] = 1.0
        consts["c_bc4"][hh, pb, 64 * hp:64 * hp + 64] = 1.0

    shared = {
        "wfg": np.ascontiguousarray(
            np.concatenate([wf1, wgam], axis=1), dtype=f32),
        "wog1": np.ascontiguousarray(wog1, f32),
        "wog2": np.ascontiguousarray(wog2, f32),
        "wo": np.ascontiguousarray(wo, f32),
        "normw": np.ascontiguousarray(np.asarray(normw, f32).reshape(KB, 128).T),
        "w1t": np.ascontiguousarray(inputs["L0_ffn_w1_w"].T, f32),
        "w2t": np.ascontiguousarray(inputs["L0_ffn_w2_w"].T, f32),
        "h1t": np.ascontiguousarray(inputs["L0_ffn_h1_w"].T, f32),
        "h2t": np.ascontiguousarray(inputs["L0_ffn_h2_w"].T, f32),
        "w3t": np.ascontiguousarray(inputs["L0_ffn_w3_w"].T, f32),
        "s1": np.ascontiguousarray(np.asarray(inputs["L0_ffn_w1_s"], f32).reshape(HB, 128).T),
        "s2": np.ascontiguousarray(np.asarray(inputs["L0_ffn_w2_s"], f32).reshape(HB, 128).T),
        "sh1": np.ascontiguousarray(np.asarray(inputs["L0_ffn_h1_s"], f32).reshape(HB, 128).T),
        "sh2": np.ascontiguousarray(np.asarray(inputs["L0_ffn_h2_s"], f32).reshape(HB, 128).T),
        "s3": np.ascontiguousarray(np.asarray(inputs["L0_ffn_w3_s"], f32).reshape(KB, 128).T),
    }
    shared.update(consts)

    in_maps = []
    for c in range(NCORES):
        hsel = slice(256 * (c % 4), 256 * (c % 4) + 256)
        gsel = slice(4 * (c % 4), 4 * (c % 4) + 4)
        m = dict(shared)
        m["x_slabT"] = np.ascontiguousarray(x[c * SLAB:(c + 1) * SLAB].T, f32)
        m["wq"] = np.ascontiguousarray(wq[:, hsel], f32)
        m["wk"] = np.ascontiguousarray(wk[:, hsel], f32)
        m["wv"] = np.ascontiguousarray(wv[:, hsel], f32)
        m["wf2"] = np.ascontiguousarray(wf2[:, hsel], f32)
        m["wfg"] = np.ascontiguousarray(
            np.concatenate([wf1, wgam[:, gsel]], axis=1), f32)
        msel = np.zeros((64, 2), f32)
        msel[:, 0 if c < 4 else 1] = 1.0
        m["c_msel"] = msel
        in_maps.append(m)
    return in_maps


def kernel(**inputs):
    in_maps = _prep_inputs(inputs)
    nc = build_program()
    res = run_bass_kernel_spmd(nc, in_maps, core_ids=list(range(NCORES)))
    out = np.concatenate([np.asarray(r["y_slabT"]).T for r in res.results], axis=0)
    return np.ascontiguousarray(out.reshape(B, N, D), dtype=np.float32)


# version 08
# speedup vs baseline: 1.0791x; 1.0488x over previous
import numpy as np
import ml_dtypes
import concourse.bass as bass
import concourse.mybir as mybir
from concourse.bass_utils import run_bass_kernel_spmd
from concourse.aot_env import allow_ici_all_to_all

F32 = mybir.dt.float32
BF16 = mybir.dt.bfloat16
U8 = mybir.dt.uint8
AF = mybir.ActivationFunctionType
AL = mybir.AluOpType

NCORES = 8
B, N, D, H, HD = 2, 8192, 1024, 16, 64
TOK = B * N
SLAB = TOK // NCORES
C = 128
NCHUNK = N // C
SUPER = 512
NSUPER = N // SUPER
CPS = SUPER // C
DHID = 4096
KB = D // 128
HB = DHID // 128
HALF = 1024

_STAGE = 3
_SUB = None
SHIFT = 32 * float(np.log(2.0))
UNSCALE = 2.0 ** 64


class Ctr:
    def __init__(self, nc):
        self.nc = nc
        self.h = {}
        self.v = {}

    def new(self, name):
        self.h[name] = self.nc.alloc_semaphore(name=name)
        self.v[name] = 0

    def inc(self, inst, name, n=1):
        inst.then_inc(self.h[name], n)
        for st in _DW._state.values():
            if st.get("last") is inst:
                st["pending_inc"] = True
        self.v[name] += n
        return self.v[name]

    def val(self, name):
        return self.v[name]

    def wait(self, eng, name, val=None):
        eng.wait_ge(self.h[name], self.v[name] if val is None else val)



_DW_OPS = {
    "tensor_tensor", "tensor_add", "tensor_sub", "tensor_mul", "tensor_max",
    "tensor_scalar", "tensor_single_scalar", "tensor_scalar_add",
    "tensor_scalar_sub", "tensor_scalar_mul", "tensor_scalar_max",
    "tensor_scalar_min", "tensor_copy", "tensor_relu", "tensor_reduce",
    "reduce_max", "reduce_sum", "scalar_tensor_tensor", "tensor_tensor_scan",
    "select", "reciprocal", "memset", "memzero", "activation", "copy", "mul",
    "add", "sign", "square", "sqrt", "bn_stats", "bn_aggr", "transpose",
    "stream_shuffle", "copy_predicated",
}


def _ap_span(ap):
    try:
        t = ap.tensor
        name = t.name
        row = 1
        for d in t.shape[1:]:
            row *= int(d)
        off = int(ap.offset)
        pairs = [(abs(int(s)), int(c)) for s, c in ap.ap]
    except Exception:
        return None
    if pairs and pairs[0][0] == row and row > 1:
        pcnt = pairs[0][1]
        fpairs = pairs[1:]
    else:
        pcnt = None
        fpairs = pairs
    fspan = 1
    for s, c in fpairs:
        fspan += s * (c - 1)
    p_lo = off // row
    f_lo = off % row
    if pcnt is None or f_lo + fspan > row:
        return (name, 0, 1 << 20, 0, 1 << 40)
    return (name, p_lo, p_lo + pcnt, f_lo, f_lo + fspan)


class _DW:

    _state = {}

    def __init__(self, e):
        object.__setattr__(self, "_e", e)
        key = getattr(e, "engine", None)
        key = str(key) if key is not None else id(type(e))
        st = _DW._state.setdefault(key, {"W": [], "R": []})
        object.__setattr__(self, "_st", st)

    @staticmethod
    def _overlap(iv, lst):
        if iv is None:
            return True
        name, pl, ph, fl, fh = iv
        for n2, pl2, ph2, fl2, fh2 in lst:
            if n2 == name and pl < ph2 and pl2 < ph and fl < fh2 and fl2 < fh:
                return True
        return False

    def __getattr__(self, name):
        a = getattr(self._e, name)
        if name == "drain":
            st = self._st

            def dr(_a=a):
                st["W"] = []
                st["R"] = []
                st["pending_inc"] = False
                return _a()
            return dr
        if name not in _DW_OPS:
            return a
        st = self._st

        def call(*args, _a=a, **kw):
            from concourse.ap import AP as _AP
            aps = [x for x in args if isinstance(x, _AP)]
            aps += [v for v in kw.values() if isinstance(v, _AP)]
            outs = aps[:1]
            ins = aps[1:]
            need = st.get("pending_inc", False)
            if not need:
                for x in ins:
                    if _DW._overlap(_ap_span(x), st["W"]):
                        need = True
                        break
            if not need:
                for x in outs:
                    iv = _ap_span(x)
                    if _DW._overlap(iv, st["W"]) or _DW._overlap(iv, st["R"]):
                        need = True
                        break
            if need or len(st["W"]) > 64:
                self._e.drain()
                st["W"] = []
                st["R"] = []
                st["pending_inc"] = False
            r = _a(*args, **kw)
            for x in outs:
                iv = _ap_span(x)
                st["W"].append(iv if iv else ("?", 0, 1 << 20, 0, 1 << 40))
            for x in ins:
                iv = _ap_span(x)
                st["R"].append(iv if iv else ("?", 0, 1 << 20, 0, 1 << 40))
            st["last"] = r
            return r
        return call


def build_program():
    _DW._state = {}
    nc = bass.Bass("TRN2", target_bir_lowering=False, debug=False)

    def din(name, shape, dtype=F32):
        return nc.dram_tensor(name, list(shape), dtype, kind="ExternalInput").ap()

    xsT = din("x_slabT", (D, SLAB))
    wq_d = din("wq", (D, 256)); wk_d = din("wk", (D, 256)); wv_d = din("wv", (D, 256))
    wfg_d = din("wfg", (D, 68))
    wf2_d = din("wf2", (64, 256))
    wog1_d = din("wog1", (D, 64)); wog2_d = din("wog2", (64, D))
    wo_d = din("wo", (D, D))
    normw_d = din("normw", (128, KB))
    w1t = din("w1t", (D, DHID)); w2t = din("w2t", (D, DHID))
    h1t = din("h1t", (DHID, DHID)); h2t = din("h2t", (DHID, DHID))
    w3t = din("w3t", (DHID, D))
    s1_d = din("s1", (128, HB)); s2_d = din("s2", (128, HB))
    sh1_d = din("sh1", (128, HB)); sh2_d = din("sh2", (128, HB))
    s3_d = din("s3", (128, KB))
    c_idbf = din("c_idbf", (128, 128), BF16)
    c_idf32 = din("c_idf32", (128, 128))
    c_idpair = din("c_idpair", (128, 64), BF16)
    c_id2pair = din("c_id2pair", (128, 2, 64))
    c_onescol = din("c_onescol", (128, 1))
    c_onesrow = din("c_onesrow", (1, 128))
    c_sel4 = din("c_sel4", (128, 2, 4))
    c_bc4 = din("c_bc4", (4, 2, 128))
    c_msel = din("c_msel", (64, 2))
    c_maskl4 = din("c_maskl4", (128, 4, 128), U8)
    c_masku4 = din("c_masku4", (128, 4, 128), U8)

    yT = nc.dram_tensor("y_slabT", [D, SLAB], F32, kind="ExternalOutput").ap()

    x1_sp = nc.dram_tensor("x1_sp", [D, SLAB], F32).ap()
    x2_sp = nc.dram_tensor("x2_sp", [D, SLAB], F32).ap()
    ag_in = nc.dram_tensor("ag_in", [D, SLAB], BF16)
    xg = nc.dram_tensor("xg", [4, D, SLAB], BF16)
    a2a_in = nc.dram_tensor("a2a_in", [8, 256, SLAB], F32)
    a2a_out = nc.dram_tensor("a2a_out", [8, 256, SLAB], F32)

    ctr = Ctr(nc)
    for s in ["LD", "LW", "LWa", "LWb", "ST", "S2", "MM", "VE", "SC", "GP", "CC"]:
        ctr.new(s)

    _guards = []

    def alloc(name, shape, dtype=F32):
        g = nc.sbuf_tensor(name, list(shape), dtype)
        h = g.__enter__()
        _guards.append(g)
        return h

    def pal(name, shape):
        g = nc.psum_tensor(name, list(shape), F32)
        h = g.__enter__()
        _guards.append(g)
        return h

    def free_to(mark):
        while len(_guards) > mark:
            _guards.pop().__exit__(None, None, None)

    sb_idbf = alloc("sb_idbf", (128, 128), BF16)
    sb_idf32 = alloc("sb_idf32", (128, 128))
    sb_idpair = alloc("sb_idpair", (128, 64), BF16)
    sb_id2pair = alloc("sb_id2pair", (128, 2, 64))
    sb_onescol = alloc("sb_onescol", (128, 1))
    sb_onesrow = alloc("sb_onesrow", (1, 128))
    sb_sel4 = alloc("sb_sel4", (128, 2, 4))
    sb_bc4 = alloc("sb_bc4", (4, 2, 128))
    sb_msel = alloc("sb_msel", (64, 2))
    sb_maskl = alloc("sb_maskl", (128, 4, 128), U8)
    sb_masku = alloc("sb_masku", (128, 4, 128), U8)
    sb_zeros = alloc("sb_zeros", (128, 4, 128))
    sb_normw = alloc("sb_normw", (128, KB))
    sb_rows = alloc("sb_rows", (1, 2048))
    sb_scales = alloc("sb_scales", (128, 6, HB))
    PMARK = len(_guards)

    blk_guard = nc.Block()
    blk = blk_guard.__enter__()

    def WV(f):
        blk.vector(lambda e: f(_DW(e)))

    def WS(f):
        blk.scalar(lambda e: f(_DW(e)))

    def WG(f):
        blk.gpsimd(lambda e: f(_DW(e)))


    @blk.sync
    def _(e):
        for dst, src in [(sb_idbf, c_idbf), (sb_idf32, c_idf32), (sb_idpair, c_idpair),
                         (sb_id2pair, c_id2pair), (sb_onescol, c_onescol),
                         (sb_onesrow, c_onesrow), (sb_sel4, c_sel4), (sb_bc4, c_bc4),
                         (sb_maskl, c_maskl4), (sb_masku, c_masku4)]:
            ctr.inc(e.dma_start(out=dst[:], in_=src), "LD", 16)
        ctr.inc(e.dma_start(out=sb_msel[:], in_=c_msel), "LD", 16)
        ctr.inc(e.dma_start(out=sb_normw[:], in_=normw_d), "LD", 16)
        for i, s in zip([0, 1, 3, 4], [s1_d, s2_d, sh1_d, sh2_d]):
            ctr.inc(e.dma_start(out=sb_scales[:, i, :], in_=s), "LD", 16)
        ctr.inc(e.dma_start(out=sb_scales[:, 5, 0:KB], in_=s3_d), "LD", 16)

    @WV
    def _(e):
        ctr.wait(e, "LD")
        e.memset(sb_zeros[:], 0.0)
        e.tensor_mul(sb_scales[:, 2, :], sb_scales[:, 0, :], sb_scales[:, 1, :])
        ctr.inc(e.tensor_scalar_mul(sb_scales[:, 5, 0:KB], sb_scales[:, 5, 0:KB], 0.5), "VE")

    def barrier():
        names = ["LD", "LW", "LWa", "LWb", "ST", "S2", "MM", "VE", "SC", "GP", "CC"]
        for sec in [blk.sync, blk.gpsimd, blk.scalar, blk.vector, blk.tensor]:
            def body(e, _n=tuple(names)):
                for nm in _n:
                    if ctr.val(nm):
                        ctr.wait(e, nm)
            sec(body)

    def ln_stage(xt, twid, out_tiles, scr, p_stat, p_sq, p_bc, nw=None):
        nq = twid // 512
        ld_now = ctr.val("LD")
        ve_now = ctr.val("VE")

        @blk.tensor
        def _(e):
            e.wait_ge(ctr.h["LD"], ld_now)
            e.wait_ge(ctr.h["VE"], ve_now)
            for tq in range(nq):
                for kb in range(KB):
                    mi = e.matmul(p_stat[0:1, tq * 512:(tq + 1) * 512], sb_onescol[:],
                                  xt[kb][:, tq * 512:(tq + 1) * 512],
                                  start=(kb == 0), stop=(kb == KB - 1))
            ctr.inc(mi, "MM")
        mm_sx = ctr.val("MM")

        sc0 = ctr.val("SC")

        @WS
        def _(e):
            e.wait_ge(ctr.h["LD"], ld_now)
            e.wait_ge(ctr.h["VE"], ve_now)
            for kb in range(KB):
                if kb >= 2:
                    e.wait_ge(ctr.h["MM"], mm_sx + kb - 1)
                ctr.inc(e.activation(scr[:, kb % 2, 0:twid], xt[kb], AF.Square), "SC")

        @blk.tensor
        def _(e):
            for kb in range(KB):
                e.wait_ge(ctr.h["SC"], sc0 + kb + 1)
                for tq in range(nq):
                    mi = e.matmul(p_sq[0:1, tq * 512:(tq + 1) * 512], sb_onescol[:],
                                  scr[:, kb % 2, tq * 512:(tq + 1) * 512],
                                  start=(kb == 0), stop=(kb == KB - 1))
                ctr.inc(mi, "MM")
        mm_sq = ctr.val("MM")

        r0 = scr[0:1, 0, 0:twid]
        r1 = scr[0:1, 1, 0:twid]

        @WV
        def _(e):
            e.wait_ge(ctr.h["MM"], mm_sq)
            e.tensor_scalar_mul(sb_rows[0:1, 0:twid], p_stat[0:1, 0:twid], 1.0 / D)
            e.tensor_scalar_mul(p_sq[0:1, 0:twid], p_sq[0:1, 0:twid], 1.0 / D)
            e.tensor_mul(r0, sb_rows[0:1, 0:twid], sb_rows[0:1, 0:twid])
            e.tensor_sub(p_sq[0:1, 0:twid], p_sq[0:1, 0:twid], r0)
            ctr.inc(e.tensor_scalar_add(p_sq[0:1, 0:twid], p_sq[0:1, 0:twid], 1e-5), "VE")
        v_var = ctr.val("VE")

        @WS
        def _(e):
            e.wait_ge(ctr.h["VE"], v_var)
            ctr.inc(e.activation(p_stat[0:1, 0:twid], p_sq[0:1, 0:twid], AF.Sqrt), "SC")
        sc_sq = ctr.val("SC")

        @WV
        def _(e):
            e.wait_ge(ctr.h["SC"], sc_sq)
            e.reciprocal(r1, p_stat[0:1, 0:twid])
            e.tensor_mul(r0, r1, r1)
            e.tensor_mul(r0, p_sq[0:1, 0:twid], r0)
            e.tensor_scalar(r0, r0, -0.5, 1.5, AL.mult, AL.add)
            ctr.inc(e.tensor_mul(sb_rows[0:1, 1024:1024 + twid], r1, r0), "VE")

        for tq in range(nq):
            v_pre = ctr.val("VE")

            @blk.tensor
            def _(e, tq=tq, v_pre=v_pre):
                e.wait_ge(ctr.h["VE"], v_pre)
                e.matmul(p_bc[:, 0:512], sb_onesrow[:],
                         sb_rows[0:1, tq * 512:(tq + 1) * 512], start=True, stop=True)
                ctr.inc(e.matmul(p_bc[:, 512:1024], sb_onesrow[:],
                                 sb_rows[0:1, 1024 + tq * 512:1024 + (tq + 1) * 512],
                                 start=True, stop=True), "MM")
            mm_bc = ctr.val("MM")

            @WV
            def _(e, tq=tq, mm_bc=mm_bc):
                e.wait_ge(ctr.h["MM"], mm_bc)
                sl = slice(tq * 512, (tq + 1) * 512)
                for kb in range(KB):
                    e.tensor_sub(scr[:, 2, 0:512], xt[kb][:, sl], p_bc[:, 0:512])
                    if nw is None:
                        vi = e.tensor_mul(out_tiles[kb][:, sl], scr[:, 2, 0:512],
                                          p_bc[:, 512:1024])
                    else:
                        vi = e.scalar_tensor_tensor(out_tiles[kb][:, sl], scr[:, 2, 0:512],
                                                    nw[:, kb:kb + 1], p_bc[:, 512:1024],
                                                    AL.mult, AL.mult)
                ctr.inc(vi, "VE")

    def ffn_pass(src_ap, spill_ap, make_ln_out, final_store, uid):
        mark0 = len(_guards)
        xln = alloc("xln" + uid, (128, KB, HALF), BF16)
        Hb = alloc("Hb" + uid, (128, HB, HALF), BF16)
        Gb = alloc("Gb" + uid, (128, HB, HALF), BF16)
        wb = alloc("wb" + uid, (128, 2, HB, 128), BF16)
        scr = alloc("scr" + uid, (128, 4, HALF))
        xres = alloc("xres" + uid, (128, 2, 512))

        GbF = Gb[:].bitcast(F32)
        HbF = Hb[:].bitcast(F32)

        for hf in range(2):
            t0 = hf * HALF
            markp = len(_guards)

            p_stat = pal("pstA" + uid + str(hf), (128, 1024))
            p_sq = pal("psqA" + uid + str(hf), (128, 1024))
            p_bc = pal("pbcA" + uid + str(hf), (128, 1024))

            xtiles = [HbF[:, 4 * kb:4 * kb + 4, :].rearrange("p a b -> p (a b)")
                      for kb in range(KB)]
            xtiles = [HbF[:, 2 * kb:2 * kb + 2, :].rearrange("p a b -> p (a b)")
                      for kb in range(KB)]

            @blk.sync
            def _(e):
                for nm in ["MM", "SC", "GP", "VE", "ST", "LD"]:
                    if ctr.val(nm):
                        ctr.wait(e, nm)
                for kb in range(KB):
                    ctr.inc(e.dma_start(out=xtiles[kb],
                                        in_=src_ap[kb * 128:(kb + 1) * 128, t0:t0 + HALF]),
                            "LD", 16)

            ln_stage(xtiles, HALF, [xln[:, kb, :] for kb in range(KB)],
                     scr, p_stat, p_sq, p_bc)
            free_to(markp)
            if _SUB == 'ln':
                break

            p_f = pal("pf" + uid + str(hf), (128, 2048))
            p_f2 = pal("pf2" + uid + str(hf), (128, 2048))
            pbank = [p_f[:, i * 512:(i + 1) * 512] for i in range(4)] + \
                    [p_f2[:, i * 512:(i + 1) * 512] for i in range(4)]

            lwp0 = [ctr.val("LWa"), ctr.val("LWb")]
            mm0 = ctr.val("MM")
            ve_xln = ctr.val("VE")
            ve0 = ctr.val("VE")

            sc0_pre = ctr.val("SC")
            gp_base = ctr.val("GP")

            def _w12_gp_mul(e, ob):
                p = ob % 2
                for tq in range(2):
                    e.wait_ge(ctr.h["SC"], sc0_pre + 4 * ob + 2 * tq + 2)
                    ctr.inc(e.tensor_mul(scr[:, p, tq * 512:(tq + 1) * 512],
                                         scr[:, p, tq * 512:(tq + 1) * 512],
                                         scr[:, 2 + p, tq * 512:(tq + 1) * 512]), "GP")

            @WG
            def _(e):
                e.wait_ge(ctr.h["VE"], ve_xln)
                for ob in range(HB):
                    p = ob % 2
                    sem = "LWa" if p == 0 else "LWb"
                    if ob >= 2:
                        e.wait_ge(ctr.h["MM"], mm0 + (ob - 1) * 32)
                    ctr.inc(e.dma_start(
                        out=wb[:, p, 0:8, :],
                        in_=w1t[:, ob * 128:(ob + 1) * 128].rearrange("(k p) n -> p k n", p=128)),
                        sem, 16)
                    ctr.inc(e.dma_start(
                        out=wb[:, p, 8:16, :],
                        in_=w2t[:, ob * 128:(ob + 1) * 128].rearrange("(k p) n -> p k n", p=128)),
                        sem, 16)
                    if ob >= 2:
                        _w12_gp_mul(e, ob - 2)
                _w12_gp_mul(e, HB - 2)
                _w12_gp_mul(e, HB - 1)

            @blk.tensor
            def _(e):
                e.wait_ge(ctr.h["VE"], ve_xln)
                for ob in range(HB):
                    p = ob % 2
                    sem = "LWa" if p == 0 else "LWb"
                    e.wait_ge(ctr.h[sem], lwp0[p] + 32 * (ob // 2 + 1))
                    if ob >= 2:
                        e.wait_ge(ctr.h["VE"], ve0 + 2 * (ob - 1))
                    for w12 in range(2):
                        for kb in range(KB):
                            for tq in range(2):
                                bank = pbank[4 * p + 2 * w12 + tq]
                                mi = e.matmul(bank[:, 0:512], wb[:, p, 8 * w12 + kb, :],
                                              xln[:, kb, tq * 512:(tq + 1) * 512],
                                              start=(kb == 0), stop=(kb == KB - 1))
                    ctr.inc(mi, "MM", 32)

            sc0 = ctr.val("SC")

            @WS
            def _(e):
                for ob in range(HB):
                    p = ob % 2
                    e.wait_ge(ctr.h["MM"], mm0 + 32 * (ob + 1))
                    for tq in range(2):
                        ctr.inc(e.activation(scr[:, p, tq * 512:(tq + 1) * 512],
                                             pbank[4 * p + tq][:, 0:512], AF.Sigmoid,
                                             scale=sb_scales[:, 0, ob:ob + 1]), "SC")
                        ctr.inc(e.mul(scr[:, 2 + p, tq * 512:(tq + 1) * 512],
                                      pbank[4 * p + 2 + tq][:, 0:512],
                                      sb_scales[:, 1, ob:ob + 1]), "SC")

            gp0 = gp_base

            @WV
            def _(e):
                for ob in range(HB):
                    p = ob % 2
                    for tq in range(2):
                        e.wait_ge(ctr.h["GP"], gp0 + 2 * ob + tq + 1)
                        ctr.inc(e.scalar_tensor_tensor(Hb[:, ob, tq * 512:(tq + 1) * 512],
                                                       pbank[4 * p + tq][:, 0:512],
                                                       sb_scales[:, 0, ob:ob + 1],
                                                       scr[:, p, tq * 512:(tq + 1) * 512],
                                                       AL.mult, AL.mult), "VE")

            if _SUB == 'w12':
                free_to(markp)
                break
            hstages = [(h1t, 3, Hb, Gb), (h2t, 4, Gb, Hb)]
            if _SUB == 'h1':
                hstages = hstages[:1]
            for (wt, sidx, src_b, dst_b) in hstages:
                lwp1 = [ctr.val("LWa"), ctr.val("LWb")]
                mm1 = ctr.val("MM")
                sc1 = ctr.val("SC")
                pre_ve = ctr.val("VE")
                pre_sc = ctr.val("SC")
                pre_st = ctr.val("ST")

                @WG
                def _(e, wt=wt, mm1=mm1, pre_ve=pre_ve):
                    e.wait_ge(ctr.h["VE"], pre_ve)
                    e.wait_ge(ctr.h["MM"], mm1)
                    for ob in range(HB):
                        p = ob % 2
                        sem = "LWa" if p == 0 else "LWb"
                        if ob >= 2:
                            e.wait_ge(ctr.h["MM"], mm1 + (ob - 1) * 64)
                        ctr.inc(e.dma_start(
                            out=wb[:, p, :, :],
                            in_=wt[:, ob * 128:(ob + 1) * 128].rearrange(
                                "(k p) n -> p k n", p=128)), sem, 16)

                @blk.tensor
                def _(e, lwp1=lwp1, sc1=sc1, src_b=src_b, pre_ve=pre_ve):
                    e.wait_ge(ctr.h["VE"], pre_ve)
                    e.wait_ge(ctr.h["SC"], sc1)
                    for ob in range(HB):
                        p = ob % 2
                        sem = "LWa" if p == 0 else "LWb"
                        e.wait_ge(ctr.h[sem], lwp1[p] + 16 * (ob // 2 + 1))
                        if ob >= 2:
                            e.wait_ge(ctr.h["SC"], sc1 + 2 * (ob - 1))
                        for kk in range(HB):
                            for tq in range(2):
                                bank = pbank[4 * p + tq]
                                mi = e.matmul(bank[:, 0:512], wb[:, p, kk, :],
                                              src_b[:, kk, tq * 512:(tq + 1) * 512],
                                              start=(kk == 0), stop=(kk == HB - 1))
                        ctr.inc(mi, "MM", 64)

                @WS
                def _(e, mm1=mm1, sidx=sidx, dst_b=dst_b, pre_st=pre_st):
                    e.wait_ge(ctr.h["ST"], pre_st)
                    for ob in range(HB):
                        p = ob % 2
                        e.wait_ge(ctr.h["MM"], mm1 + 64 * (ob + 1))
                        for tq in range(2):
                            ctr.inc(e.activation(dst_b[:, ob, tq * 512:(tq + 1) * 512],
                                                 pbank[4 * p + tq][:, 0:512], AF.Gelu,
                                                 scale=sb_scales[:, sidx, ob:ob + 1]), "SC")

            if _SUB in ('h1', 'h2'):
                free_to(markp)
                break
            free_to(markp)

            p_w = pal("pw" + uid + str(hf), (128, 1024))
            lwp3 = [ctr.val("LWa"), ctr.val("LWb")]
            mm3 = ctr.val("MM")
            ve3 = ctr.val("VE")
            ld3 = ctr.val("LD")
            sc_h2 = ctr.val("SC")

            x1tiles = [GbF[:, 2 * kb:2 * kb + 2, :].rearrange("p a b -> p (a b)")
                       for kb in range(KB)]

            @WG
            def _(e):
                e.wait_ge(ctr.h["SC"], sc_h2)
                for ob in range(KB):
                    p = ob % 2
                    sem = "LWa" if p == 0 else "LWb"
                    if ob >= 2:
                        e.wait_ge(ctr.h["MM"], mm3 + (ob - 1) * 64)
                    ctr.inc(e.dma_start(
                        out=wb[:, p, :, :],
                        in_=w3t[:, ob * 128:(ob + 1) * 128].rearrange("(k p) n -> p k n", p=128)),
                        sem, 16)

            @blk.sync
            def _(e):
                ctr.wait(e, "LD")
                for ob in range(KB):
                    if ob >= 1:
                        e.wait_ge(ctr.h["VE"], ve3 + 2 * ob)
                    for tq in range(2):
                        ctr.inc(e.dma_start(out=xres[:, tq, :],
                                            in_=src_ap[ob * 128:(ob + 1) * 128,
                                                       t0 + tq * 512:t0 + (tq + 1) * 512]),
                                "LD", 16)

            @blk.tensor
            def _(e):
                e.wait_ge(ctr.h["SC"], sc_h2)
                for ob in range(KB):
                    p = ob % 2
                    sem = "LWa" if p == 0 else "LWb"
                    e.wait_ge(ctr.h[sem], lwp3[p] + 16 * (ob // 2 + 1))
                    if ob >= 1:
                        e.wait_ge(ctr.h["VE"], ve3 + 2 * ob)
                    for kk in range(HB):
                        for tq in range(2):
                            bank = p_w[:, tq * 512:(tq + 1) * 512]
                            mi = e.matmul(bank, wb[:, p, kk, :],
                                          Hb[:, kk, tq * 512:(tq + 1) * 512],
                                          start=(kk == 0), stop=(kk == HB - 1))
                    ctr.inc(mi, "MM", 64)

            @WV
            def _(e):
                for ob in range(KB):
                    e.wait_ge(ctr.h["MM"], mm3 + 64 * (ob + 1))
                    e.wait_ge(ctr.h["LD"], ld3 + 32 * (ob + 1))
                    for tq in range(2):
                        ctr.inc(e.scalar_tensor_tensor(
                            x1tiles[ob][:, tq * 512:(tq + 1) * 512],
                            p_w[:, tq * 512:(tq + 1) * 512],
                            sb_scales[:, 5, ob:ob + 1],
                            xres[:, tq, :], AL.mult, AL.add), "VE")
            ve_w3 = ctr.val("VE")

            dst_ap = yT if final_store else spill_ap

            @blk.sync
            def _(e):
                ctr.wait(e, "ST")
                for ob in range(KB):
                    e.wait_ge(ctr.h["VE"], ve3 + 2 * (ob + 1))
                    ctr.inc(e.dma_start(
                        out=dst_ap[ob * 128:(ob + 1) * 128, t0:t0 + HALF],
                        in_=x1tiles[ob]), "ST", 16)

            if make_ln_out:
                p_stat = pal("pstB" + uid + str(hf), (128, 1024))
                p_sq = pal("psqB" + uid + str(hf), (128, 1024))
                p_bc2 = pal("pbcB" + uid + str(hf), (128, 1024))
                ln_stage(x1tiles, HALF, [xln[:, kb, :] for kb in range(KB)],
                         scr, p_stat, p_sq, p_bc2)

                @blk.sync
                def _(e):
                    ctr.wait(e, "VE")
                    ctr.wait(e, "ST")
                    for kb in range(KB):
                        ctr.inc(e.dma_start(
                            out=ag_in.ap()[kb * 128:(kb + 1) * 128, t0:t0 + HALF],
                            in_=xln[:, kb, :]), "ST", 16)
            free_to(markp)

        free_to(mark0)

    def attn():
        mark0 = len(_guards)
        wq_s = alloc("wq_s", (128, KB, 256), BF16)
        wk_s = alloc("wk_s", (128, KB, 256), BF16)
        wv_s = alloc("wv_s", (128, KB, 256), BF16)
        wfg_s = alloc("wfg_s", (128, KB, 68), BF16)
        wf2_s = alloc("wf2_s", (64, 2, 128), BF16)
        xgt = alloc("xgt", (128, 2, KB, SUPER), BF16)
        qsil = alloc("qsil", (128, 2, 2, SUPER))
        ksil = alloc("ksil", (128, 2, 2, SUPER))
        khat = alloc("khat", (128, 2, 2, SUPER))
        vbf = alloc("vbf", (128, 2, 2, SUPER), BF16)
        sgk = alloc("sgk", (128, 2, 2, SUPER))
        gkb = alloc("gkb", (128, 2, 2, SUPER))
        rbb = alloc("rbb", (128, 2, SUPER))
        rgbb = alloc("rgbb", (128, 2, 2, SUPER))
        zf_bf = alloc("zf_bf", (64, SUPER), BF16)
        arows = alloc("arows", (4, 4, SUPER))
        Gw = alloc("Gw", (128, 2, SUPER))
        Gsc = alloc("Gsc", (128, CPS, 2, 6))
        egc = alloc("egc", (128, CPS, 2, 1))
        E1 = alloc("E1", (128, 2, SUPER)); E2 = alloc("E2", (128, 2, SUPER))
        E4 = alloc("E4", (128, 2, SUPER))
        tE = alloc("tE", (128, 2, SUPER))
        ksq = tE
        Qmid = alloc("Qmid", (128, 2, SUPER), BF16); Qful = alloc("Qful", (128, 2, SUPER), BF16)
        Kdot = alloc("Kdot", (128, 2, SUPER), BF16); Khb = alloc("Khb", (128, 2, SUPER), BF16)
        Adot = alloc("Adot", (128, 2, SUPER), BF16); Akm = alloc("Akm", (128, 2, SUPER), BF16)
        Xb = alloc("Xb", (128, 4, CPS, 128), BF16)
        PbA = alloc("PbA", (128, 4, CPS, 128), BF16); PtA = alloc("PtA", (128, 4, CPS, 128), BF16)
        PbB = alloc("PbB", (128, 4, CPS, 128), BF16); PtB = alloc("PtB", (128, 4, CPS, 128), BF16)
        POb = alloc("POb", (128, 4, CPS, 128), BF16)
        KTb = alloc("KTb", (128, 4, CPS, HD), BF16)
        QHb = alloc("QHb", (128, CPS, 4, C), BF16)
        WPb = alloc("WPb", (128, CPS, 4, HD), BF16)
        OLb = alloc("OLb", (128, CPS, 4, HD))
        SLb = alloc("SLb", (128, CPS, 2, HD))
        Sst = alloc("Sst", (128, 2, HD))
        Sbf = alloc("Sbf", (128, 2, HD), BF16)
        Otm = alloc("Otm", (128, 4, HD))
        Ovm = alloc("Ovm", (64, 4, SUPER))
        Ovm_lo = alloc("Ovm_lo", (64, 4, SUPER))
        Ovm_hi = alloc("Ovm_hi", (64, 4, SUPER))

        pmh = pal("pmh", (128, 2048))
        pful = pal("pful", (128, 2048))
        pfB = [pful[:, i * 512:(i + 1) * 512] for i in range(4)]
        pmB = [pmh[:, i * 512:(i + 1) * 512] for i in range(4)]

        def pmv(lo, hi):
            return pmh[:].rearrange("p (h x) -> p h x", x=512)[:, :, lo:hi]

        def pfv(lo, hi, x):
            return pful[:, lo:hi].rearrange("p (h x) -> p h x", x=x)

        @WG
        def _(e):
            ctr.wait(e, "CC")
            for dst, src in [(wq_s, wq_d), (wk_s, wk_d), (wv_s, wv_d), (wfg_s, wfg_d)]:
                ctr.inc(e.dma_start(out=dst[:],
                                    in_=src.rearrange("(k p) n -> p k n", p=128)), "LW", 16)
            ctr.inc(e.dma_start(out=wf2_s[:],
                                in_=wf2_d.rearrange("k (m n) -> k m n", n=128)), "LW", 16)
        lw_w = ctr.val("LW")

        @WV
        def _(e):
            e.memset(QHb[:], 0.0)
            e.memset(WPb[:], 0.0)
            e.memset(Sst[:], 0.0)
            ctr.inc(e.memset(Sbf[:], 0.0), "VE")

        pstate = {"ld": {-1: 0}, "zfg": {}, "vechunk": {-1: 0}}

        def sc_body(sc_i):
            ts = sc_i * SUPER
            rk = ts // SLAB
            tl = ts % SLAB
            p = sc_i % 2

            prev_ld = pstate["ld"][sc_i - 1]
            guard_mm = pstate["zfg"].get(sc_i - 2, 0)

            @blk.sync
            def _(e, rk=rk, tl=tl, p=p, prev_ld=prev_ld, guard_mm=guard_mm):
                ctr.wait(e, "CC")
                e.wait_ge(ctr.h["LD"], prev_ld)
                if guard_mm:
                    e.wait_ge(ctr.h["MM"], guard_mm)
                for kb in range(KB):
                    ctr.inc(e.dma_start(out=xgt[:, p, kb, :],
                                        in_=xg.ap()[rk, kb * 128:(kb + 1) * 128, tl:tl + SUPER]),
                            "LD", 16)
            ld_x = ctr.val("LD")
            pstate["ld"][sc_i] = ld_x

            def proj_mm(w_s, pb, bank, extra_sc=None, extra_ve=None):
                @blk.tensor
                def _(e, w_s=w_s, pb=pb, bank=bank, extra_sc=extra_sc,
                      extra_ve=extra_ve, p=p):
                    e.wait_ge(ctr.h["LD"], ld_x)
                    e.wait_ge(ctr.h["LW"], lw_w)
                    if extra_ve is not None:
                        e.wait_ge(ctr.h["VE"], extra_ve)
                    if extra_sc is not None:
                        e.wait_ge(ctr.h["SC"], extra_sc)
                    for kb in range(KB):
                        mi = e.matmul(bank[:, 0:512], w_s[:, kb, pb * 128:(pb + 1) * 128],
                                      xgt[:, p, kb, :], start=(kb == 0), stop=(kb == KB - 1))
                    ctr.inc(mi, "MM")
                return ctr.val("MM")

            def proj_ev(kind, dst, bank, mm_at, scale=None):
                if kind == "copy":
                    @WV
                    def _(e, dst=dst, bank=bank, mm_at=mm_at):
                        e.wait_ge(ctr.h["MM"], mm_at)
                        ctr.inc(e.tensor_copy(dst, bank), "VE")
                    return ctr.val("VE")
                @WS
                def _(e, kind=kind, dst=dst, bank=bank, mm_at=mm_at):
                    e.wait_ge(ctr.h["MM"], mm_at)
                    if kind == "silu":
                        ctr.inc(e.activation(dst, bank, AF.Silu), "SC")
                    elif kind == "sig":
                        ctr.inc(e.activation(dst, bank, AF.Sigmoid), "SC")
                return ctr.val("SC")

            ve_pfree = pstate["vechunk"][sc_i - 1]
            m = proj_mm(wq_s, 0, pfB[0], extra_ve=ve_pfree)
            proj_ev("silu", qsil[:, p, 0, :], pfB[0][:, 0:512], m)
            m = proj_mm(wq_s, 1, pfB[1]); proj_ev("silu", qsil[:, p, 1, :], pfB[1][:, 0:512], m)
            m = proj_mm(wk_s, 0, pfB[2]); proj_ev("silu", ksil[:, p, 0, :], pfB[2][:, 0:512], m)
            m = proj_mm(wk_s, 1, pfB[3]); sc_k1 = proj_ev("silu", ksil[:, p, 1, :], pfB[3][:, 0:512], m)
            sc_q = ctr.val("SC") - 2
            m = proj_mm(wv_s, 0, pfB[0], extra_sc=sc_q - 1)
            proj_ev("copy", vbf[:, p, 0, :], pfB[0][:, 0:512], m)
            m = proj_mm(wv_s, 1, pfB[1], extra_sc=sc_q)
            proj_ev("copy", vbf[:, p, 1, :], pfB[1][:, 0:512], m)

            @blk.tensor
            def _(e, pre_sc=ctr.val("SC") - 1, p=p):
                e.wait_ge(ctr.h["SC"], pre_sc)
                for kb in range(KB):
                    mi = e.matmul(pfB[2][0:68, 0:512], wfg_s[:, kb, :], xgt[:, p, kb, :],
                                  start=(kb == 0), stop=(kb == KB - 1))
                ctr.inc(mi, "MM")
            mm_zfg = ctr.val("MM")
            pstate["zfg"][sc_i] = mm_zfg

            @WV
            def _(e, mm_zfg=mm_zfg):
                e.wait_ge(ctr.h["MM"], mm_zfg)
                ctr.inc(e.tensor_copy(zf_bf[:, :], pfB[2][0:64, 0:512]), "VE")
            ve_zf = ctr.val("VE")
            @WS
            def _(e, mm_zfg=mm_zfg):
                e.wait_ge(ctr.h["MM"], mm_zfg)
                ctr.inc(e.activation(arows[0:4, 3, :], pfB[2][64:68, 0:512], AF.Sigmoid), "SC")
            sc_zf = ctr.val("SC")

            @blk.tensor
            def _(e, sc_zf=sc_zf, ve_zf=ve_zf):
                e.wait_ge(ctr.h["SC"], sc_zf)
                e.wait_ge(ctr.h["VE"], ve_zf)
                ctr.inc(e.matmul(pfB[3][:, 0:512], wf2_s[:, 0, :], zf_bf[:, :],
                                 start=True, stop=True), "MM")
                ctr.inc(e.matmul(pfB[0][:, 0:512], wf2_s[:, 1, :], zf_bf[:, :],
                                 start=True, stop=True), "MM")
            mm_zf2 = ctr.val("MM")

            @WS
            def _(e, mm_zf2=mm_zf2, p=p):
                e.wait_ge(ctr.h["MM"], mm_zf2 - 1)
                ctr.inc(e.activation(sgk[:, p, 0, :], pfB[3][:, 0:512], AF.Sigmoid), "SC")
                e.wait_ge(ctr.h["MM"], mm_zf2)
                ctr.inc(e.activation(sgk[:, p, 1, :], pfB[0][:, 0:512], AF.Sigmoid), "SC")
                for pb in range(2):
                    ctr.inc(e.activation(gkb[:, p, pb, :], sgk[:, p, pb, :], AF.Ln), "SC")
            sc_gk = ctr.val("SC")

            @WV
            def _(e, sc_k1=sc_k1, p=p):
                e.wait_ge(ctr.h["SC"], sc_k1)
                for pb in range(2):
                    ctr.inc(e.tensor_mul(ksq[:, pb, :], ksil[:, p, pb, :], ksil[:, p, pb, :]), "VE")
            ve_ksq = ctr.val("VE")

            @blk.tensor
            def _(e, ve_ksq=ve_ksq, sc_gk=sc_gk):
                e.wait_ge(ctr.h["VE"], ve_ksq)
                e.wait_ge(ctr.h["SC"], sc_gk)
                for pb in range(2):
                    mi = e.matmul(pfB[1][0:4, 0:512], sb_sel4[:, pb, :], ksq[:, pb, :],
                                  start=(pb == 0), stop=(pb == 1))
                ctr.inc(mi, "MM")
            mm_n2 = ctr.val("MM")

            @WV
            def _(e, mm_n2=mm_n2):
                e.wait_ge(ctr.h["MM"], mm_n2)
                ctr.inc(e.tensor_scalar_max(arows[0:4, 0, :], pfB[1][0:4, 0:512], 1e-24), "VE")
            ve_n2 = ctr.val("VE")

            @WS
            def _(e, ve_n2=ve_n2):
                e.wait_ge(ctr.h["VE"], ve_n2)
                ctr.inc(e.activation(arows[0:4, 1, :], arows[0:4, 0, :], AF.Sqrt), "SC")
            sc_n2 = ctr.val("SC")

            @WV
            def _(e, sc_n2=sc_n2):
                e.wait_ge(ctr.h["SC"], sc_n2)
                e.reciprocal(arows[0:4, 1, :], arows[0:4, 1, :])
                e.tensor_mul(arows[0:4, 2, :], arows[0:4, 1, :], arows[0:4, 1, :])
                e.tensor_mul(arows[0:4, 2, :], arows[0:4, 0, :], arows[0:4, 2, :])
                e.tensor_scalar(arows[0:4, 2, :], arows[0:4, 2, :], -0.5, 1.5, AL.mult, AL.add)
                e.tensor_mul(arows[0:4, 1, :], arows[0:4, 1, :], arows[0:4, 2, :])
                e.tensor_mul(arows[0:4, 2, :], arows[0:4, 1, :], arows[0:4, 3, :])
                ctr.inc(e.tensor_scalar_mul(arows[0:4, 2, :], arows[0:4, 2, :], -1.0), "VE")
            ve_r = ctr.val("VE")

            for pb in range(2):
                @blk.tensor
                def _(e, pb=pb, ve_r=ve_r, pre_ve=ctr.val("VE")):
                    e.wait_ge(ctr.h["VE"], max(ve_r, pre_ve))
                    e.matmul(pfB[2][:, 0:512], sb_bc4[:, pb, :], arows[0:4, 1, :],
                             start=True, stop=True)
                    ctr.inc(e.matmul(pfB[3][:, 0:512], sb_bc4[:, pb, :], arows[0:4, 2, :],
                                     start=True, stop=True), "MM")
                mm_b = ctr.val("MM")

                @WV
                def _(e, pb=pb, mm_b=mm_b, p=p):
                    e.wait_ge(ctr.h["MM"], mm_b)
                    e.tensor_copy(rbb[:, pb, :], pfB[2][:, 0:512])
                    ctr.inc(e.tensor_copy(rgbb[:, p, pb, :], pfB[3][:, 0:512]), "VE")

            @WV
            def _(e, p=p):
                for pb in range(2):
                    ctr.inc(e.tensor_mul(khat[:, p, pb, :], ksil[:, p, pb, :], rbb[:, pb, :]), "VE")
            ve_kh = ctr.val("VE")

            yield

            @WV
            def _(e, sc_gk=sc_gk, ve_kh=ve_kh, pre_sc=ctr.val("SC"),
                  pre_mm=ctr.val("MM"), p=p):
                e.wait_ge(ctr.h["SC"], max(sc_gk, pre_sc))
                e.wait_ge(ctr.h["VE"], ve_kh)
                e.wait_ge(ctr.h["MM"], pre_mm)
                for cc in range(CPS):
                    ch = slice(cc * C, (cc + 1) * C)
                    for pb in range(2):
                        e.tensor_tensor_scan(Gw[:, pb, ch], gkb[:, p, pb, ch], gkb[:, p, pb, ch],
                                             0.0, AL.add, AL.bypass)
                for pb in range(2):
                    Gv = Gw[:, pb, :].rearrange("p (c x) -> p c x", x=C)
                    gmv = Gv[:, :, 63:64]
                    gcv = Gv[:, :, 127:128]
                    e.tensor_scalar(Gsc[:, :, pb, 0:1], gmv, -1.0, -SHIFT,
                                    AL.mult, AL.add)
                    e.tensor_copy(Gsc[:, :, pb, 1:2], gcv)
                    e.tensor_add(Gsc[:, :, pb, 3:4], gcv, Gsc[:, :, pb, 0:1])
                    e.tensor_scalar(Gsc[:, :, pb, 5:6], gmv, 1.0, -SHIFT,
                                    AL.mult, AL.add)
                for cc in range(CPS):
                    for pb in range(2):
                        gc_ = Gw[:, pb, cc * C + 127:cc * C + 128]
                        e.tensor_scalar_add(E1[:, pb, cc * C:(cc + 1) * C],
                                            Gw[:, pb, cc * C:(cc + 1) * C],
                                            Gsc[:, cc, pb, 0:1])
                        e.tensor_scalar(E2[:, pb, cc * C:(cc + 1) * C],
                                        Gw[:, pb, cc * C:(cc + 1) * C],
                                        -1.0, Gsc[:, cc, pb, 5:6],
                                        AL.mult, AL.add)
                        e.tensor_scalar(E4[:, pb, cc * C:(cc + 1) * C],
                                        Gw[:, pb, cc * C:(cc + 1) * C],
                                        -1.0, gc_,
                                        AL.mult, AL.add)
                ctr.inc(e.engine_nop() if False else e.tensor_copy(
                    Gsc[:, 0, 0, 5:6], Gsc[:, 0, 0, 0:1]), "VE")
            ve_g = ctr.val("VE")

            @WS
            def _(e, ve_g=ve_g):
                e.wait_ge(ctr.h["VE"], ve_g)
                for pb in range(2):
                    e.activation(E1[:, pb, :], E1[:, pb, :], AF.Exp)
                    e.activation(Gw[:, pb, :], Gw[:, pb, :], AF.Exp)
                    e.activation(E2[:, pb, :], E2[:, pb, :], AF.Exp)
                    e.activation(E4[:, pb, :], E4[:, pb, :], AF.Exp)
                for cc in range(CPS):
                    for pb in range(2):
                        e.activation(Gsc[:, cc, pb, 2:3], Gsc[:, cc, pb, 0:1],
                                     AF.Exp, scale=-1.0)
                        mi = e.activation(Gsc[:, cc, pb, 4:5], Gsc[:, cc, pb, 3:4], AF.Exp)
                ctr.inc(mi, "SC")
            sc_e = ctr.val("SC")

            @WV
            def _(e, sc_e=sc_e, pre_mm=ctr.val("MM"), p=p):
                e.wait_ge(ctr.h["SC"], sc_e)
                e.wait_ge(ctr.h["MM"], pre_mm)
                for cc in range(CPS):
                    for pb in range(2):
                        e.tensor_mul(egc[:, cc, pb, :], Gsc[:, cc, pb, 2:3],
                                     Gsc[:, cc, pb, 4:5])
                e.tensor_mul(Qmid[:], qsil[:, p], E1[:])
                e.tensor_mul(Qful[:], qsil[:, p], Gw[:])
                e.tensor_mul(Kdot[:], khat[:, p], E2[:])
                e.tensor_mul(Khb[:], khat[:, p], E4[:])
                e.tensor_mul(tE[:], sgk[:, p], E1[:])
                e.tensor_mul(tE[:], tE[:], rgbb[:, p])
                e.tensor_mul(Adot[:], tE[:], ksil[:, p])
                for cc in range(CPS):
                    for pb in range(2):
                        mi = e.tensor_scalar_mul(Akm[:, pb, cc * C:(cc + 1) * C],
                                                 Adot[:, pb, cc * C:(cc + 1) * C],
                                                 Gsc[:, cc, pb, 2:3])
                ctr.inc(mi, "VE")
            ve_p = ctr.val("VE")

            @blk.tensor
            def _(e, ve_p=ve_p, pre_sc=ctr.val("SC"), p=p):
                e.wait_ge(ctr.h["VE"], ve_p)
                e.wait_ge(ctr.h["SC"], pre_sc)
                for cc in range(CPS):
                    ch = slice(cc * C, (cc + 1) * C)
                    for hh in range(4):
                        pb, hp = hh // 2, hh % 2
                        rg = 64 * hp
                        tp = (rg, 0) if rg else None
                        e.matmul(pmB[hh][:, 128 * cc:128 * cc + 64],
                                 vbf[rg:rg + 64, p, pb, ch],
                                 sb_idpair[rg:rg + 64, :], start=True, stop=False,
                                 tile_position=tp)
                        mi = e.matmul(pmB[hh][:, 128 * cc + 64:128 * cc + 128],
                                      Akm[rg:rg + 64, pb, ch],
                                      sb_idpair[rg:rg + 64, :], start=False, stop=True,
                                      tile_position=tp)
                ctr.inc(mi, "MM")
            mm_B = ctr.val("MM")

            @WV
            def _(e, mm_B=mm_B):
                e.wait_ge(ctr.h["MM"], mm_B)
                for hh in range(4):
                    vi = e.tensor_copy(Xb[:, hh, :, :], pmB[hh][:, 0:512])
                ctr.inc(vi, "SC")
            sc_B = ctr.val("SC")

            @blk.tensor
            def _(e, sc_B=sc_B):
                e.wait_ge(ctr.h["SC"], sc_B)
                for cc in range(CPS):
                    ch = slice(cc * C, (cc + 1) * C)
                    for hh in range(4):
                        pb, hp = hh // 2, hh % 2
                        rg = 64 * hp
                        tp = (rg, 0) if rg else None
                        ctr.inc(e.matmul(pmB[hh][:, 128 * cc:128 * (cc + 1)],
                                         Adot[rg:rg + 64, pb, ch],
                                         Kdot[rg:rg + 64, pb, ch], start=True, stop=True,
                                         tile_position=tp), "MM")
            mm_M = ctr.val("MM")

            @WV
            def _(e, mm_M=mm_M):
                e.wait_ge(ctr.h["MM"], mm_M)
                for hh in range(4):
                    e.select(PbA[:, hh, :, :], sb_maskl[:], pmB[hh][:, 0:512],
                             sb_zeros[:], add_drain=True)
                ctr.inc(e.tensor_scalar_mul(PbA[:], PbA[:], UNSCALE), "VE")
            ve_N = ctr.val("VE")

            @blk.tensor
            def _(e, ve_N=ve_N):
                e.wait_ge(ctr.h["VE"], ve_N)
                for hh in range(4):
                    for cc in range(CPS):
                        mi = e.matmul(pful[:, 512 * hh + 128 * cc:512 * hh + 128 * cc + 128],
                                      PbA[:, hh, cc, :], sb_idbf[:], start=True, stop=True)
                ctr.inc(mi, "MM")
            mm_NT = ctr.val("MM")

            @WS
            def _(e, mm_NT=mm_NT):
                e.wait_ge(ctr.h["MM"], mm_NT)
                ctr.inc(e.copy(PtA[:], pful[:, 0:2048]), "SC")
            sc_NT = ctr.val("SC")

            curP, curPt, nxtP, nxtPt = PbA, PtA, PbB, PtB
            prev_sc = sc_NT
            for lvl in range(1, 7):
                @blk.tensor
                def _(e, curP=curP, curPt=curPt, prev_sc=prev_sc,
                      pre_ve=ctr.val("VE"), lvl=lvl):
                    e.wait_ge(ctr.h["SC"], prev_sc)
                    e.wait_ge(ctr.h["VE"], pre_ve)
                    for hh in range(4):
                        for cc in range(CPS):
                            sl = pful[:, 512 * hh + 128 * cc:512 * hh + 128 * cc + 128]
                            e.matmul(sl, sb_idbf[:], Xb[:, hh, cc, :],
                                     start=True, stop=False)
                            mi = e.matmul(sl, curPt[:, hh, cc, :], Xb[:, hh, cc, :],
                                          start=False, stop=True)
                    ctr.inc(mi, "MM")
                    if lvl < 6:
                        for hh in range(4):
                            for cc in range(CPS):
                                mi = e.matmul(
                                    pmB[hh][:, 128 * cc:128 * (cc + 1)],
                                    curP[:, hh, cc, :], curPt[:, hh, cc, :],
                                    start=True, stop=True)
                        ctr.inc(mi, "MM")
                mm_ap = ctr.val("MM")

                @WV
                def _(e, mm_ap=mm_ap, lvl=lvl):
                    e.wait_ge(ctr.h["MM"], mm_ap)
                    ctr.inc(e.tensor_copy(Xb[:], pful[:, 0:2048]), "VE")
                ve_ap = ctr.val("VE")

                if lvl < 6:
                    @WS
                    def _(e, nxtPt=nxtPt, mm_ap=mm_ap):
                        e.wait_ge(ctr.h["MM"], mm_ap)
                        pv = pmh[:].rearrange("p (h x) -> p h x", x=512)
                        ctr.inc(e.copy(nxtPt[:], pv), "SC")
                    sc_pt = ctr.val("SC")

                    if lvl < 5:
                        @blk.tensor
                        def _(e, curPt=curPt, curP=curP, ve_ap=ve_ap):
                            e.wait_ge(ctr.h["VE"], ve_ap)
                            for hh in range(4):
                                for cc in range(CPS):
                                    mi = e.matmul(
                                        pful[:, 512 * hh + 128 * cc:512 * hh + 128 * cc + 128],
                                        curPt[:, hh, cc, :], curP[:, hh, cc, :],
                                        start=True, stop=True)
                            ctr.inc(mi, "MM")
                        mm_bp = ctr.val("MM")

                        @WS
                        def _(e, nxtP=nxtP, mm_bp=mm_bp):
                            e.wait_ge(ctr.h["MM"], mm_bp)
                            ctr.inc(e.copy(nxtP[:], pful[:, 0:2048]), "SC")
                    prev_sc = ctr.val("SC")
                    curP, curPt, nxtP, nxtPt = nxtP, nxtPt, curP, curPt

            @blk.tensor
            def _(e, pre_ve=ctr.val("VE"), pre_sc=ctr.val("SC")):
                e.wait_ge(ctr.h["VE"], pre_ve)
                e.wait_ge(ctr.h["SC"], pre_sc)
                for cc in range(CPS):
                    ch = slice(cc * C, (cc + 1) * C)
                    for hh in range(4):
                        pb, hp = hh // 2, hh % 2
                        rg = 64 * hp
                        tp = (rg, 0) if rg else None
                        ctr.inc(e.matmul(pmB[hh][:, 128 * cc:128 * (cc + 1)],
                                         Kdot[rg:rg + 64, pb, ch],
                                         Qmid[rg:rg + 64, pb, ch], start=True, stop=True,
                                         tile_position=tp), "MM")
            mm_QK = ctr.val("MM")

            @WV
            def _(e, mm_QK=mm_QK):
                e.wait_ge(ctr.h["MM"], mm_QK)
                for hh in range(4):
                    e.select(POb[:, hh, :, :], sb_masku[:], pmB[hh][:, 0:512],
                             sb_zeros[:], add_drain=True)
                ctr.inc(e.tensor_scalar_mul(POb[:], POb[:], UNSCALE), "VE")
            ve_PO = ctr.val("VE")

            @blk.tensor
            def _(e, ve_PO=ve_PO):
                e.wait_ge(ctr.h["VE"], ve_PO)
                for cc in range(CPS):
                    ch = slice(cc * C, (cc + 1) * C)
                    for hh in range(4):
                        pb, hp = hh // 2, hh % 2
                        rg = 64 * hp
                        tp = (rg, 0) if rg else None
                        ctr.inc(e.matmul(pmB[hh][:, 64 * cc:64 * cc + 64],
                                         Khb[rg:rg + 64, pb, ch],
                                         sb_idpair[rg:rg + 64, :], start=True, stop=True,
                                         tile_position=tp), "MM")
            mm_KT = ctr.val("MM")

            @WV
            def _(e, mm_KT=mm_KT):
                e.wait_ge(ctr.h["MM"], mm_KT)
                for hh in range(4):
                    vi = e.tensor_copy(KTb[:, hh, :, :], pmB[hh][:, 0:256])
                ctr.inc(vi, "SC")
            sc_KT = ctr.val("SC")

            @blk.tensor
            def _(e, sc_KT=sc_KT, pre_ve=ctr.val("VE")):
                e.wait_ge(ctr.h["SC"], sc_KT)
                e.wait_ge(ctr.h["VE"], pre_ve)
                for cc in range(CPS):
                    for hh in range(4):
                        mi = e.matmul(pful[:, 256 * cc + 64 * hh:256 * cc + 64 * hh + 64],
                                      POb[:, hh, cc, :], Xb[:, hh, cc, 0:64],
                                      start=True, stop=True)
                ctr.inc(mi, "MM")
                for cc in range(CPS):
                    for hh in range(4):
                        q, hp = hh // 2, hh % 2
                        mi = e.matmul(
                            pful[64 * hp:64 * hp + 64,
                                 1024 + 256 * cc + 128 * q:1024 + 256 * cc + 128 * q + 128],
                            Xb[:, hh, cc, 64:128], POb[:, hh, cc, :],
                            start=True, stop=True, tile_position=(0, 64 * hp))
                ctr.inc(mi, "MM")
                for cc in range(CPS):
                    for hh in range(4):
                        q, hp = hh // 2, hh % 2
                        mi = e.matmul(
                            pmB[0][64 * hp:64 * hp + 64, 128 * cc + 64 * q:128 * cc + 64 * q + 64],
                            Xb[:, hh, cc, 64:128], KTb[:, hh, cc, :],
                            start=True, stop=True, tile_position=(0, 64 * hp))
                ctr.inc(mi, "MM")
                for cc in range(CPS):
                    for hh in range(4):
                        q, hp = hh // 2, hh % 2
                        mi = e.matmul(
                            pmB[1][64 * hp:64 * hp + 64, 128 * cc + 64 * q:128 * cc + 64 * q + 64],
                            KTb[:, hh, cc, :], Xb[:, hh, cc, 0:64],
                            start=True, stop=True, tile_position=(0, 64 * hp))
                ctr.inc(mi, "MM")
            mm_post = ctr.val("MM")

            @WV
            def _(e, mm_post=mm_post):
                e.wait_ge(ctr.h["MM"], mm_post)
                e.tensor_copy(OLb[:], pful[:, 0:1024])
                e.tensor_copy(SLb[:], pmB[1][:, 0:512])
                for cc in range(CPS):
                    ch = slice(cc * C, (cc + 1) * C)
                    for hh in range(4):
                        q, hp = hh // 2, hh % 2
                        e.tensor_tensor(
                            QHb[64 * hp:64 * hp + 64, cc, hh, :],
                            Qful[64 * hp:64 * hp + 64, q, ch],
                            pful[64 * hp:64 * hp + 64,
                                 1024 + 256 * cc + 128 * q:1024 + 256 * cc + 128 * q + 128],
                            AL.add)
                for cc in range(CPS):
                    for hh in range(4):
                        q, hp = hh // 2, hh % 2
                        vi = e.scalar_tensor_tensor(
                            WPb[64 * hp:64 * hp + 64, cc, hh, :],
                            sb_id2pair[64 * hp:64 * hp + 64, 0, :],
                            egc[64 * hp:64 * hp + 64, cc, q, :],
                            pmB[0][64 * hp:64 * hp + 64,
                                   128 * cc + 64 * q:128 * cc + 64 * q + 64],
                            AL.mult, AL.add)
                ctr.inc(vi, "VE")
            ve_chunk = ctr.val("VE")
            pstate["vechunk"][sc_i] = ve_chunk

            yield

            for cc in range(CPS):
                @blk.tensor
                def _(e, cc=cc, pre_ve=ctr.val("VE"), pre_sc=ctr.val("SC")):
                    e.wait_ge(ctr.h["VE"], pre_ve)
                    e.wait_ge(ctr.h["SC"], pre_sc)
                    if cc > 0:
                        for hh in range(4):
                            mi = e.matmul(pmh[0:64, 1024 + 128 * hh:1024 + 128 * (hh + 1)],
                                          Otm[:, hh, :], sb_idf32[:],
                                          start=True, stop=True)
                    for hh in range(4):
                        mi = e.matmul(pmh[:, 1536 + 64 * hh:1536 + 64 * (hh + 1)],
                                      QHb[:, cc, hh, :],
                                      Sbf[:, hh // 2, :], start=True, stop=True)
                    for hh in range(4):
                        q, hp = hh // 2, hh % 2
                        mi = e.matmul(pmh[64 * hp:64 * hp + 64,
                                          1792 + 64 * q:1792 + 64 * q + 64],
                                      WPb[:, cc, hh, :], Sbf[:, q, :],
                                      start=True, stop=True, tile_position=(0, 64 * hp))
                    ctr.inc(mi, "MM")
                mm_sw = ctr.val("MM")

                @WV
                def _(e, cc=cc, mm_sw=mm_sw):
                    e.wait_ge(ctr.h["MM"], mm_sw)
                    if cc > 0:
                        e.tensor_copy(Ovm[:, :, (cc - 1) * C:cc * C],
                                      pmh[0:64, 1024:1536].rearrange("p (h x) -> p h x", x=128))
                    e.tensor_tensor(Otm[:],
                                    pmh[:, 1536:1792].rearrange("p (h x) -> p h x", x=64),
                                    OLb[:, cc, :, :], AL.add)
                    e.tensor_tensor(Sst[:],
                                    pmh[:, 1792:1920].rearrange("p (q x) -> p q x", x=64),
                                    SLb[:, cc, :, :], AL.add)
                    ctr.inc(e.tensor_copy(Sbf[:], Sst[:]), "VE")

            @blk.tensor
            def _(e, pre_ve=ctr.val("VE")):
                e.wait_ge(ctr.h["VE"], pre_ve)
                for hh in range(4):
                    mi = e.matmul(pmh[0:64, 1024 + 128 * hh:1024 + 128 * (hh + 1)],
                                  Otm[:, hh, :], sb_idf32[:], start=True, stop=True)
                ctr.inc(mi, "MM")
            mm_ot = ctr.val("MM")

            @WV
            def _(e, mm_ot=mm_ot):
                e.wait_ge(ctr.h["MM"], mm_ot)
                ctr.inc(e.tensor_copy(Ovm[:, :, (CPS - 1) * C:CPS * C],
                                      pmh[0:64, 1024:1536].rearrange("p (h x) -> p h x", x=128)),
                        "SC")

            sc_o = ctr.val("SC")
            dsh = ts // 2048
            pre_s2m = ctr.val("S2")

            @WV
            def _(e, sc_o=sc_o, pre_s2m=pre_s2m):
                e.wait_ge(ctr.h["SC"], sc_o)
                e.wait_ge(ctr.h["S2"], pre_s2m)
                e.tensor_scalar_mul(Ovm_lo[:], Ovm[:], sb_msel[:, 0:1])
                ctr.inc(e.tensor_scalar_mul(Ovm_hi[:], Ovm[:], sb_msel[:, 1:2]), "VE")
            ve_ovm = ctr.val("VE")

            @WS
            def _(e, ve_ovm=ve_ovm, dsh=dsh, tl2=ts % 2048):
                e.wait_ge(ctr.h["VE"], ve_ovm)
                ctr.wait(e, "S2")
                for hh in range(4):
                    ctr.inc(e.dma_start(
                        out=a2a_in.ap()[dsh, 64 * hh:64 * hh + 64, tl2:tl2 + SUPER],
                        in_=Ovm_lo[:, hh, :]), "S2", 16)
                    ctr.inc(e.dma_start(
                        out=a2a_in.ap()[dsh + 4, 64 * hh:64 * hh + 64, tl2:tl2 + SUPER],
                        in_=Ovm_hi[:, hh, :]), "S2", 16)

        gens = [sc_body(i) for i in range(NSUPER)]
        next(gens[0])
        for i in range(NSUPER):
            next(gens[i])
            if i + 1 < NSUPER:
                next(gens[i + 1])
            for _ in gens[i]:
                pass

        free_to(mark0)

    def post_attn():
        mark0 = len(_guards)
        wo_s = alloc("wo_s", (128, KB, KB, 128), BF16)
        wog1_s = alloc("wog1_s", (128, KB, 64), BF16)
        wog2_s = alloc("wog2_s", (64, KB, 128), BF16)
        o_t = alloc("o_t", (128, KB, 512))
        og_t = alloc("og_t", (128, KB, 512))
        oln = alloc("oln", (128, KB, 512), BF16)
        xlt = alloc("xlt", (128, KB, 512), BF16)
        zog1 = alloc("zog1", (64, 512), BF16)
        scr4 = alloc("scr4", (128, 4, 512))
        x1t = alloc("x1t", (128, KB, 512))
        x2t = alloc("x2t", (128, KB, 512))

        @WG
        def _(e):
            ctr.wait(e, "CC")
            ctr.inc(e.dma_start(out=wo_s[:],
                                in_=wo_d.rearrange("(k p) (o n) -> p k o n", p=128, n=128)),
                    "LW", 16)
            ctr.inc(e.dma_start(out=wog1_s[:],
                                in_=wog1_d.rearrange("(k p) n -> p k n", p=128)), "LW", 16)
            ctr.inc(e.dma_start(out=wog2_s[:],
                                in_=wog2_d.rearrange("k (o n) -> k o n", n=128)), "LW", 16)
        lw_p4 = ctr.val("LW")

        for tt in range(4):
            t0 = tt * 512
            markt = len(_guards)

            pre = {nm: ctr.val(nm) for nm in ["MM", "VE", "SC", "GP", "ST"]}

            @blk.sync
            def _(e, t0=t0, pre=dict(pre)):
                ctr.wait(e, "LD")
                ctr.wait(e, "CC")
                for nm in ["MM", "VE", "SC"]:
                    e.wait_ge(ctr.h[nm], pre[nm])
                for kb in range(KB):
                    ctr.inc(e.dma_start(
                        out=o_t[:, kb, :],
                        in_=a2a_out.ap()[kb // 2, 128 * (kb % 2):128 * (kb % 2) + 128,
                                         t0:t0 + 512]), "LD", 16)
                for kb in range(KB):
                    ctr.inc(e.dma_start(
                        out=xlt[:, kb, :],
                        in_=ag_in.ap()[kb * 128:(kb + 1) * 128, t0:t0 + 512]), "LD", 16)
            ld4 = ctr.val("LD")

            @WG
            def _(e, t0=t0, ld4=ld4):
                e.wait_ge(ctr.h["LD"], ld4)
                ctr.wait(e, "LW")
                for kb in range(KB):
                    ctr.inc(e.dma_start(
                        out=o_t[:, kb, :],
                        in_=a2a_out.ap()[4 + kb // 2, 128 * (kb % 2):128 * (kb % 2) + 128,
                                         t0:t0 + 512], accum_op=AL.add), "LW", 16)
            lw4 = ctr.val("LW")

            p4a = pal("p4a" + str(tt), (128, 2048))
            p4B = [p4a[:, i * 512:(i + 1) * 512] for i in range(4)]

            @blk.tensor
            def _(e, ld4=ld4):
                e.wait_ge(ctr.h["LD"], ld4)
                e.wait_ge(ctr.h["LW"], lw_p4)
                for kb in range(KB):
                    mi = e.matmul(p4B[0][0:64, 0:512], wog1_s[:, kb, :], xlt[:, kb, :],
                                  start=(kb == 0), stop=(kb == KB - 1))
                ctr.inc(mi, "MM")
            mm_g1 = ctr.val("MM")

            @WV
            def _(e, mm_g1=mm_g1):
                e.wait_ge(ctr.h["MM"], mm_g1)
                ctr.inc(e.tensor_copy(zog1[:], p4B[0][0:64, 0:512]), "SC")
            sc_g1 = ctr.val("SC")

            @blk.tensor
            def _(e, sc_g1=sc_g1):
                e.wait_ge(ctr.h["SC"], sc_g1)
                for ob in range(KB):
                    if ob >= 3:
                        e.wait_ge(ctr.h["SC"], sc_g1 + ob - 2)
                    ctr.inc(e.matmul(p4B[ob % 4][:, 0:512], wog2_s[:, ob, :], zog1[:],
                                     start=True, stop=True), "MM")
            mm_g2 = ctr.val("MM")

            @WS
            def _(e, mm_g2=mm_g2):
                for ob in range(KB):
                    e.wait_ge(ctr.h["MM"], mm_g2 - KB + ob + 1)
                    ctr.inc(e.activation(og_t[:, ob, :], p4B[ob % 4][:, 0:512],
                                         AF.Sigmoid), "SC")
            sc_og = ctr.val("SC")

            @WV
            def _(e, sc_og=sc_og, ld4=ld4, lw4=lw4):
                e.wait_ge(ctr.h["SC"], sc_og)
                e.wait_ge(ctr.h["LD"], ld4)
                e.wait_ge(ctr.h["LW"], lw4)
                for kb in range(KB):
                    ctr.inc(e.tensor_mul(o_t[:, kb, :], o_t[:, kb, :], og_t[:, kb, :]), "VE")
            free_to(markt)

            p_st = pal("pst4" + str(tt), (128, 512))
            p_sq = pal("psq4" + str(tt), (128, 512))
            p_bc = pal("pbc4" + str(tt), (128, 1024))
            ln_stage([o_t[:, kb, :] for kb in range(KB)], 512,
                     [oln[:, kb, :] for kb in range(KB)], scr4, p_st, p_sq, p_bc,
                     nw=sb_normw)
            free_to(markt)

            p4w = pal("p4w" + str(tt), (128, 1024))
            mm4 = ctr.val("MM")
            ve4 = ctr.val("VE")
            ld4b = ctr.val("LD")
            st4 = ctr.val("ST")

            @blk.sync
            def _(e, t0=t0, ve4=ve4):
                ctr.wait(e, "LD")
                e.wait_ge(ctr.h["VE"], ve4)
                for ob in range(KB):
                    ctr.inc(e.dma_start(out=x1t[:, ob, :],
                                        in_=x1_sp[ob * 128:(ob + 1) * 128, t0:t0 + 512]),
                            "LD", 16)
            ld_x1 = ctr.val("LD")

            @blk.tensor
            def _(e, mm4=mm4, ve4=ve4, pre_ve=ctr.val("VE")):
                e.wait_ge(ctr.h["VE"], pre_ve)
                for ob in range(KB):
                    if ob >= 2:
                        e.wait_ge(ctr.h["VE"], ve4 + ob - 1)
                    bank = p4w[:, (ob % 2) * 512:(ob % 2) * 512 + 512]
                    for kb in range(KB):
                        mi = e.matmul(bank, wo_s[:, kb, ob, :], oln[:, kb, :],
                                      start=(kb == 0), stop=(kb == KB - 1))
                    ctr.inc(mi, "MM")

            @WV
            def _(e, mm4=mm4, ld_x1=ld_x1, st4=st4):
                e.wait_ge(ctr.h["LD"], ld_x1)
                e.wait_ge(ctr.h["ST"], st4)
                for ob in range(KB):
                    e.wait_ge(ctr.h["MM"], mm4 + ob + 1)
                    ctr.inc(e.tensor_add(x2t[:, ob, :],
                                         p4w[:, (ob % 2) * 512:(ob % 2) * 512 + 512],
                                         x1t[:, ob, :]), "VE")
            ve_x2 = ctr.val("VE")

            dst4 = yT if _STAGE == 2 else x2_sp

            @blk.sync
            def _(e, t0=t0, ve_x2=ve_x2, dst4=dst4):
                ctr.wait(e, "ST")
                for ob in range(KB):
                    e.wait_ge(ctr.h["VE"], ve_x2 - KB + ob + 1)
                    ctr.inc(e.dma_start(out=dst4[ob * 128:(ob + 1) * 128, t0:t0 + 512],
                                        in_=x2t[:, ob, :]), "ST", 16)
            free_to(markt)

        free_to(mark0)

    ffn_pass(xsT, x1_sp, make_ln_out=True, final_store=(_STAGE == 1), uid="f1")
    barrier()

    if _STAGE >= 2:
        @WG
        def _(e):
            ctr.wait(e, "ST")
            ctr.inc(e.collective_compute(
                "AllGather", AL.bypass,
                replica_groups=[[0, 1, 2, 3], [4, 5, 6, 7]],
                ins=[ag_in.ap().opt()], outs=[xg.ap().opt()]), "CC")
        attn()
        barrier()

        @WG
        def _(e):
            ctr.wait(e, "S2")
            with allow_ici_all_to_all():
                ctr.inc(e.collective_compute(
                    "AllToAll", AL.bypass,
                    replica_groups=[[0, 1, 2, 3, 4, 5, 6, 7]],
                    ins=[a2a_in.ap().opt()], outs=[a2a_out.ap().opt()]), "CC")
        post_attn()
        barrier()

    if _STAGE >= 3:
        ffn_pass(x2_sp, None, make_ln_out=False, final_store=True, uid="f2")

    @blk.sync
    def _(e):
        ctr.wait(e, "ST")

    blk_guard.__exit__(None, None, None)
    free_to(0)
    return nc


def _prep_inputs(inputs):
    f32 = np.float32
    x = np.ascontiguousarray(inputs["L0_attn_x"], dtype=f32).reshape(TOK, D)
    wq = inputs["L0_attn_Wq"]; wk = inputs["L0_attn_Wk"]; wv = inputs["L0_attn_Wv"]
    wgam = inputs["L0_attn_Wgamma"]; wf1 = inputs["L0_attn_Wf1"]; wf2 = inputs["L0_attn_Wf2"]
    wog1 = inputs["L0_attn_Wog1"]; wog2 = inputs["L0_attn_Wog2"]
    normw = inputs["L0_attn_norm_w"]; wo = inputs["L0_attn_Wo"]

    bf = ml_dtypes.bfloat16
    consts = {
        "c_idbf": np.eye(128, dtype=f32).astype(bf),
        "c_idf32": np.eye(128, dtype=f32),
        "c_idpair": np.concatenate([np.eye(64, dtype=f32)] * 2, axis=0).astype(bf),
        "c_id2pair": np.stack([np.concatenate([np.eye(64, dtype=f32)] * 2, axis=0)] * 2,
                              axis=1),
        "c_onescol": np.ones((128, 1), f32),
        "c_onesrow": np.ones((1, 128), f32),
        "c_sel4": np.zeros((128, 2, 4), f32),
        "c_bc4": np.zeros((4, 2, 128), f32),
        "c_maskl4": np.broadcast_to(np.tril(np.ones((128, 128), np.uint8), -1)[:, None, :],
                                    (128, 4, 128)).copy(),
        "c_masku4": np.broadcast_to(np.tril(np.ones((128, 128), np.uint8), 0).T[:, None, :],
                                    (128, 4, 128)).copy(),
    }
    for hh in range(4):
        pb, hp = hh // 2, hh % 2
        consts["c_sel4"][64 * hp:64 * hp + 64, pb, hh] = 1.0
        consts["c_bc4"][hh, pb, 64 * hp:64 * hp + 64] = 1.0

    shared = {
        "wfg": np.ascontiguousarray(
            np.concatenate([wf1, wgam], axis=1), dtype=f32),
        "wog1": np.ascontiguousarray(wog1, f32),
        "wog2": np.ascontiguousarray(wog2, f32),
        "wo": np.ascontiguousarray(wo, f32),
        "normw": np.ascontiguousarray(np.asarray(normw, f32).reshape(KB, 128).T),
        "w1t": np.ascontiguousarray(inputs["L0_ffn_w1_w"].T, f32),
        "w2t": np.ascontiguousarray(inputs["L0_ffn_w2_w"].T, f32),
        "h1t": np.ascontiguousarray(inputs["L0_ffn_h1_w"].T, f32),
        "h2t": np.ascontiguousarray(inputs["L0_ffn_h2_w"].T, f32),
        "w3t": np.ascontiguousarray(inputs["L0_ffn_w3_w"].T, f32),
        "s1": np.ascontiguousarray(np.asarray(inputs["L0_ffn_w1_s"], f32).reshape(HB, 128).T),
        "s2": np.ascontiguousarray(np.asarray(inputs["L0_ffn_w2_s"], f32).reshape(HB, 128).T),
        "sh1": np.ascontiguousarray(np.asarray(inputs["L0_ffn_h1_s"], f32).reshape(HB, 128).T),
        "sh2": np.ascontiguousarray(np.asarray(inputs["L0_ffn_h2_s"], f32).reshape(HB, 128).T),
        "s3": np.ascontiguousarray(np.asarray(inputs["L0_ffn_w3_s"], f32).reshape(KB, 128).T),
    }
    shared.update(consts)

    in_maps = []
    for c in range(NCORES):
        hsel = slice(256 * (c % 4), 256 * (c % 4) + 256)
        gsel = slice(4 * (c % 4), 4 * (c % 4) + 4)
        m = dict(shared)
        m["x_slabT"] = np.ascontiguousarray(x[c * SLAB:(c + 1) * SLAB].T, f32)
        m["wq"] = np.ascontiguousarray(wq[:, hsel], f32)
        m["wk"] = np.ascontiguousarray(wk[:, hsel], f32)
        m["wv"] = np.ascontiguousarray(wv[:, hsel], f32)
        m["wf2"] = np.ascontiguousarray(wf2[:, hsel], f32)
        m["wfg"] = np.ascontiguousarray(
            np.concatenate([wf1, wgam[:, gsel]], axis=1), f32)
        msel = np.zeros((64, 2), f32)
        msel[:, 0 if c < 4 else 1] = 1.0
        m["c_msel"] = msel
        in_maps.append(m)
    return in_maps


def kernel(**inputs):
    in_maps = _prep_inputs(inputs)
    nc = build_program()
    res = run_bass_kernel_spmd(nc, in_maps, core_ids=list(range(NCORES)))
    out = np.concatenate([np.asarray(r["y_slabT"]).T for r in res.results], axis=0)
    return np.ascontiguousarray(out.reshape(B, N, D), dtype=np.float32)
